# version 20
# baseline (speedup 1.0000x reference)
import contextlib
import numpy as np
import concourse.bass as bass
import concourse.mybir as mybir
from concourse.bass_utils import run_bass_kernel_spmd

F32 = mybir.dt.float32
F32R = mybir.dt.float32r
AF = mybir.ActivationFunctionType
OP = mybir.AluOpType
AX = mybir.AxisListType

B, S, E, H, D, CHK = 2, 4096, 2048, 16, 128, 256
NCORE = 8
TPC = (B * S) // NCORE
CPC = TPC // CHK
NF = E // 128
NST = 2
STW = TPC // NST
GW = 136

_DEBUG_OUTPUTS = False
_SKIP_CC = False


class Prog:
    ENGINES = ("tensor", "vector", "scalar", "sync", "gpsimd")

    def __init__(self, nc):
        self.nc = nc
        self.ops = {e: [] for e in self.ENGINES}
        self.sems = {}
        self.counts = {}
        self.last_wait = {e: {} for e in self.ENGINES}
        for e in self.ENGINES:
            self._mksem("p_" + e)
        self._mksem("cc")
        self.bank_rel = [None] * 8

    def _mksem(self, name):
        if name not in self.sems:
            self.sems[name] = self.nc.alloc_semaphore(name=name)
            self.counts[name] = 0

    def _wait(self, eng, tok):
        if tok is None:
            return
        sem, val, owner = tok
        if val <= 0 or owner == eng:
            return
        lw = self.last_wait[eng]
        if lw.get(sem, 0) >= val:
            return
        lw[sem] = val
        h = self.sems[sem]
        self.ops[eng].append(lambda e, h=h, v=val: e.wait_ge(h, v))

    def op(self, eng, fn, waits=(), tok=False):
        for t in waits:
            self._wait(eng, t)
        if tok:
            pn = "p_" + eng
            h = self.sems[pn]
            self.counts[pn] += 1
            val = self.counts[pn]
            self.ops[eng].append(lambda e, fn=fn, h=h: fn(e).then_inc(h, 1))
            return (pn, val, eng)
        self.ops[eng].append(fn)
        return None

    def dma(self, eng, stream, fn, waits=()):
        self._mksem(stream)
        for t in waits:
            self._wait(eng, t)
        h = self.sems[stream]
        self.counts[stream] += 16
        val = self.counts[stream]
        self.ops[eng].append(lambda e, fn=fn, h=h: fn(e).then_inc(h, 16))
        return (stream, val, None)

    def cc_op(self, fn, waits=()):
        for t in waits:
            self._wait("gpsimd", t)
        h = self.sems["cc"]
        self.counts["cc"] += 1
        val = self.counts["cc"]
        self.ops["gpsimd"].append(lambda e, fn=fn, h=h: fn(e).then_inc(h, 1))
        return ("cc", val, None)

    def barrier(self):
        toks = [(s, self.counts[s], None) for s in self.sems if self.counts[s] > 0]
        for eng in self.ENGINES:
            own = "p_" + eng
            for s, v, _ in toks:
                if s == own:
                    continue
                self._wait(eng, (s, v, None))


def build_program():
    nc = bass.Bass("TRN2", target_bir_lowering=False,
                   detect_race_conditions=False)
    ikind = "ExternalOutput" if _DEBUG_OUTPUTS else "Internal"

    XT = nc.dram_tensor("XT", [E, TPC], F32, kind="ExternalInput").ap()
    WQKVP = nc.dram_tensor("WQKVP", [E, 3 * E], F32, kind="ExternalInput").ap()
    WOUTS = nc.dram_tensor("WOUTS", [NF, 128, E], F32, kind="ExternalInput").ap()
    WFFNS = nc.dram_tensor("WFFNS", [NF, 128, E], F32, kind="ExternalInput").ap()
    BIAS = nc.dram_tensor("BIAS", [E, 1], F32, kind="ExternalInput").ap()
    COS = nc.dram_tensor("COS", [128, TPC], F32, kind="ExternalInput").ap()
    SINS = nc.dram_tensor("SINS", [128, TPC], F32, kind="ExternalInput").ap()
    MASKUT = nc.dram_tensor("MASKUT", [128, 512], F32, kind="ExternalInput").ap()
    IDENT = nc.dram_tensor("IDENT", [128, 128], F32, kind="ExternalInput").ap()
    PROT = nc.dram_tensor("PROT", [128, 128], F32, kind="ExternalInput").ap()
    SEGM = nc.dram_tensor("SEGM", [128, 4], F32, kind="ExternalInput").ap()
    SELMAT = nc.dram_tensor("SELMAT", [16, E], F32, kind="ExternalInput").ap()
    OUT = nc.dram_tensor("OUT", [E, TPC], F32, kind="ExternalOutput").ap()
    QF = nc.dram_tensor("QF", [E, TPC], F32, kind=ikind).ap()
    KF = nc.dram_tensor("KF", [E, TPC], F32, kind=ikind).ap()
    VT = nc.dram_tensor("VT", [H, CPC, 2, 128, D], F32, kind=ikind).ap()
    GS = nc.dram_tensor("GS", [H, CPC, 128, GW], F32, kind=ikind).ap()
    FF = nc.dram_tensor("FF", [E, TPC], F32, kind=ikind).ap()
    XN = nc.dram_tensor("XN", [E, TPC], F32, kind=ikind).ap()
    WOH = nc.dram_tensor("WOH", [NF, 128, E], F32R, kind=ikind).ap()
    WOL = nc.dram_tensor("WOL", [NF, 128, E], F32R, kind=ikind).ap()
    WT = nc.dram_tensor("WT", [NF, 128, E], F32R, kind=ikind).ap()
    AO = nc.dram_tensor("AO", [E, TPC], F32, kind=ikind).ap()
    DEN = nc.dram_tensor("DEN", [16, H * TPC // 16], F32, kind=ikind).ap()
    QRAW = (nc.dram_tensor("QRAW", [E, TPC], F32, kind="ExternalOutput").ap()
            if _DEBUG_OUTPUTS else None)
    SCL = (nc.dram_tensor("SCL", [128, 2], F32, kind="ExternalOutput").ap()
           if _DEBUG_OUTPUTS else None)
    WTS_D = (nc.dram_tensor("WTS_D", [128, E], F32R, kind="ExternalOutput").ap()
             if _DEBUG_OUTPUTS else None)
    XNH_D = (nc.dram_tensor("XNH_D", [128, NF * TPC], F32, kind="ExternalOutput").ap()
             if _DEBUG_OUTPUTS else None)
    XNL_D = (nc.dram_tensor("XNL_D", [128, NF * TPC], F32, kind="ExternalOutput").ap()
             if _DEBUG_OUTPUTS else None)
    WTD_D = (nc.dram_tensor("WTD_D", [NF, 128, E], F32, kind="ExternalOutput").ap()
             if _DEBUG_OUTPUTS else None)
    WTB_D = (nc.dram_tensor("WTB_D", [128, E], F32R, kind="ExternalOutput").ap()
             if _DEBUG_OUTPUTS else None)
    WTS = (nc.dram_tensor("WTS", [128, E], F32, kind="ExternalOutput").ap()
           if _DEBUG_OUTPUTS else None)
    XNC = (nc.dram_tensor("XNC", [128, NF * STW], F32, kind="ExternalOutput").ap()
           if _DEBUG_OUTPUTS else None)
    AOD = (nc.dram_tensor("AOD", [128, NF * STW], F32, kind="ExternalOutput").ap()
           if _DEBUG_OUTPUTS else None)
    ADH = (nc.dram_tensor("ADH", [128, NF * STW], F32R, kind="ExternalOutput").ap()
           if _DEBUG_OUTPUTS else None)
    ADL = (nc.dram_tensor("ADL", [128, NF * STW], F32R, kind="ExternalOutput").ap()
           if _DEBUG_OUTPUTS else None)
    WOHB = (nc.dram_tensor("WOHB", [128, E], F32R, kind="ExternalOutput").ap()
            if _DEBUG_OUTPUTS else None)
    ACC = (nc.dram_tensor("ACC", [128, 34], F32, kind="ExternalOutput").ap()
           if _DEBUG_OUTPUTS else None)
    CCIN = nc.dram_tensor("CCIN", [128, H * GW], F32).ap()
    CCOUT = nc.dram_tensor("CCOUT", [512, H * GW], F32).ap()

    pb = [nc.alloc_psum_tensor(f"pb{i}", [128, 512], F32).ap() for i in range(8)]
    p = Prog(nc)

    stack_all = contextlib.ExitStack()
    sb = lambda name, shape, dt=F32: stack_all.enter_context(
        nc.sbuf_tensor(name, shape, dt)).ap()

    ones_sb = sb("ones_sb", [128, 128])
    segm_sb = sb("segm_sb", [128, 4])
    scale_sb = sb("scale_sb", [128, 1])
    rscaleb_sb = sb("rscaleb_sb", [128, 1])
    thr_sb = sb("thr_sb", [128, 1])
    thrn_sb = sb("thrn_sb", [128, 1])
    acc32 = sb("acc32", [128, 34])
    den16r = sb("den16r", [16, H * TPC // 16])
    wfb = [sb(f"wfb{i}", [128, TPC]) for i in range(2)]
    wtsB = sb("wtsB", [128, E], F32R)
    bfb = [sb(f"bfb{i}", [128, 1]) for i in range(2)]

    stA = contextlib.ExitStack()
    sa = lambda name, shape, dt=F32: stA.enter_context(
        nc.sbuf_tensor(name, shape, dt)).ap()
    cos_sb = sa("cos_sb", [128, TPC])
    sins_sb = sa("sins_sb", [128, TPC])
    ident_sb = sa("ident_sb", [128, 128])
    prot_sb = sa("prot_sb", [128, 128])
    mu_sb = sa("mu_sb", [128, TPC])
    rs_sb = sa("rs_sb", [128, TPC])
    xnh_sb = sa("xnh_sb", [128, NF * TPC], F32R)
    xnl_sb = sa("xnl_sb", [128, NF * TPC], F32R)
    pool = [sa(f"pool{i}", [128, TPC]) for i in range(4)]
    tp1 = sa("tp1s", [128, STW])
    tp2 = sa("tp2s", [128, STW])
    tp3 = sa("tp3s", [128, STW])
    wqb = [sa(f"wqb{i}", [128, 256]) for i in range(2)]
    wqh = [sa(f"wqh{i}", [128, 256], F32R) for i in range(2)]
    wql = [sa(f"wql{i}", [128, 256], F32R) for i in range(2)]

    p.dma("sync", "cst", lambda e: e.dma_start(out=cos_sb, in_=COS))
    p.dma("sync", "cst", lambda e: e.dma_start(out=sins_sb, in_=SINS))
    p.dma("sync", "cst", lambda e: e.dma_start(out=ident_sb, in_=IDENT))
    p.dma("sync", "cst", lambda e: e.dma_start(out=prot_sb, in_=PROT))
    tok_consts = p.dma("sync", "cst", lambda e: e.dma_start(out=segm_sb, in_=SEGM))
    tok_consts = ("cst", p.counts["cst"], None)
    tok_ones = p.op("vector", lambda e: e.memset(ones_sb, 1.0), tok=True)
    vtile_one = sa("vtileA", [128, 8 * 132])
    vtile_all = [vtile_one, vtile_one]
    vtile = [[vtile_one[:, i * 132:(i + 1) * 132] for i in range(8)]
             for q in range(2)]
    ktile1 = [sa(f"ktile_{i}", [128, 128]) for i in range(8)]
    ktile = [ktile1, ktile1]
    gst_all = sa("gstA", [128, CPC * GW])
    gst = [gst_all[:, c * GW:(c + 1) * GW] for c in range(CPC)]
    gseg_sb = sa("gseg_sb", [128, GW])

    tok_vones = None
    for q in range(2):
        for i in range(8):
            p.op("vector", lambda e, q=q, i=i: e.memset(vtile[q][i][:, 0:128], 0.0))
            tok_vones = p.op("vector", lambda e, q=q, i=i: e.memset(
                vtile[q][i][:, 128:132], 1.0), tok=(q == 1 and i == 7))
    for c in range(4):
        p.op("vector", lambda e, c=c: e.memset(gst[c][:, 129:GW], 0.0))
    tok_gz = p.op("vector", lambda e: e.memset(gseg_sb[:, 129:GW], 0.0), tok=True)

    xb = pool[0:2]
    xsq = pool[2:4]
    rel_xb = [[], []]
    rel_xsq = [[], []]
    tok_stats = None
    for f in range(NF):
        tx = p.dma("sync", f"xb{f % 2}", lambda e, f=f: e.dma_start(
            out=xb[f % 2], in_=XT[f * 128:(f + 1) * 128, :]), waits=rel_xb[f % 2])
        tsq = p.op("scalar", lambda e, f=f: e.activation(
            xsq[f % 2], xb[f % 2], AF.Square), waits=[tx] + rel_xsq[f % 2], tok=True)
        tok_s = tok_q = None
        for st in range(NST):
            sl = slice(st * STW, (st + 1) * STW)
            tok_s = p.op("tensor", lambda e, f=f, st=st, sl=sl: e.matmul(
                pb[st][:, :], ones_sb, xb[f % 2][:, sl],
                start=(f == 0), stop=(f == NF - 1)),
                waits=[tx, tok_ones] + ([p.bank_rel[st]] if f == 0 else []),
                tok=(st == NST - 1))
        for st in range(NST):
            sl = slice(st * STW, (st + 1) * STW)
            tok_q = p.op("tensor", lambda e, f=f, st=st, sl=sl: e.matmul(
                pb[2 + st][:, :], ones_sb, xsq[f % 2][:, sl],
                start=(f == 0), stop=(f == NF - 1)),
                waits=[tsq] + ([p.bank_rel[2 + st]] if f == 0 else []),
                tok=(st == NST - 1))
        rel_xb[f % 2] = [tok_s, tsq]
        rel_xsq[f % 2] = [tok_q]
        tok_stats = tok_q

    tmpA, tmpB = pool[2], pool[3]
    for st in range(NST):
        sl = slice(st * STW, (st + 1) * STW)
        p.op("vector", lambda e, st=st, sl=sl: e.tensor_scalar_mul(
            mu_sb[:, sl], pb[st][:, :], 1.0 / E), waits=[tok_stats])
        p.op("vector", lambda e, st=st, sl=sl: e.tensor_scalar_mul(
            tmpB[:, sl], pb[2 + st][:, :], 1.0 / E))
    p.op("vector", lambda e: e.tensor_mul(tmpA, mu_sb, mu_sb))
    p.op("vector", lambda e: e.tensor_sub(tmpB, tmpB, tmpA))
    tok_var = p.op("vector", lambda e: e.tensor_scalar_add(tmpB, tmpB, 1e-5), tok=True)
    for i in range(4):
        p.bank_rel[i] = tok_var
    tok_sd = p.op("scalar", lambda e: e.activation(tmpA, tmpB, AF.Sqrt),
                  waits=[tok_var], tok=True)
    tok_rs = p.op("vector", lambda e: e.reciprocal(rs_sb, tmpA), waits=[tok_sd],
                  tok=True)

    xn_tok = [None] * NF
    rel2 = [rel_xb[0] + [tok_rs], rel_xb[1] + [tok_rs]]
    tmp1, tmp2 = pool[2], pool[3]
    rel_tmp1 = [tok_var]
    last_xn_store = None
    for f in range(NF):
        sl = slice(f * TPC, (f + 1) * TPC)
        tx = p.dma("sync", f"xb{f % 2}", lambda e, f=f: e.dma_start(
            out=xb[f % 2], in_=XT[f * 128:(f + 1) * 128, :]), waits=rel2[f % 2])
        p.op("vector", lambda e, f=f: e.tensor_sub(tmp1, xb[f % 2], mu_sb),
             waits=[tx, tok_rs] + rel_tmp1)
        rel_tmp1 = []
        p.op("vector", lambda e, f=f: e.tensor_mul(tmp2, tmp1, rs_sb))
        rel2[f % 2] = [p.op("vector", lambda e, f=f, sl=sl: e.tensor_copy(
            xnh_sb[:, sl], tmp2), tok=True)]
        xn_tok[f] = p.op("vector", lambda e, f=f, sl=sl: e.tensor_sub(
            xnl_sb[:, sl], tmp2, xnh_sb[:, sl].bitcast(F32)), tok=True)


    rel_tp = {"tp1": [], "tp2": [], "tp3": []}

    def rope_elu(raw, dst, rel_dst, tok_raw, rhbanks):
        dst_tok = None
        raw_rel = None
        for st in range(NST):
            sl = slice(st * STW, (st + 1) * STW)
            bk = rhbanks[st]
            trh = p.op("tensor", lambda e, sl=sl, bk=bk: e.matmul(
                pb[bk][:, :], prot_sb, raw[:, sl], start=True, stop=True),
                waits=[tok_raw, tok_consts] +
                      ([p.bank_rel[bk]] if p.bank_rel[bk] else []), tok=True)
            p.op("vector", lambda e, sl=sl: e.tensor_mul(
                tp1, raw[:, sl], cos_sb[:, sl]),
                waits=[tok_raw, tok_consts] + rel_tp["tp1"])
            tmul2 = p.op("vector", lambda e, sl=sl, bk=bk: e.tensor_mul(
                tp2, pb[bk][:, :], sins_sb[:, sl]),
                waits=[trh] + rel_tp["tp2"], tok=True)
            p.bank_rel[bk] = tmul2
            p.op("vector", lambda e: e.tensor_add(
                tp3, tp1, tp2), waits=rel_tp["tp3"])
            raw_rel = tmin = p.op("vector", lambda e: e.tensor_scalar_min(
                tp1, tp3, 0.0), tok=True)
            texp = p.op("scalar", lambda e: e.activation(
                tp2, tp1, AF.Exp), waits=[tmin], tok=True)
            dst_tok = p.op("vector", lambda e, sl=sl: e.scalar_tensor_tensor(
                dst[:, sl], tp3, 0.0, tp2, OP.max, OP.add),
                waits=[texp] + ([rel_dst] if rel_dst else []), tok=True)
            rel_tp["tp1"] = [texp]
            rel_tp["tp2"] = [dst_tok]
            rel_tp["tp3"] = [dst_tok]
        return dst_tok, raw_rel

    vraw, kraw = pool[0], pool[1]
    kout = [pool[2], pool[3]]
    rel_wq = [[], []]
    rel_vraw = []
    rel_kraw = []
    rel_kout = [[], []]
    rel_vtile1 = [None] * 8
    rel_vtile = [rel_vtile1, rel_vtile1]
    rel_ktile1 = [None] * 8
    rel_ktile = [rel_ktile1, rel_ktile1]
    rel_gst = [None] * 4
    rel_gseg = None
    last_vt_store = None
    last_kf_store = None
    ccin_last = None
    vt_rel_extra = [None, None]
    rel_wcv = [[], []]
    pend = None
    for h in range(H + 1):
        pq = h % 2
        if h < H:
            tok_acc = None
            for f in range(NF):
                fp = f % 2
                tw = p.dma("sync", f"wq{fp}", lambda e, h=h, f=f, fp=fp: e.dma_start(
                    out=wqb[fp],
                    in_=WQKVP[f * 128:(f + 1) * 128, h * 256:(h + 1) * 256]),
                    waits=rel_wq[fp])
                tcv = p.op("vector", lambda e, fp=fp: e.tensor_copy(
                    wqh[fp], wqb[fp]), waits=[tw] + rel_wcv[fp])
                tcv = p.op("vector", lambda e, fp=fp: e.tensor_sub(
                    wql[fp], wqb[fp], wqh[fp].bitcast(F32)), tok=True)
                rel_wq[fp] = [tcv]
                mm_w = [tcv, xn_tok[f]]
                if f == 0:
                    mm_w += [t for t in p.bank_rel[0:4] if t]
                    mm_w += rel_vraw + rel_kraw
                last = None
                for col in range(2):
                    cs = slice(col * 128, (col + 1) * 128)
                    b0 = 2 * col
                    for st in range(NST):
                        sl = slice(f * TPC + st * STW, f * TPC + (st + 1) * STW)
                        p.op("tensor", lambda e, f=f, fp=fp, cs=cs, st=st, sl=sl,
                             b0=b0:
                             e.matmul(pb[b0 + st][:, :], wqh[fp][:, cs],
                                      xnh_sb[:, sl],
                                      start=(f == 0), stop=False), waits=mm_w)
                        p.op("tensor", lambda e, fp=fp, cs=cs, st=st, sl=sl, b0=b0:
                             e.matmul(pb[b0 + st][:, :], wqh[fp][:, cs],
                                      xnl_sb[:, sl],
                                      start=False, stop=False))
                    for st in range(NST):
                        sl = slice(f * TPC + st * STW, f * TPC + (st + 1) * STW)
                        last = p.op("tensor",
                                    lambda e, f=f, fp=fp, cs=cs, st=st, sl=sl,
                                    b0=b0:
                             e.matmul(pb[b0 + st][:, :], wql[fp][:, cs],
                                      xnh_sb[:, sl],
                                      start=False, stop=(f == NF - 1)),
                             tok=(col == 1 and st == NST - 1))
                rel_wcv[fp] = [last]
                tok_acc = last
            ev_v = []
            ev_k = []
            for st in range(NST):
                sl = slice(st * STW, (st + 1) * STW)
                t = p.op("scalar", lambda e, st=st, sl=sl: e.copy(
                    vraw[:, sl], pb[st][:, :]), waits=[tok_acc] + rel_vraw, tok=True)
                ev_v.append(t)
                p.bank_rel[st] = t
            for st in range(NST):
                sl = slice(st * STW, (st + 1) * STW)
                t = p.op("scalar", lambda e, st=st, sl=sl: e.copy(
                    kraw[:, sl], pb[2 + st][:, :]), waits=rel_kraw, tok=True)
                ev_k.append(t)
                p.bank_rel[2 + st] = t
        if pend is not None:
            p1 = pend["p"]
            hh = pend["h"]
            kt_toks = []
            last_ktr = None
            for i in range(8):
                bk = 4 + (i % 2)
                w = [pend["tok_k"]] + ([p.bank_rel[bk]] if p.bank_rel[bk] else [])
                ttr = p.op("tensor", lambda e, i=i, bk=bk, p1=p1: e.transpose(
                    pb[bk][:, 0:128], kout[p1][:, i * 128:(i + 1) * 128], ident_sb),
                    waits=w, tok=True)
                last_ktr = ttr
                tev = p.op("scalar", lambda e, i=i, bk=bk, p1=p1: e.copy(
                    ktile[p1][i], pb[bk][:, 0:128]),
                    waits=[ttr] + ([rel_ktile[p1][i]] if rel_ktile[p1][i] else []),
                    tok=True)
                p.bank_rel[bk] = tev
                kt_toks.append(tev)
            rel_kout[p1] = [last_ktr, pend["kf_store"]]
            gacc_tok = None
            gs_last = None
            gs_evs = []
            vt_toks_p = pend["vt_toks"]
            for c in range(CPC):
                bk = 6 + (c % 2)
                w = [kt_toks[2 * c], kt_toks[2 * c + 1], vt_toks_p[2 * c],
                     vt_toks_p[2 * c + 1]] + ([p.bank_rel[bk]] if p.bank_rel[bk] else [])
                p.op("tensor", lambda e, c=c, bk=bk, p1=p1: e.matmul(
                    pb[bk][:, 0:129], ktile[p1][2 * c], vtile[p1][2 * c][:, 0:129],
                    start=True, stop=False), waits=w)
                tg = p.op("tensor", lambda e, c=c, bk=bk, p1=p1: e.matmul(
                    pb[bk][:, 0:129], ktile[p1][2 * c + 1],
                    vtile[p1][2 * c + 1][:, 0:129],
                    start=False, stop=True), tok=True)
                rel_ktile[p1][2 * c] = tg
                rel_ktile[p1][2 * c + 1] = tg
                rel_vtile[p1][2 * c] = tg
                rel_vtile[p1][2 * c + 1] = tg
                vt_rel_extra[p1] = pend["vt_store"]
                tev = p.op("scalar", lambda e, c=c, bk=bk: e.copy(
                    gst[c][:, 0:129], pb[bk][:, 0:129]),
                    waits=[tg, tok_gz] + ([rel_gst[c]] if rel_gst[c] else []), tok=True)
                p.bank_rel[bk] = tev
                gs_evs.append(tev)
                if c == 0:
                    gacc_tok = p.op("vector", lambda e: e.tensor_copy(
                        gseg_sb[:, 0:129], gst[0][:, 0:129]),
                        waits=[tev] + ([rel_gseg] if rel_gseg else []), tok=True)
                else:
                    gacc_tok = p.op("vector", lambda e, c=c: e.tensor_add(
                        gseg_sb[:, 0:129], gseg_sb[:, 0:129], gst[c][:, 0:129]),
                        waits=[tev, gacc_tok], tok=True)
            gs_last = p.dma("gpsimd", "sgs", lambda e, hh=hh: e.dma_start(
                out=GS[hh].rearrange("c m w -> m c w"),
                in_=gst_all.rearrange("m (c w) -> m c w", w=GW)), waits=gs_evs)
            for c in range(CPC):
                rel_gst[c] = gs_last
            ccin_last = p.dma("gpsimd", "scc", lambda e, hh=hh: e.dma_start(
                out=CCIN[:, hh * GW:(hh + 1) * GW], in_=gseg_sb), waits=[gacc_tok])
            rel_gseg = ccin_last
            pend = None
        if h < H:
            vt_toks = []
            last_tr = None
            for i in range(8):
                bk = 4 + (i % 2)
                w = [ev_v[i // 4], tok_consts] + \
                    ([p.bank_rel[bk]] if p.bank_rel[bk] else [])
                ttr = p.op("tensor", lambda e, i=i, bk=bk, pq=pq: e.transpose(
                    pb[bk][:, 0:128], vraw[:, i * 128:(i + 1) * 128], ident_sb),
                    waits=w, tok=True)
                last_tr = ttr
                tev = p.op("scalar", lambda e, i=i, bk=bk, pq=pq: e.copy(
                    vtile[pq][i][:, 0:128], pb[bk][:, 0:128]),
                    waits=[ttr, tok_vones, vt_rel_extra[pq]] +
                          ([rel_vtile[pq][i]] if rel_vtile[pq][i] else []),
                    tok=True)
                p.bank_rel[bk] = tev
                vt_toks.append(tev)
            last_vt_store = p.dma("gpsimd", "svt",
                                  lambda e, h=h, pq=pq: e.dma_start(
                out=VT[h].rearrange("c t m d -> m c t d"),
                in_=vtile_all[pq].rearrange("m (i w) -> m i w", w=132)[:, :, 0:128]),
                waits=vt_toks)
            for i in range(8):
                rel_vtile[pq][i] = last_vt_store
            rel_vraw = [last_tr]
            tok_k, kraw_rel = rope_elu(kraw, kout[pq],
                                       (rel_kout[pq][0] if rel_kout[pq] else None),
                                       ev_k[1], (6, 7))
            rel_kraw = [kraw_rel]
            last_kf_store = p.dma("gpsimd", "skf", lambda e, h=h, pq=pq: e.dma_start(
                out=KF[h * 128:(h + 1) * 128, :], in_=kout[pq]), waits=[tok_k])
            pend = {"h": h, "p": pq, "tok_k": tok_k, "vt_toks": vt_toks,
                    "vt_store": last_vt_store, "kf_store": last_kf_store}

    if _SKIP_CC:
        for s in range(4):
            tok_cc = p.dma("gpsimd", "scc", lambda e, s=s: e.dma_start(
                out=CCOUT[s * 128:(s + 1) * 128, :], in_=CCIN),
                waits=[ccin_last])
    else:
        tok_cc = p.cc_op(lambda e: e.collective_compute(
            "AllGather", OP.bypass,
            replica_groups=[[0, 1, 2, 3], [4, 5, 6, 7]],
            ins=[CCIN.opt()], outs=[CCOUT.opt()]), waits=[ccin_last])

    qraw = [pool[0], pool[1]]
    qout = [pool[2], pool[3]]
    rel_qraw = [rel_vraw + rel_kraw, list(rel_kraw)]
    rel_wf = [[], []]
    tred = None
    rel_qout = [list(rel_kout[0]), list(rel_kout[1])]
    last_qf_store = [None, None]
    pend3 = None
    for b in range(H // 2 + 1):
        if pend3 is not None:
            bb = pend3["b"]
            for j in range(2):
                hq = 2 * bb + j
                tok_q, raw_rel = rope_elu(qraw[j], qout[j],
                                          (rel_qout[j][0] if rel_qout[j] else None),
                                          pend3["ev"][j][1],
                                          (4, 5) if j == 0 else (6, 7))
                rel_qraw[j] = [raw_rel] + pend3["extra"][j]
                last_qf_store[j] = p.dma("gpsimd", f"sqf{j}",
                                         lambda e, hq=hq, j=j: e.dma_start(
                    out=QF[hq * 128:(hq + 1) * 128, :], in_=qout[j]), waits=[tok_q])
                rel_qout[j] = [last_qf_store[j]]
            pend3 = None
        if b < H // 2:
            tok_acc = None
            for f in range(NF):
                fp = f % 2
                tw = p.dma("sync", f"wq{fp}", lambda e, b=b, f=f, fp=fp: e.dma_start(
                    out=wqb[fp],
                    in_=WQKVP[f * 128:(f + 1) * 128,
                              2 * E + b * 256:2 * E + (b + 1) * 256]),
                    waits=rel_wq[fp])
                tcv = p.op("vector", lambda e, fp=fp: e.tensor_copy(
                    wqh[fp], wqb[fp]), waits=[tw] + rel_wcv[fp])
                tcv = p.op("vector", lambda e, fp=fp: e.tensor_sub(
                    wql[fp], wqb[fp], wqh[fp].bitcast(F32)), tok=True)
                rel_wq[fp] = [tcv]
                mm_w = [tcv, xn_tok[f]]
                if f == 0:
                    mm_w += [t for t in p.bank_rel[0:4] if t]
                last = None
                for col in range(2):
                    cs = slice(col * 128, (col + 1) * 128)
                    b0 = 2 * col
                    for st in range(NST):
                        sl = slice(f * TPC + st * STW, f * TPC + (st + 1) * STW)
                        p.op("tensor", lambda e, f=f, fp=fp, cs=cs, st=st, sl=sl,
                             b0=b0:
                             e.matmul(pb[b0 + st][:, :], wqh[fp][:, cs],
                                      xnh_sb[:, sl],
                                      start=(f == 0), stop=False), waits=mm_w)
                        p.op("tensor", lambda e, fp=fp, cs=cs, st=st, sl=sl, b0=b0:
                             e.matmul(pb[b0 + st][:, :], wqh[fp][:, cs],
                                      xnl_sb[:, sl],
                                      start=False, stop=False))
                    for st in range(NST):
                        sl = slice(f * TPC + st * STW, f * TPC + (st + 1) * STW)
                        last = p.op("tensor",
                                    lambda e, f=f, fp=fp, cs=cs, st=st, sl=sl,
                                    b0=b0:
                             e.matmul(pb[b0 + st][:, :], wql[fp][:, cs],
                                      xnh_sb[:, sl],
                                      start=False, stop=(f == NF - 1)),
                             tok=(col == 1 and st == NST - 1))
                rel_wcv[fp] = [last]
                tok_acc = last
            for k in range(4):
                i = 4 * b + k
                twfb = p.dma("sync", f"lwf{i % 2}", lambda e, i=i: e.dma_start(
                    out=wfb[i % 2],
                    in_=WFFNS[i // 2, :, (i % 2) * TPC:(i % 2 + 1) * TPC]),
                    waits=rel_wf[i % 2])
                tred = p.op("vector", lambda e, i=i: e.tensor_reduce(
                    acc32[:, i:i + 1], wfb[i % 2], AX.X, OP.add,
                    apply_absolute_value=True), waits=[twfb], tok=True)
                rel_wf[i % 2] = [tred]
        if b < H // 2:
            evs = []
            extra = [[], []]
            for j in range(2):
                hq = 2 * b + j
                ev = []
                for st in range(NST):
                    sl = slice(st * STW, (st + 1) * STW)
                    t = p.op("scalar", lambda e, j=j, st=st, sl=sl: e.copy(
                        qraw[j][:, sl], pb[2 * j + st][:, :]),
                        waits=[tok_acc] + rel_qraw[j], tok=True)
                    ev.append(t)
                    p.bank_rel[2 * j + st] = t
                if _DEBUG_OUTPUTS:
                    tqr = p.dma("gpsimd", "sqr", lambda e, hq=hq, j=j: e.dma_start(
                        out=QRAW[hq * 128:(hq + 1) * 128, :], in_=qraw[j]),
                        waits=[ev[0], ev[1]])
                    extra[j] = [tqr]
                evs.append(ev)
            pend3 = {"b": b, "ev": evs, "extra": extra}

    p.op("vector", lambda e: e.memset(gst_all[:, 0:512], 0.0), waits=[tred])
    p.op("vector", lambda e: e.memset(gst_all[:, 0:512], 1.0))
    p.op("vector", lambda e: e.tensor_reduce(
        acc32[:, 32:33], acc32[:, 0:32], AX.X, OP.add))
    p.op("vector", lambda e: e.memset(gst_all[:, 0:512], 0.0))
    tred = p.op("vector", lambda e: e.memset(gst_all[:, 0:512], 1.0), tok=True)
    if _DEBUG_OUTPUTS:
        p.dma("gpsimd", "sdbg", lambda e: e.dma_start(out=ACC, in_=acc32),
              waits=[tred])
    tsc = p.op("tensor", lambda e: e.matmul(
        pb[6][:, 0:1], ones_sb, acc32[:, 32:33], start=True, stop=True),
        waits=[tred] + ([p.bank_rel[6]] if p.bank_rel[6] else []), tok=True)
    tok_scale = p.op("vector", lambda e: e.tensor_scalar_mul(
        scale_sb, pb[6][:, 0:1], 1.0 / (E * E)), waits=[tsc], tok=True)
    p.bank_rel[6] = tok_scale
    p.op("vector", lambda e: e.memset(gst_all[:, 0:512], 0.0))
    p.op("vector", lambda e: e.memset(gst_all[:, 0:512], 1.0))
    p.op("vector", lambda e: e.tensor_scalar_add(rscaleb_sb, scale_sb, 1e-10))
    p.op("vector", lambda e: e.memset(gst_all[:, 0:512], 0.0))
    p.op("vector", lambda e: e.memset(gst_all[:, 0:512], 1.0))
    p.op("vector", lambda e: e.tensor_scalar_mul(thr_sb, rscaleb_sb, 0.5))
    p.op("vector", lambda e: e.tensor_scalar_mul(thrn_sb, rscaleb_sb, -0.5))
    p.op("vector", lambda e: e.memset(gst_all[:, 0:512], 0.0))
    tok_thr = p.op("vector", lambda e: e.memset(gst_all[:, 0:512], 1.0), tok=True)

    if _DEBUG_OUTPUTS:
        p.dma("gpsimd", "sdbg", lambda e: e.dma_start(
            out=SCL[:, 0:1], in_=scale_sb, allow_slow_non_contiguous=True),
            waits=[tok_thr])
        p.dma("gpsimd", "sdbg", lambda e: e.dma_start(
            out=SCL[:, 1:2], in_=thr_sb, allow_slow_non_contiguous=True),
            waits=[tok_thr])
    if _DEBUG_OUTPUTS:
        p.dma("gpsimd", "sdbg", lambda e: e.dma_start(
            out=WTS_D, in_=xnh_sb[:, 0:2048]), waits=[tok_thr])
        p.dma("gpsimd", "sdbg", lambda e: e.dma_start(
            out=XNH_D, in_=xnh_sb.bitcast(F32)), waits=[tok_thr])
        p.dma("gpsimd", "sdbg", lambda e: e.dma_start(
            out=XNL_D, in_=xnl_sb.bitcast(F32)), waits=[tok_thr])
    m2h = pool[0]
    ffb = [pool[1][:, 0:512], pool[1][:, 512:1024]]
    rel_bfw = [rel_wf[0], rel_wf[1]]
    rel_m2 = list(rel_qraw[0])
    rel_wtsB = []
    rel_ffb = [list(rel_qraw[1]), list(rel_qraw[1])]
    rel_bfb = [[], []]
    last_ff_store = [None, None]
    rel_bfbank = [None, None, None, None]
    for oc in range(NF):
        par = oc % 2
        tbi = p.dma("scalar", f"lbf{par}", lambda e, oc=oc, par=par: e.dma_start(
            out=bfb[par], in_=BIAS[oc * 128:(oc + 1) * 128, :]), waits=rel_bfb[par])
        ttn = None
        for half in range(2):
            twf = p.dma("sync", f"lws{half}", lambda e, oc=oc, half=half:
                        e.dma_start(
                out=wfb[half], in_=WFFNS[oc, :, half * TPC:(half + 1) * TPC]),
                waits=rel_bfw[half])
            p.op("vector", lambda e, half=half: e.tensor_scalar(
                m2h, wfb[half], thrn_sb[:, 0:1], None, OP.is_lt),
                waits=[twf, tok_thr] + rel_m2 + rel_wtsB)
            rel_m2 = []
            ttn = p.op("vector", lambda e, half=half: e.scalar_tensor_tensor(
                wtsB[:, half * TPC:(half + 1) * TPC], wfb[half], thr_sb[:, 0:1],
                m2h, OP.is_gt, OP.subtract), tok=True)
            rel_bfw[half] = [ttn]
        tdmp = None
        if _DEBUG_OUTPUTS:
            tdmp = p.dma("gpsimd", "sdbg", lambda e, oc=oc: e.dma_start(
                out=WTD_D[oc], in_=wtsB.bitcast(F32)), waits=[ttn])
        for st in range(NST):
            bk = 2 * par + st
            wmm = [ttn] + ([rel_bfbank[bk]] if rel_bfbank[bk] else [])
            if oc == 0 and st == 0:
                wmm += [t for t in p.bank_rel[0:4] if t]
            tmf = None
            for f in range(NF):
                sl = slice(f * TPC + st * STW, f * TPC + (st + 1) * STW)
                p.op("tensor", lambda e, f=f, sl=sl, bk=bk: e.matmul(
                    pb[bk][:, :], wtsB[:, f * 128:(f + 1) * 128], xnh_sb[:, sl],
                    start=(f == 0), stop=False), waits=wmm)
                tmf = p.op("tensor", lambda e, f=f, sl=sl, bk=bk: e.matmul(
                    pb[bk][:, :], wtsB[:, f * 128:(f + 1) * 128], xnl_sb[:, sl],
                    start=False, stop=(f == NF - 1)), tok=(f == NF - 1))
            te1 = p.op("vector", lambda e, par=par, bk=bk: e.tensor_scalar(
                ffb[par], pb[bk][:, :], scale_sb[:, 0:1], bfb[par][:, 0:1],
                OP.mult, OP.add),
                waits=[tmf, tok_scale, tbi] + rel_ffb[par], tok=True)
            rel_bfbank[bk] = te1
            last_ff_store[par] = p.dma("gpsimd", f"sff{par}",
                                       lambda e, oc=oc, st=st, par=par: e.dma_start(
                out=FF[oc * 128:(oc + 1) * 128, st * STW:(st + 1) * STW],
                in_=ffb[par]), waits=[te1])
            rel_ffb[par] = [last_ff_store[par]]
            rel_wtsB = [tmf]
        if tdmp is not None:
            rel_wtsB = rel_wtsB + [tdmp]
        rel_bfb[par] = [rel_bfbank[2 * par + 1]]

    p.barrier()
    stA.close()

    stB = contextlib.ExitStack()
    sbB = lambda name, shape, dt=F32: stB.enter_context(
        nc.sbuf_tensor(name, shape, dt)).ap()
    mask_sb = sbB("mask_sb", [128, 512])
    den16 = sbB("den16", [16, H * TPC // 16])
    den16b = sbB("den16b", [16, H * TPC // 16])

    tok_mask = p.dma("sync", "cst", lambda e: e.dma_start(out=mask_sb, in_=MASKUT))

    sbase_sb = sbB("sbase_sb", [128, H * GW])
    ccseg = [sbB("ccseg0", [128, H * GW])] * 2
    qh_sb = [[sbB(f"qh{q}_{i}", [128, TPC]) for i in range(2)] for q in range(2)]
    kh_sb = [[sbB(f"kh{q}_{i}", [128, TPC]) for i in range(2)] for q in range(2)]
    vtb = [[sbB(f"vtb{q}_{i}", [128, 256]) for i in range(2)] for q in range(2)]
    gsb = [[sbB(f"gsb{q}_{i}", [128, GW]) for i in range(2)] for q in range(2)]
    scur = [[sbB(f"scur{q}_{i}", [128, 132]) for i in range(CPC)] for q in range(2)]
    attsb = [[sbB(f"attsb{q}_{i}", [128, 512]) for i in range(2)] for q in range(2)]
    aoc = [[sbB(f"aoc{q}_{i}", [128, 256]) for i in range(2)] for q in range(2)]
    densb = [[sbB(f"densb{q}_{i}", [1, 256]) for i in range(2)] for q in range(2)]

    tok_sbase = None
    rel_ccseg = [[], []]
    for s in range(4):
        tcs = p.dma("scalar", "lcs0", lambda e, s=s: e.dma_start(
            out=ccseg[0], in_=CCOUT[s * 128:(s + 1) * 128, :]),
            waits=[tok_cc] + rel_ccseg[0] + rel_ccseg[1])
        if s == 0:
            tok_sbase = p.op("vector", lambda e: e.tensor_scalar(
                sbase_sb, ccseg[0], segm_sb[:, 0:1], None, OP.mult),
                waits=[tcs, tok_consts], tok=True)
        else:
            tok_sbase = p.op("vector", lambda e, s=s: e.scalar_tensor_tensor(
                sbase_sb, ccseg[s % 2], segm_sb[:, s:s + 1], sbase_sb,
                OP.mult, OP.add), waits=[tcs], tok=True)
        rel_ccseg[s % 2] = [tok_sbase]

    rel_qh = [[[last_qf_store[0], last_qf_store[1]] for _ in range(2)]
              for _ in range(2)]
    rel_kh = [[[last_kf_store] for _ in range(2)] for _ in range(2)]
    rel_vtb = [[[last_vt_store], [last_vt_store]], [[last_vt_store], [last_vt_store]]]
    rel_gsb = [[[], []], [[], []]]
    rel_attsb = [[[], []], [[], []]]
    rel_scur = [[[], [], [], []], [[], [], [], []]]
    rel_aoc = [[[], []], [[], []]]
    rel_densb = [[[], []], [[], []]]
    last_den_store = {}
    last_ao_store = {}
    for g in range(H // 2):
        scur_toks = [None, None]
        tq = [None, None]
        tk = [None, None]
        gp = g % 2
        for hp in range(2):
            h = 2 * g + hp
            tq[hp] = p.dma("scalar", f"lqh{hp}_{gp}",
                           lambda e, h=h, hp=hp, gp=gp: e.dma_start(
                out=qh_sb[gp][hp], in_=QF[h * 128:(h + 1) * 128, :]),
                waits=rel_qh[gp][hp])
            tk[hp] = p.dma("scalar", f"lkh{hp}_{gp}",
                           lambda e, h=h, hp=hp, gp=gp: e.dma_start(
                out=kh_sb[gp][hp], in_=KF[h * 128:(h + 1) * 128, :]),
                waits=rel_kh[gp][hp])
        for hp in range(2):
            h = 2 * g + hp
            scur_tok = [None] * CPC
            scur_tok[0] = p.op("vector", lambda e, h=h, hp=hp: e.tensor_copy(
                scur[hp][0][:, 0:129], sbase_sb[:, h * GW:h * GW + 129]),
                waits=[tok_sbase] + rel_scur[hp][0], tok=True)
            for c in range(1, CPC):
                tg = p.dma("scalar", f"lgb{hp}_{c % 2}",
                           lambda e, h=h, hp=hp, c=c: e.dma_start(
                    out=gsb[hp][c % 2], in_=GS[h, c - 1, :, :]),
                    waits=rel_gsb[hp][c % 2] + [rel_gst[c - 1]])
                scur_tok[c] = p.op("vector", lambda e, hp=hp, c=c: e.tensor_add(
                    scur[hp][c][:, 0:129], scur[hp][c - 1][:, 0:129],
                    gsb[hp][c % 2][:, 0:129]),
                    waits=[tg, scur_tok[c - 1]] + rel_scur[hp][c], tok=True)
                rel_gsb[hp][c % 2] = [scur_tok[c]]
            scur_toks[hp] = scur_tok
        tden = {}
        att_last = {}
        for c in range(CPC):
            for hp in range(2):
                h = 2 * g + hp
                csl = slice(c * CHK, (c + 1) * CHK)
                ab = 4 * hp
                nb = 2 + 4 * hp
                db = 3 + 4 * hp
                tvtb = p.dma("scalar", f"lvb{hp}_{c % 2}",
                             lambda e, h=h, hp=hp, c=c: e.dma_start(
                    out=vtb[hp][c % 2].rearrange("m (t d) -> m t d", t=2),
                    in_=VT[h, c].rearrange("t m d -> m t d")),
                    waits=rel_vtb[hp][c % 2])
                att_toks = []
                for t in range(2):
                    bk = ab + t
                    wmm = [tq[hp], tk[hp]] + ([p.bank_rel[bk]] if p.bank_rel[bk] else [])
                    ta = p.op("tensor", lambda e, hp=hp, c=c, t=t, csl=csl, bk=bk,
                              gp=gp: e.matmul(
                        pb[bk][:, 0:256],
                        kh_sb[gp][hp][:, c * CHK + t * 128:c * CHK + (t + 1) * 128],
                        qh_sb[gp][hp][:, csl], start=True, stop=True), waits=wmm,
                        tok=True)
                    tm = p.op("vector", lambda e, hp=hp, c=c, t=t, bk=bk: e.tensor_mul(
                        attsb[hp][c % 2][:, t * 256:(t + 1) * 256], pb[bk][:, 0:256],
                        mask_sb[:, t * 256:(t + 1) * 256]),
                        waits=[ta, tok_mask] + rel_attsb[hp][c % 2], tok=True)
                    p.bank_rel[bk] = tm
                    att_toks.append(tm)
                att_last[hp] = att_toks[1]
                wmm = [att_toks[0], att_toks[1], tvtb, scur_toks[hp][c]]
                wmm += [p.bank_rel[nb]] if p.bank_rel[nb] else []
                p.op("tensor", lambda e, hp=hp, c=c, nb=nb: e.matmul(
                    pb[nb][:, 0:256], vtb[hp][c % 2][:, 0:128],
                    attsb[hp][c % 2][:, 0:256], start=True, stop=False), waits=wmm)
                p.op("tensor", lambda e, hp=hp, c=c, nb=nb: e.matmul(
                    pb[nb][:, 0:256], vtb[hp][c % 2][:, 128:256],
                    attsb[hp][c % 2][:, 256:512], start=False, stop=False))
                tnum = p.op("tensor", lambda e, hp=hp, c=c, nb=nb, csl=csl, gp=gp:
                            e.matmul(
                    pb[nb][:, 0:256], scur[hp][c][:, 0:128], qh_sb[gp][hp][:, csl],
                    start=False, stop=True), tok=True)
                wmm = [p.bank_rel[db]] if p.bank_rel[db] else []
                p.op("tensor", lambda e, hp=hp, c=c, db=db: e.matmul(
                    pb[db][0:1, 0:256], ones_sb[:, 0:1],
                    attsb[hp][c % 2][:, 0:256], start=True, stop=False), waits=wmm)
                p.op("tensor", lambda e, hp=hp, c=c, db=db: e.matmul(
                    pb[db][0:1, 0:256], ones_sb[:, 0:1],
                    attsb[hp][c % 2][:, 256:512], start=False, stop=False))
                tden[hp] = p.op("tensor", lambda e, hp=hp, c=c, db=db, csl=csl,
                                gp=gp: e.matmul(
                    pb[db][0:1, 0:256], scur[hp][c][:, 128:129],
                    qh_sb[gp][hp][:, csl], start=False, stop=True), tok=True)
                rel_scur[hp][c] = [tden[hp]]
                rel_attsb[hp][c % 2] = [tden[hp]]
                rel_vtb[hp][c % 2] = [tnum]
                tev = p.op("scalar", lambda e, hp=hp, c=c, nb=nb: e.copy(
                    aoc[hp][c % 2], pb[nb][:, 0:256]),
                    waits=[tnum] + rel_aoc[hp][c % 2], tok=True)
                p.bank_rel[nb] = tev
                last_ao_store[hp, c % 2] = p.dma(
                    "gpsimd", f"sao{hp}_{c % 2}",
                    lambda e, h=h, hp=hp, c=c: e.dma_start(
                        out=AO[h * 128:(h + 1) * 128, c * CHK:(c + 1) * CHK],
                        in_=aoc[hp][c % 2]), waits=[tev])
                rel_aoc[hp][c % 2] = [last_ao_store[hp, c % 2]]
                tevd = p.op("scalar", lambda e, hp=hp, c=c, db=db: e.copy(
                    densb[hp][c % 2], pb[db][0:1, 0:256]),
                    waits=[tden[hp]] + rel_densb[hp][c % 2], tok=True)
                p.bank_rel[db] = tevd
                last_den_store[hp, c % 2] = p.dma(
                    "gpsimd", f"sdn{hp}_{c % 2}",
                    lambda e, h=h, hp=hp, c=c: e.dma_start(
                        out=DEN[h:h + 1, c * CHK:(c + 1) * CHK],
                        in_=densb[hp][c % 2]), waits=[tevd])
                rel_densb[hp][c % 2] = [last_den_store[hp, c % 2]]
        for hp in range(2):
            rel_qh[gp][hp] = [tden[hp]]
            rel_kh[gp][hp] = [att_last[hp]]

    tdr = p.dma("scalar", "ldn", lambda e: e.dma_start(out=den16, in_=DEN),
                waits=[last_den_store[k] for k in last_den_store])
    p.op("vector", lambda e: e.tensor_scalar_max(den16b, den16, 1e-6), waits=[tdr])
    tok_recip = p.op("vector", lambda e: e.reciprocal(den16r, den16b), tok=True)

    p.barrier()
    stB.close()

    stC = contextlib.ExitStack()
    sc = lambda name, shape, dt=F32: stC.enter_context(
        nc.sbuf_tensor(name, shape, dt)).ap()
    aodh = sc("aodh", [128, NF * STW], F32R)
    aodl = sc("aodl", [128, NF * STW], F32R)
    aotmp = [sc(f"aotmp{i}", [128, STW]) for i in range(2)]
    sett = sc("sett", [128, 256])
    wcp = [sc(f"wcp{i}", [128, E]) for i in range(2)]
    woh = [sc(f"woh{i}", [128, E], F32R) for i in range(2)]
    wol = [sc(f"wol{i}", [128, E], F32R) for i in range(2)]
    ffld = [sc(f"ffld{i}", [128, STW]) for i in range(2)]
    selmat_sb = sc("selmat_sb", [16, E])
    e1b = [sc(f"e1b{i}", [128, STW]) for i in range(2)]
    outoc = [sc(f"outoc{i}", [128, STW]) for i in range(2)]
    xrc = [sc(f"xrc{i}", [128, STW]) for i in range(2)]

    tok_sel = p.dma("scalar", "lsel", lambda e: e.dma_start(
        out=selmat_sb, in_=SELMAT))
    rel_pass_ao = [last_ao_store[k] for k in last_ao_store]
    rel_woh = [[], []]
    rel_wcp = [[], []]
    rel_ffld = [[last_ff_store[0], last_ff_store[1]], []]
    rel_aotmp = [[], []]
    rel_e1 = [[], []]
    rel_outoc = [[], []]
    rel_xrc = [[], []]
    last_out = [None, None]
    for st in range(NST):
        div_toks = [None] * NF
        for f in range(NF):
            sl = slice(f * STW, (f + 1) * STW)
            fp = f % 2
            tld = p.dma("scalar", f"lao{fp}", lambda e, f=f, st=st, fp=fp: e.dma_start(
                out=aotmp[fp],
                in_=AO[f * 128:(f + 1) * 128, st * STW:(st + 1) * STW]),
                waits=(rel_pass_ao if f < 2 else []) + rel_aotmp[fp])
            bk = 6 + fp
            wmm = [tok_recip, tok_sel] + ([p.bank_rel[bk]] if p.bank_rel[bk] else [])
            tprr = p.op("tensor", lambda e, f=f, st=st, bk=bk: e.matmul(
                pb[bk][:, :], selmat_sb[:, f * 128:(f + 1) * 128],
                den16r[:, st * STW:(st + 1) * STW], start=True, stop=True),
                waits=wmm, tok=True)
            p.op("vector", lambda e, fp=fp, bk=bk: e.tensor_mul(
                aotmp[fp], aotmp[fp], pb[bk][:, :]), waits=[tprr, tld])
            tdv = p.op("vector", lambda e, fp=fp, sl=sl: e.tensor_copy(
                aodh[:, sl], aotmp[fp]), tok=True)
            p.bank_rel[bk] = tdv
            div_toks[f] = p.op("vector", lambda e, fp=fp, sl=sl: e.tensor_sub(
                aodl[:, sl], aotmp[fp], aodh[:, sl].bitcast(F32)), tok=True)
            rel_aotmp[fp] = [div_toks[f]]
        p.op("vector", lambda e: e.memset(sett, 0.0), waits=[div_toks[NF - 1]])
        div_all = p.op("vector", lambda e: e.memset(sett, 1.0), tok=True)
        rel_pass_ao = []
        if _DEBUG_OUTPUTS and st == 0:
            p.dma("gpsimd", "sdbg", lambda e: e.dma_start(out=ADH, in_=aodh),
                  waits=[div_toks[NF - 1]])
            p.dma("gpsimd", "sdbg", lambda e: e.dma_start(out=ADL, in_=aodl),
                  waits=[div_toks[NF - 1]])
        for oc in range(NF):
            par = oc % 2
            bw, bf_ = 2 * par, 2 * par + 1
            tld = p.dma("sync", f"lwoh{par}", lambda e, oc=oc, par=par: e.dma_start(
                out=wcp[par], in_=WOUTS[oc]), waits=rel_wcp[par])
            p.op("vector", lambda e, par=par: e.tensor_copy(woh[par], wcp[par]),
                 waits=[tld] + rel_woh[par])
            twoh = p.op("vector", lambda e, par=par: e.tensor_sub(
                wol[par], wcp[par], woh[par].bitcast(F32)), tok=True)
            rel_wcp[par] = [twoh]
            if _DEBUG_OUTPUTS and st == 0 and oc == 0:
                p.dma("gpsimd", "sdbg", lambda e: e.dma_start(out=WOHB, in_=woh[0]),
                      waits=[twoh])
            tff = p.dma("scalar", f"lff{par}", lambda e, oc=oc, st=st, par=par:
                        e.dma_start(
                out=ffld[par], in_=FF[oc * 128:(oc + 1) * 128,
                                      st * STW:(st + 1) * STW]),
                waits=rel_ffld[par])
            txr = p.dma("scalar", f"lxr{par}", lambda e, oc=oc, st=st, par=par:
                        e.dma_start(
                out=xrc[par], in_=XT[oc * 128:(oc + 1) * 128,
                                     st * STW:(st + 1) * STW]), waits=rel_xrc[par])
            tok_mm1 = None
            for f in range(NF):
                sl = slice(f * STW, (f + 1) * STW)
                wmm = [twoh, div_toks[f + 2] if f <= NF - 3 else div_all]
                if f == 0:
                    wmm += [p.bank_rel[bw]] if p.bank_rel[bw] else []
                p.op("tensor", lambda e, f=f, sl=sl, bw=bw, par=par: e.matmul(
                    pb[bw][:, :], woh[par][:, f * 128:(f + 1) * 128], aodh[:, sl],
                    start=(f == 0), stop=False), waits=wmm)
                p.op("tensor", lambda e, f=f, sl=sl, bw=bw, par=par: e.matmul(
                    pb[bw][:, :], woh[par][:, f * 128:(f + 1) * 128], aodl[:, sl],
                    start=False, stop=False))
            for f in range(NF):
                sl = slice(f * STW, (f + 1) * STW)
                tok_mm1 = p.op("tensor", lambda e, f=f, sl=sl, bw=bw, par=par:
                               e.matmul(
                    pb[bw][:, :], wol[par][:, f * 128:(f + 1) * 128], aodh[:, sl],
                    start=False, stop=(f == NF - 1)),
                    tok=(f == NF - 1))
            rel_woh[par] = [tok_mm1]
            if oc >= NF - 2:
                rel_pass_ao.append(tok_mm1)
            t2 = p.op("vector", lambda e, par=par, bw=bw: e.tensor_add(
                e1b[par], ffld[par], pb[bw][:, :]),
                waits=[tok_mm1, tff] + rel_e1[par], tok=True)
            p.bank_rel[bw] = t2
            rel_ffld[par] = [t2]
            teo = p.op("vector", lambda e, par=par: e.tensor_add(
                outoc[par], e1b[par], xrc[par]),
                waits=[txr] + rel_outoc[par], tok=True)
            rel_e1[par] = [teo]
            rel_xrc[par] = [teo]
            last_out[par] = p.dma("gpsimd", f"sou{par}",
                                  lambda e, oc=oc, st=st, par=par: e.dma_start(
                out=OUT[oc * 128:(oc + 1) * 128, st * STW:(st + 1) * STW],
                in_=outoc[par]), waits=[teo])
            rel_outoc[par] = [last_out[par]]

    p._wait("gpsimd", ("sou0", p.counts["sou0"], None))
    p._wait("gpsimd", ("sou1", p.counts["sou1"], None))
    stC.close()
    stack_all.close()

    with nc.Block() as block:
        @block.tensor
        def _(e):
            for fn in p.ops["tensor"]:
                fn(e)

        @block.vector
        def _(e):
            for fn in p.ops["vector"]:
                fn(e)

        @block.scalar
        def _(e):
            for fn in p.ops["scalar"]:
                fn(e)

        @block.sync
        def _(e):
            for fn in p.ops["sync"]:
                fn(e)

        @block.gpsimd
        def _(e):
            for fn in p.ops["gpsimd"]:
                fn(e)

    return nc


def _stage_inputs(x, wqkv, wout, wffn, bias):
    x = np.ascontiguousarray(np.asarray(x, np.float32).reshape(B * S, E))
    wqkv = np.asarray(wqkv, np.float32)
    wout = np.asarray(wout, np.float32)
    wffn = np.asarray(wffn, np.float32)
    bias = np.ascontiguousarray(np.asarray(bias, np.float32).reshape(E, 1))

    wt = wqkv.T
    cols = []
    for h in range(H):
        cols.append(wt[:, 2 * E + h * D:2 * E + (h + 1) * D])
        cols.append(wt[:, E + h * D:E + (h + 1) * D])
    cols.append(wt[:, 0:E])
    wqkvp = np.ascontiguousarray(np.concatenate(cols, axis=1))

    def strips(w):
        a = w.reshape(NF, 128, NF, 128)
        return np.ascontiguousarray(a.transpose(0, 3, 2, 1).reshape(NF, 128, E))

    wouts = strips(wout)
    wffns = strips(wffn)

    invf = 1.0 / 10000.0 ** (np.arange(0, D, 2, dtype=np.float32) / D)
    pos = np.arange(S, dtype=np.float32)
    emb = np.concatenate([np.outer(pos, invf)] * 2, axis=-1)
    cos_all = np.cos(emb).astype(np.float32)
    sins_all = np.sin(emb).astype(np.float32)
    sins_all[:, 0::2] *= -1.0
    maskut = (np.arange(CHK)[None, :] >= np.arange(CHK)[:, None]).astype(np.float32)
    maskut2 = np.ascontiguousarray(
        np.concatenate([maskut[0:128, :], maskut[128:256, :]], axis=1))
    ident = np.eye(128, dtype=np.float32)
    prot = np.zeros((128, 128), np.float32)
    for m in range(128):
        prot[m + 1 if m % 2 == 0 else m - 1, m] = 1.0
    selmat = np.zeros((16, E), np.float32)
    for h in range(H):
        selmat[h, h * 128:(h + 1) * 128] = 1.0

    in_maps = []
    for c in range(NCORE):
        t0 = c * TPC
        p0 = (c % 4) * TPC
        segmask = np.zeros((128, 4), np.float32)
        segmask[:, 0:(c % 4)] = 1.0
        in_maps.append({
            "XT": np.ascontiguousarray(x[t0:t0 + TPC].T),
            "WQKVP": wqkvp,
            "WOUTS": wouts,
            "WFFNS": wffns,
            "BIAS": bias,
            "COS": np.ascontiguousarray(cos_all[p0:p0 + TPC].T),
            "SINS": np.ascontiguousarray(sins_all[p0:p0 + TPC].T),
            "MASKUT": maskut2,
            "IDENT": ident,
            "PROT": prot,
            "SEGM": segmask,
            "SELMAT": selmat,
        })
    return in_maps


def kernel(L0_attn_x, L0_attn_W_qkv, L0_attn_W_out, L0_ffn_weight, L0_ffn_bias):
    in_maps = _stage_inputs(L0_attn_x, L0_attn_W_qkv, L0_attn_W_out,
                            L0_ffn_weight, L0_ffn_bias)
    nc = build_program()
    res = run_bass_kernel_spmd(nc, in_maps, core_ids=list(range(NCORE)))
    parts = [np.asarray(res.results[c]["OUT"]).T for c in range(NCORE)]
    out = np.concatenate(parts, axis=0).reshape(B, S, E)
    return np.ascontiguousarray(out.astype(np.float32))


# version 21
# speedup vs baseline: 1.0282x; 1.0282x over previous
import contextlib
import numpy as np
import concourse.bass as bass
import concourse.mybir as mybir
from concourse.bass_utils import run_bass_kernel_spmd

F32 = mybir.dt.float32
F32R = mybir.dt.float32r
AF = mybir.ActivationFunctionType
OP = mybir.AluOpType
AX = mybir.AxisListType

B, S, E, H, D, CHK = 2, 4096, 2048, 16, 128, 256
NCORE = 8
TPC = (B * S) // NCORE
CPC = TPC // CHK
NF = E // 128
NST = 2
STW = TPC // NST
GW = 136

_DEBUG_OUTPUTS = False
_SKIP_CC = False


class Prog:
    ENGINES = ("tensor", "vector", "scalar", "sync", "gpsimd")

    def __init__(self, nc):
        self.nc = nc
        self.ops = {e: [] for e in self.ENGINES}
        self.sems = {}
        self.counts = {}
        self.last_wait = {e: {} for e in self.ENGINES}
        for e in self.ENGINES:
            self._mksem("p_" + e)
        self._mksem("cc")
        self.bank_rel = [None] * 8

    def _mksem(self, name):
        if name not in self.sems:
            self.sems[name] = self.nc.alloc_semaphore(name=name)
            self.counts[name] = 0

    def _wait(self, eng, tok):
        if tok is None:
            return
        sem, val, owner = tok
        if val <= 0 or owner == eng:
            return
        lw = self.last_wait[eng]
        if lw.get(sem, 0) >= val:
            return
        lw[sem] = val
        h = self.sems[sem]
        self.ops[eng].append(lambda e, h=h, v=val: e.wait_ge(h, v))

    def op(self, eng, fn, waits=(), tok=False):
        for t in waits:
            self._wait(eng, t)
        if tok:
            pn = "p_" + eng
            h = self.sems[pn]
            self.counts[pn] += 1
            val = self.counts[pn]
            self.ops[eng].append(lambda e, fn=fn, h=h: fn(e).then_inc(h, 1))
            return (pn, val, eng)
        self.ops[eng].append(fn)
        return None

    def dma(self, eng, stream, fn, waits=()):
        self._mksem(stream)
        for t in waits:
            self._wait(eng, t)
        h = self.sems[stream]
        self.counts[stream] += 16
        val = self.counts[stream]
        self.ops[eng].append(lambda e, fn=fn, h=h: fn(e).then_inc(h, 16))
        return (stream, val, None)

    def cc_op(self, fn, waits=()):
        for t in waits:
            self._wait("gpsimd", t)
        h = self.sems["cc"]
        self.counts["cc"] += 1
        val = self.counts["cc"]
        self.ops["gpsimd"].append(lambda e, fn=fn, h=h: fn(e).then_inc(h, 1))
        return ("cc", val, None)

    def barrier(self):
        toks = [(s, self.counts[s], None) for s in self.sems if self.counts[s] > 0]
        for eng in self.ENGINES:
            own = "p_" + eng
            for s, v, _ in toks:
                if s == own:
                    continue
                self._wait(eng, (s, v, None))


def build_program():
    nc = bass.Bass("TRN2", target_bir_lowering=False,
                   detect_race_conditions=False)
    ikind = "ExternalOutput" if _DEBUG_OUTPUTS else "Internal"

    XT = nc.dram_tensor("XT", [E, TPC], F32, kind="ExternalInput").ap()
    WQKVP = nc.dram_tensor("WQKVP", [E, 3 * E], F32, kind="ExternalInput").ap()
    WOUTS = nc.dram_tensor("WOUTS", [NF, 128, E], F32, kind="ExternalInput").ap()
    WFFNS = nc.dram_tensor("WFFNS", [NF, 128, E], F32, kind="ExternalInput").ap()
    BIAS = nc.dram_tensor("BIAS", [E, 1], F32, kind="ExternalInput").ap()
    COS = nc.dram_tensor("COS", [128, TPC], F32, kind="ExternalInput").ap()
    SINS = nc.dram_tensor("SINS", [128, TPC], F32, kind="ExternalInput").ap()
    MASKUT = nc.dram_tensor("MASKUT", [128, 512], F32, kind="ExternalInput").ap()
    IDENT = nc.dram_tensor("IDENT", [128, 128], F32, kind="ExternalInput").ap()
    PROT = nc.dram_tensor("PROT", [128, 128], F32, kind="ExternalInput").ap()
    SEGM = nc.dram_tensor("SEGM", [128, 4], F32, kind="ExternalInput").ap()
    SELMAT = nc.dram_tensor("SELMAT", [16, E], F32, kind="ExternalInput").ap()
    OUT = nc.dram_tensor("OUT", [E, TPC], F32, kind="ExternalOutput").ap()
    QF = nc.dram_tensor("QF", [E, TPC], F32, kind=ikind).ap()
    KF = nc.dram_tensor("KF", [E, TPC], F32, kind=ikind).ap()
    VT = nc.dram_tensor("VT", [H, CPC, 2, 128, D], F32, kind=ikind).ap()
    GS = nc.dram_tensor("GS", [H, CPC, 128, GW], F32, kind=ikind).ap()
    FF = nc.dram_tensor("FF", [E, TPC], F32, kind=ikind).ap()
    XN = nc.dram_tensor("XN", [E, TPC], F32, kind=ikind).ap()
    WOH = nc.dram_tensor("WOH", [NF, 128, E], F32R, kind=ikind).ap()
    WOL = nc.dram_tensor("WOL", [NF, 128, E], F32R, kind=ikind).ap()
    WT = nc.dram_tensor("WT", [NF, 128, E], F32R, kind=ikind).ap()
    AO = nc.dram_tensor("AO", [E, TPC], F32, kind=ikind).ap()
    DEN = nc.dram_tensor("DEN", [16, H * TPC // 16], F32, kind=ikind).ap()
    QRAW = (nc.dram_tensor("QRAW", [E, TPC], F32, kind="ExternalOutput").ap()
            if _DEBUG_OUTPUTS else None)
    SCL = (nc.dram_tensor("SCL", [128, 2], F32, kind="ExternalOutput").ap()
           if _DEBUG_OUTPUTS else None)
    WTS_D = (nc.dram_tensor("WTS_D", [128, E], F32R, kind="ExternalOutput").ap()
             if _DEBUG_OUTPUTS else None)
    XNH_D = (nc.dram_tensor("XNH_D", [128, NF * TPC], F32, kind="ExternalOutput").ap()
             if _DEBUG_OUTPUTS else None)
    XNL_D = (nc.dram_tensor("XNL_D", [128, NF * TPC], F32, kind="ExternalOutput").ap()
             if _DEBUG_OUTPUTS else None)
    WTD_D = (nc.dram_tensor("WTD_D", [NF, 128, E], F32, kind="ExternalOutput").ap()
             if _DEBUG_OUTPUTS else None)
    WTB_D = (nc.dram_tensor("WTB_D", [128, E], F32R, kind="ExternalOutput").ap()
             if _DEBUG_OUTPUTS else None)
    WTS = (nc.dram_tensor("WTS", [128, E], F32, kind="ExternalOutput").ap()
           if _DEBUG_OUTPUTS else None)
    XNC = (nc.dram_tensor("XNC", [128, NF * STW], F32, kind="ExternalOutput").ap()
           if _DEBUG_OUTPUTS else None)
    AOD = (nc.dram_tensor("AOD", [128, NF * STW], F32, kind="ExternalOutput").ap()
           if _DEBUG_OUTPUTS else None)
    ADH = (nc.dram_tensor("ADH", [128, NF * STW], F32R, kind="ExternalOutput").ap()
           if _DEBUG_OUTPUTS else None)
    ADL = (nc.dram_tensor("ADL", [128, NF * STW], F32R, kind="ExternalOutput").ap()
           if _DEBUG_OUTPUTS else None)
    WOHB = (nc.dram_tensor("WOHB", [128, E], F32R, kind="ExternalOutput").ap()
            if _DEBUG_OUTPUTS else None)
    ACC = (nc.dram_tensor("ACC", [128, 34], F32, kind="ExternalOutput").ap()
           if _DEBUG_OUTPUTS else None)
    CCIN = nc.dram_tensor("CCIN", [128, H * GW], F32).ap()
    CCOUT = nc.dram_tensor("CCOUT", [512, H * GW], F32).ap()

    pb = [nc.alloc_psum_tensor(f"pb{i}", [128, 512], F32).ap() for i in range(8)]
    p = Prog(nc)

    stack_all = contextlib.ExitStack()
    sb = lambda name, shape, dt=F32: stack_all.enter_context(
        nc.sbuf_tensor(name, shape, dt)).ap()

    ones_sb = sb("ones_sb", [128, 128])
    segm_sb = sb("segm_sb", [128, 4])
    scale_sb = sb("scale_sb", [128, 1])
    rscaleb_sb = sb("rscaleb_sb", [128, 1])
    thr_sb = sb("thr_sb", [128, 1])
    thrn_sb = sb("thrn_sb", [128, 1])
    acc32 = sb("acc32", [128, 34])
    den16r = sb("den16r", [16, H * TPC // 16])
    wfb = [sb(f"wfb{i}", [128, TPC]) for i in range(2)]
    wtsB = sb("wtsB", [128, E], F32R)
    bfb = [sb(f"bfb{i}", [128, 1]) for i in range(2)]

    stA = contextlib.ExitStack()
    sa = lambda name, shape, dt=F32: stA.enter_context(
        nc.sbuf_tensor(name, shape, dt)).ap()
    cos_sb = sa("cos_sb", [128, TPC])
    sins_sb = sa("sins_sb", [128, TPC])
    ident_sb = sa("ident_sb", [128, 128])
    prot_sb = sa("prot_sb", [128, 128])
    mu_sb = sa("mu_sb", [128, TPC])
    rs_sb = sa("rs_sb", [128, TPC])
    xnh_sb = sa("xnh_sb", [128, NF * TPC], F32R)
    xnl_sb = sa("xnl_sb", [128, NF * TPC], F32R)
    pool = [sa(f"pool{i}", [128, TPC]) for i in range(4)]
    tp1 = sa("tp1s", [128, STW])
    tp2 = sa("tp2s", [128, STW])
    tp3 = sa("tp3s", [128, STW])
    wqb = [sa(f"wqb{i}", [128, 256]) for i in range(2)]
    wqh = [sa(f"wqh{i}", [128, 256], F32R) for i in range(2)]
    wql = [sa(f"wql{i}", [128, 256], F32R) for i in range(2)]

    p.dma("sync", "cst", lambda e: e.dma_start(out=cos_sb, in_=COS))
    p.dma("sync", "cst", lambda e: e.dma_start(out=sins_sb, in_=SINS))
    p.dma("sync", "cst", lambda e: e.dma_start(out=ident_sb, in_=IDENT))
    p.dma("sync", "cst", lambda e: e.dma_start(out=prot_sb, in_=PROT))
    tok_consts = p.dma("sync", "cst", lambda e: e.dma_start(out=segm_sb, in_=SEGM))
    tok_consts = ("cst", p.counts["cst"], None)
    tok_ones = p.op("vector", lambda e: e.memset(ones_sb, 1.0), tok=True)
    vtile_one = sa("vtileA", [128, 8 * 132])
    vtile_all = [vtile_one, vtile_one]
    vtile = [[vtile_one[:, i * 132:(i + 1) * 132] for i in range(8)]
             for q in range(2)]
    ktile1 = [sa(f"ktile_{i}", [128, 128]) for i in range(8)]
    ktile = [ktile1, ktile1]
    gst_all = sa("gstA", [128, CPC * GW])
    gst = [gst_all[:, c * GW:(c + 1) * GW] for c in range(CPC)]
    gseg_sb = sa("gseg_sb", [128, GW])

    tok_vones = None
    for q in range(2):
        for i in range(8):
            p.op("vector", lambda e, q=q, i=i: e.memset(vtile[q][i][:, 0:128], 0.0))
            tok_vones = p.op("vector", lambda e, q=q, i=i: e.memset(
                vtile[q][i][:, 128:132], 1.0), tok=(q == 1 and i == 7))
    for c in range(4):
        p.op("vector", lambda e, c=c: e.memset(gst[c][:, 129:GW], 0.0))
    tok_gz = p.op("vector", lambda e: e.memset(gseg_sb[:, 129:GW], 0.0), tok=True)

    xb = pool[0:2]
    xsq = pool[2:4]
    rel_xb = [[], []]
    rel_xsq = [[], []]
    tok_stats = None
    for f in range(NF):
        tx = p.dma("sync", f"xb{f % 2}", lambda e, f=f: e.dma_start(
            out=xb[f % 2], in_=XT[f * 128:(f + 1) * 128, :]), waits=rel_xb[f % 2])
        tsq = p.op("scalar", lambda e, f=f: e.activation(
            xsq[f % 2], xb[f % 2], AF.Square), waits=[tx] + rel_xsq[f % 2], tok=True)
        tok_s = tok_q = None
        for st in range(NST):
            sl = slice(st * STW, (st + 1) * STW)
            tok_s = p.op("tensor", lambda e, f=f, st=st, sl=sl: e.matmul(
                pb[st][:, :], ones_sb, xb[f % 2][:, sl],
                start=(f == 0), stop=(f == NF - 1)),
                waits=[tx, tok_ones] + ([p.bank_rel[st]] if f == 0 else []),
                tok=(st == NST - 1))
        for st in range(NST):
            sl = slice(st * STW, (st + 1) * STW)
            tok_q = p.op("tensor", lambda e, f=f, st=st, sl=sl: e.matmul(
                pb[2 + st][:, :], ones_sb, xsq[f % 2][:, sl],
                start=(f == 0), stop=(f == NF - 1)),
                waits=[tsq] + ([p.bank_rel[2 + st]] if f == 0 else []),
                tok=(st == NST - 1))
        rel_xb[f % 2] = [tok_s, tsq]
        rel_xsq[f % 2] = [tok_q]
        tok_stats = tok_q

    tmpA, tmpB = pool[2], pool[3]
    for st in range(NST):
        sl = slice(st * STW, (st + 1) * STW)
        p.op("vector", lambda e, st=st, sl=sl: e.tensor_scalar_mul(
            mu_sb[:, sl], pb[st][:, :], 1.0 / E), waits=[tok_stats])
        p.op("vector", lambda e, st=st, sl=sl: e.tensor_scalar_mul(
            tmpB[:, sl], pb[2 + st][:, :], 1.0 / E))
    p.op("vector", lambda e: e.tensor_mul(tmpA, mu_sb, mu_sb))
    p.op("vector", lambda e: e.tensor_sub(tmpB, tmpB, tmpA))
    tok_var = p.op("vector", lambda e: e.tensor_scalar_add(tmpB, tmpB, 1e-5), tok=True)
    for i in range(4):
        p.bank_rel[i] = tok_var
    tok_sd = p.op("scalar", lambda e: e.activation(tmpA, tmpB, AF.Sqrt),
                  waits=[tok_var], tok=True)
    tok_rs = p.op("vector", lambda e: e.reciprocal(rs_sb, tmpA), waits=[tok_sd],
                  tok=True)

    xn_tok = [None] * NF
    rel2 = [rel_xb[0] + [tok_rs], rel_xb[1] + [tok_rs]]
    tmp1, tmp2 = pool[2], pool[3]
    rel_tmp1 = [tok_var]
    last_xn_store = None
    for f in range(NF):
        sl = slice(f * TPC, (f + 1) * TPC)
        tx = p.dma("sync", f"xb{f % 2}", lambda e, f=f: e.dma_start(
            out=xb[f % 2], in_=XT[f * 128:(f + 1) * 128, :]), waits=rel2[f % 2])
        p.op("vector", lambda e, f=f: e.tensor_sub(tmp1, xb[f % 2], mu_sb),
             waits=[tx, tok_rs] + rel_tmp1)
        rel_tmp1 = []
        p.op("vector", lambda e, f=f: e.tensor_mul(tmp2, tmp1, rs_sb))
        rel2[f % 2] = [p.op("vector", lambda e, f=f, sl=sl: e.tensor_copy(
            xnh_sb[:, sl], tmp2), tok=True)]
        xn_tok[f] = p.op("vector", lambda e, f=f, sl=sl: e.tensor_sub(
            xnl_sb[:, sl], tmp2, xnh_sb[:, sl].bitcast(F32)), tok=True)


    rel_tp = {"tp1": [], "tp2": [], "tp3": []}

    def rope_elu(raw, dst, rel_dst, tok_raw, rhbanks):
        dst_tok = None
        raw_rel = None
        for st in range(NST):
            sl = slice(st * STW, (st + 1) * STW)
            bk = rhbanks[st]
            trh = p.op("tensor", lambda e, sl=sl, bk=bk: e.matmul(
                pb[bk][:, :], prot_sb, raw[:, sl], start=True, stop=True),
                waits=[tok_raw, tok_consts] +
                      ([p.bank_rel[bk]] if p.bank_rel[bk] else []), tok=True)
            p.op("vector", lambda e, sl=sl: e.tensor_mul(
                tp1, raw[:, sl], cos_sb[:, sl]),
                waits=[tok_raw, tok_consts] + rel_tp["tp1"])
            tmul2 = p.op("vector", lambda e, sl=sl, bk=bk: e.tensor_mul(
                tp2, pb[bk][:, :], sins_sb[:, sl]),
                waits=[trh] + rel_tp["tp2"], tok=True)
            p.bank_rel[bk] = tmul2
            p.op("vector", lambda e: e.tensor_add(
                tp3, tp1, tp2), waits=rel_tp["tp3"])
            raw_rel = tmin = p.op("vector", lambda e: e.tensor_scalar_min(
                tp1, tp3, 0.0), tok=True)
            texp = p.op("scalar", lambda e: e.activation(
                tp2, tp1, AF.Exp), waits=[tmin], tok=True)
            dst_tok = p.op("vector", lambda e, sl=sl: e.scalar_tensor_tensor(
                dst[:, sl], tp3, 0.0, tp2, OP.max, OP.add),
                waits=[texp] + ([rel_dst] if rel_dst else []), tok=True)
            rel_tp["tp1"] = [texp]
            rel_tp["tp2"] = [dst_tok]
            rel_tp["tp3"] = [dst_tok]
        return dst_tok, raw_rel

    vraw, kraw = pool[0], pool[1]
    kout = [pool[2], pool[3]]
    rel_wq = [[], []]
    rel_vraw = []
    rel_kraw = []
    rel_kout = [[], []]
    rel_vtile1 = [None] * 8
    rel_vtile = [rel_vtile1, rel_vtile1]
    rel_ktile1 = [None] * 8
    rel_ktile = [rel_ktile1, rel_ktile1]
    rel_gst = [None] * 4
    rel_gseg = None
    last_vt_store = None
    last_kf_store = None
    ccin_last = None
    vt_rel_extra = [None, None]
    rel_wcv = [[], []]
    pend = None
    for h in range(H + 1):
        pq = h % 2
        if h < H:
            tok_acc = None
            for f in range(NF):
                fp = f % 2
                tw = p.dma("sync", f"wq{fp}", lambda e, h=h, f=f, fp=fp: e.dma_start(
                    out=wqb[fp],
                    in_=WQKVP[f * 128:(f + 1) * 128, h * 256:(h + 1) * 256]),
                    waits=rel_wq[fp])
                tcv = p.op("vector", lambda e, fp=fp: e.tensor_copy(
                    wqh[fp], wqb[fp]), waits=[tw] + rel_wcv[fp])
                tcv = p.op("vector", lambda e, fp=fp: e.tensor_sub(
                    wql[fp], wqb[fp], wqh[fp].bitcast(F32)), tok=True)
                rel_wq[fp] = [tcv]
                mm_w = [tcv, xn_tok[f]]
                if f == 0:
                    mm_w += [t for t in p.bank_rel[0:4] if t]
                    mm_w += rel_vraw + rel_kraw
                last = None
                for col in range(2):
                    cs = slice(col * 128, (col + 1) * 128)
                    b0 = 2 * col
                    for st in range(NST):
                        sl = slice(f * TPC + st * STW, f * TPC + (st + 1) * STW)
                        p.op("tensor", lambda e, f=f, fp=fp, cs=cs, st=st, sl=sl,
                             b0=b0:
                             e.matmul(pb[b0 + st][:, :], wqh[fp][:, cs],
                                      xnh_sb[:, sl],
                                      start=(f == 0), stop=False), waits=mm_w)
                        p.op("tensor", lambda e, fp=fp, cs=cs, st=st, sl=sl, b0=b0:
                             e.matmul(pb[b0 + st][:, :], wqh[fp][:, cs],
                                      xnl_sb[:, sl],
                                      start=False, stop=False))
                    for st in range(NST):
                        sl = slice(f * TPC + st * STW, f * TPC + (st + 1) * STW)
                        last = p.op("tensor",
                                    lambda e, f=f, fp=fp, cs=cs, st=st, sl=sl,
                                    b0=b0:
                             e.matmul(pb[b0 + st][:, :], wql[fp][:, cs],
                                      xnh_sb[:, sl],
                                      start=False, stop=(f == NF - 1)),
                             tok=(col == 1 and st == NST - 1))
                rel_wcv[fp] = [last]
                tok_acc = last
            ev_v = []
            ev_k = []
            for st in range(NST):
                sl = slice(st * STW, (st + 1) * STW)
                t = p.op("scalar", lambda e, st=st, sl=sl: e.copy(
                    vraw[:, sl], pb[st][:, :]), waits=[tok_acc] + rel_vraw, tok=True)
                ev_v.append(t)
                p.bank_rel[st] = t
            for st in range(NST):
                sl = slice(st * STW, (st + 1) * STW)
                t = p.op("scalar", lambda e, st=st, sl=sl: e.copy(
                    kraw[:, sl], pb[2 + st][:, :]), waits=rel_kraw, tok=True)
                ev_k.append(t)
                p.bank_rel[2 + st] = t
        if pend is not None:
            p1 = pend["p"]
            hh = pend["h"]
            kt_toks = []
            last_ktr = None
            for i in range(8):
                bk = 4 + (i % 2)
                w = [pend["tok_k"]] + ([p.bank_rel[bk]] if p.bank_rel[bk] else [])
                ttr = p.op("tensor", lambda e, i=i, bk=bk, p1=p1: e.transpose(
                    pb[bk][:, 0:128], kout[p1][:, i * 128:(i + 1) * 128], ident_sb),
                    waits=w, tok=True)
                last_ktr = ttr
                tev = p.op("scalar", lambda e, i=i, bk=bk, p1=p1: e.copy(
                    ktile[p1][i], pb[bk][:, 0:128]),
                    waits=[ttr] + ([rel_ktile[p1][i]] if rel_ktile[p1][i] else []),
                    tok=True)
                p.bank_rel[bk] = tev
                kt_toks.append(tev)
            rel_kout[p1] = [last_ktr, pend["kf_store"]]
            gacc_tok = None
            gs_last = None
            gs_evs = []
            vt_toks_p = pend["vt_toks"]
            for c in range(CPC):
                bk = 6 + (c % 2)
                w = [kt_toks[2 * c], kt_toks[2 * c + 1], vt_toks_p[2 * c],
                     vt_toks_p[2 * c + 1]] + ([p.bank_rel[bk]] if p.bank_rel[bk] else [])
                p.op("tensor", lambda e, c=c, bk=bk, p1=p1: e.matmul(
                    pb[bk][:, 0:129], ktile[p1][2 * c], vtile[p1][2 * c][:, 0:129],
                    start=True, stop=False), waits=w)
                tg = p.op("tensor", lambda e, c=c, bk=bk, p1=p1: e.matmul(
                    pb[bk][:, 0:129], ktile[p1][2 * c + 1],
                    vtile[p1][2 * c + 1][:, 0:129],
                    start=False, stop=True), tok=True)
                rel_ktile[p1][2 * c] = tg
                rel_ktile[p1][2 * c + 1] = tg
                rel_vtile[p1][2 * c] = tg
                rel_vtile[p1][2 * c + 1] = tg
                vt_rel_extra[p1] = pend["vt_store"]
                tev = p.op("scalar", lambda e, c=c, bk=bk: e.copy(
                    gst[c][:, 0:129], pb[bk][:, 0:129]),
                    waits=[tg, tok_gz] + ([rel_gst[c]] if rel_gst[c] else []), tok=True)
                p.bank_rel[bk] = tev
                gs_evs.append(tev)
                if c == 0:
                    gacc_tok = p.op("vector", lambda e: e.tensor_copy(
                        gseg_sb[:, 0:129], gst[0][:, 0:129]),
                        waits=[tev] + ([rel_gseg] if rel_gseg else []), tok=True)
                else:
                    gacc_tok = p.op("vector", lambda e, c=c: e.tensor_add(
                        gseg_sb[:, 0:129], gseg_sb[:, 0:129], gst[c][:, 0:129]),
                        waits=[tev, gacc_tok], tok=True)
            gs_last = p.dma("gpsimd", "sgs", lambda e, hh=hh: e.dma_start(
                out=GS[hh].rearrange("c m w -> m c w"),
                in_=gst_all.rearrange("m (c w) -> m c w", w=GW)), waits=gs_evs)
            for c in range(CPC):
                rel_gst[c] = gs_last
            ccin_last = p.dma("gpsimd", "scc", lambda e, hh=hh: e.dma_start(
                out=CCIN[:, hh * GW:(hh + 1) * GW], in_=gseg_sb), waits=[gacc_tok])
            rel_gseg = ccin_last
            pend = None
        if h < H:
            vt_toks = []
            last_tr = None
            for i in range(8):
                bk = 4 + (i % 2)
                w = [ev_v[i // 4], tok_consts] + \
                    ([p.bank_rel[bk]] if p.bank_rel[bk] else [])
                ttr = p.op("tensor", lambda e, i=i, bk=bk, pq=pq: e.transpose(
                    pb[bk][:, 0:128], vraw[:, i * 128:(i + 1) * 128], ident_sb),
                    waits=w, tok=True)
                last_tr = ttr
                tev = p.op("scalar", lambda e, i=i, bk=bk, pq=pq: e.copy(
                    vtile[pq][i][:, 0:128], pb[bk][:, 0:128]),
                    waits=[ttr, tok_vones, vt_rel_extra[pq]] +
                          ([rel_vtile[pq][i]] if rel_vtile[pq][i] else []),
                    tok=True)
                p.bank_rel[bk] = tev
                vt_toks.append(tev)
            last_vt_store = p.dma("gpsimd", "svt",
                                  lambda e, h=h, pq=pq: e.dma_start(
                out=VT[h].rearrange("c t m d -> m c t d"),
                in_=vtile_all[pq].rearrange("m (i w) -> m i w", w=132)[:, :, 0:128]),
                waits=vt_toks)
            for i in range(8):
                rel_vtile[pq][i] = last_vt_store
            rel_vraw = [last_tr]
            tok_k, kraw_rel = rope_elu(kraw, kout[pq],
                                       (rel_kout[pq][0] if rel_kout[pq] else None),
                                       ev_k[1], (6, 7))
            rel_kraw = [kraw_rel]
            last_kf_store = p.dma("gpsimd", "skf", lambda e, h=h, pq=pq: e.dma_start(
                out=KF[h * 128:(h + 1) * 128, :], in_=kout[pq]), waits=[tok_k])
            pend = {"h": h, "p": pq, "tok_k": tok_k, "vt_toks": vt_toks,
                    "vt_store": last_vt_store, "kf_store": last_kf_store}

    if _SKIP_CC:
        for s in range(4):
            tok_cc = p.dma("gpsimd", "scc", lambda e, s=s: e.dma_start(
                out=CCOUT[s * 128:(s + 1) * 128, :], in_=CCIN),
                waits=[ccin_last])
    else:
        tok_cc = p.cc_op(lambda e: e.collective_compute(
            "AllGather", OP.bypass,
            replica_groups=[[0, 1, 2, 3], [4, 5, 6, 7]],
            ins=[CCIN.opt()], outs=[CCOUT.opt()]), waits=[ccin_last])

    qraw = [pool[0], pool[1]]
    qout = [pool[2], pool[3]]
    rel_qraw = [rel_vraw + rel_kraw, list(rel_kraw)]
    rel_wf = [[], []]
    tred = None
    rel_qout = [list(rel_kout[0]), list(rel_kout[1])]
    last_qf_store = [None, None]
    pend3 = None
    for b in range(H // 2 + 1):
        if pend3 is not None:
            bb = pend3["b"]
            for j in range(2):
                hq = 2 * bb + j
                tok_q, raw_rel = rope_elu(qraw[j], qout[j],
                                          (rel_qout[j][0] if rel_qout[j] else None),
                                          pend3["ev"][j][1],
                                          (4, 5) if j == 0 else (6, 7))
                rel_qraw[j] = [raw_rel] + pend3["extra"][j]
                last_qf_store[j] = p.dma("gpsimd", f"sqf{j}",
                                         lambda e, hq=hq, j=j: e.dma_start(
                    out=QF[hq * 128:(hq + 1) * 128, :], in_=qout[j]), waits=[tok_q])
                rel_qout[j] = [last_qf_store[j]]
            pend3 = None
        if b < H // 2:
            tok_acc = None
            for f in range(NF):
                fp = f % 2
                tw = p.dma("sync", f"wq{fp}", lambda e, b=b, f=f, fp=fp: e.dma_start(
                    out=wqb[fp],
                    in_=WQKVP[f * 128:(f + 1) * 128,
                              2 * E + b * 256:2 * E + (b + 1) * 256]),
                    waits=rel_wq[fp])
                tcv = p.op("vector", lambda e, fp=fp: e.tensor_copy(
                    wqh[fp], wqb[fp]), waits=[tw] + rel_wcv[fp])
                tcv = p.op("vector", lambda e, fp=fp: e.tensor_sub(
                    wql[fp], wqb[fp], wqh[fp].bitcast(F32)), tok=True)
                rel_wq[fp] = [tcv]
                mm_w = [tcv, xn_tok[f]]
                if f == 0:
                    mm_w += [t for t in p.bank_rel[0:4] if t]
                last = None
                for col in range(2):
                    cs = slice(col * 128, (col + 1) * 128)
                    b0 = 2 * col
                    for st in range(NST):
                        sl = slice(f * TPC + st * STW, f * TPC + (st + 1) * STW)
                        p.op("tensor", lambda e, f=f, fp=fp, cs=cs, st=st, sl=sl,
                             b0=b0:
                             e.matmul(pb[b0 + st][:, :], wqh[fp][:, cs],
                                      xnh_sb[:, sl],
                                      start=(f == 0), stop=False), waits=mm_w)
                        p.op("tensor", lambda e, fp=fp, cs=cs, st=st, sl=sl, b0=b0:
                             e.matmul(pb[b0 + st][:, :], wqh[fp][:, cs],
                                      xnl_sb[:, sl],
                                      start=False, stop=False))
                    for st in range(NST):
                        sl = slice(f * TPC + st * STW, f * TPC + (st + 1) * STW)
                        last = p.op("tensor",
                                    lambda e, f=f, fp=fp, cs=cs, st=st, sl=sl,
                                    b0=b0:
                             e.matmul(pb[b0 + st][:, :], wql[fp][:, cs],
                                      xnh_sb[:, sl],
                                      start=False, stop=(f == NF - 1)),
                             tok=(col == 1 and st == NST - 1))
                rel_wcv[fp] = [last]
                tok_acc = last
            for k in range(4):
                i = 4 * b + k
                twfb = p.dma("sync", f"lwf{i % 2}", lambda e, i=i: e.dma_start(
                    out=wfb[i % 2],
                    in_=WFFNS[i // 2, :, (i % 2) * TPC:(i % 2 + 1) * TPC]),
                    waits=rel_wf[i % 2])
                tred = p.op("vector", lambda e, i=i: e.tensor_reduce(
                    acc32[:, i:i + 1], wfb[i % 2], AX.X, OP.add,
                    apply_absolute_value=True), waits=[twfb], tok=True)
                rel_wf[i % 2] = [tred]
        if b < H // 2:
            evs = []
            extra = [[], []]
            for j in range(2):
                hq = 2 * b + j
                ev = []
                for st in range(NST):
                    sl = slice(st * STW, (st + 1) * STW)
                    t = p.op("scalar", lambda e, j=j, st=st, sl=sl: e.copy(
                        qraw[j][:, sl], pb[2 * j + st][:, :]),
                        waits=[tok_acc] + rel_qraw[j], tok=True)
                    ev.append(t)
                    p.bank_rel[2 * j + st] = t
                if _DEBUG_OUTPUTS:
                    tqr = p.dma("gpsimd", "sqr", lambda e, hq=hq, j=j: e.dma_start(
                        out=QRAW[hq * 128:(hq + 1) * 128, :], in_=qraw[j]),
                        waits=[ev[0], ev[1]])
                    extra[j] = [tqr]
                evs.append(ev)
            pend3 = {"b": b, "ev": evs, "extra": extra}

    p.op("vector", lambda e: e.memset(gst_all[:, 0:512], 0.0), waits=[tred])
    p.op("vector", lambda e: e.memset(gst_all[:, 0:512], 1.0))
    p.op("vector", lambda e: e.tensor_reduce(
        acc32[:, 32:33], acc32[:, 0:32], AX.X, OP.add))
    p.op("vector", lambda e: e.memset(gst_all[:, 0:512], 0.0))
    tred = p.op("vector", lambda e: e.memset(gst_all[:, 0:512], 1.0), tok=True)
    if _DEBUG_OUTPUTS:
        p.dma("gpsimd", "sdbg", lambda e: e.dma_start(out=ACC, in_=acc32),
              waits=[tred])
    tsc = p.op("tensor", lambda e: e.matmul(
        pb[6][:, 0:1], ones_sb, acc32[:, 32:33], start=True, stop=True),
        waits=[tred] + ([p.bank_rel[6]] if p.bank_rel[6] else []), tok=True)
    tok_scale = p.op("vector", lambda e: e.tensor_scalar_mul(
        scale_sb, pb[6][:, 0:1], 1.0 / (E * E)), waits=[tsc], tok=True)
    p.bank_rel[6] = tok_scale
    p.op("vector", lambda e: e.memset(gst_all[:, 0:512], 0.0))
    p.op("vector", lambda e: e.memset(gst_all[:, 0:512], 1.0))
    p.op("vector", lambda e: e.tensor_scalar_add(rscaleb_sb, scale_sb, 1e-10))
    p.op("vector", lambda e: e.memset(gst_all[:, 0:512], 0.0))
    p.op("vector", lambda e: e.memset(gst_all[:, 0:512], 1.0))
    p.op("vector", lambda e: e.tensor_scalar_mul(thr_sb, rscaleb_sb, 0.5))
    p.op("vector", lambda e: e.tensor_scalar_mul(thrn_sb, rscaleb_sb, -0.5))
    p.op("vector", lambda e: e.memset(gst_all[:, 0:512], 0.0))
    tok_thr = p.op("vector", lambda e: e.memset(gst_all[:, 0:512], 1.0), tok=True)

    if _DEBUG_OUTPUTS:
        p.dma("gpsimd", "sdbg", lambda e: e.dma_start(
            out=SCL[:, 0:1], in_=scale_sb, allow_slow_non_contiguous=True),
            waits=[tok_thr])
        p.dma("gpsimd", "sdbg", lambda e: e.dma_start(
            out=SCL[:, 1:2], in_=thr_sb, allow_slow_non_contiguous=True),
            waits=[tok_thr])
    if _DEBUG_OUTPUTS:
        p.dma("gpsimd", "sdbg", lambda e: e.dma_start(
            out=WTS_D, in_=xnh_sb[:, 0:2048]), waits=[tok_thr])
        p.dma("gpsimd", "sdbg", lambda e: e.dma_start(
            out=XNH_D, in_=xnh_sb.bitcast(F32)), waits=[tok_thr])
        p.dma("gpsimd", "sdbg", lambda e: e.dma_start(
            out=XNL_D, in_=xnl_sb.bitcast(F32)), waits=[tok_thr])
    m2h = pool[0]
    ffb = [pool[1][:, 0:512], pool[1][:, 512:1024]]
    rel_bfw = [rel_wf[0], rel_wf[1]]
    rel_m2 = list(rel_qraw[0])
    rel_wtsB = []
    rel_ffb = [list(rel_qraw[1]), list(rel_qraw[1])]
    rel_bfb = [[], []]
    last_ff_store = [None, None]
    rel_bfbank = [None, None, None, None]
    for oc in range(NF):
        par = oc % 2
        tbi = p.dma("scalar", f"lbf{par}", lambda e, oc=oc, par=par: e.dma_start(
            out=bfb[par], in_=BIAS[oc * 128:(oc + 1) * 128, :]), waits=rel_bfb[par])
        ttn = None
        for half in range(2):
            twf = p.dma("sync", f"lws{half}", lambda e, oc=oc, half=half:
                        e.dma_start(
                out=wfb[half], in_=WFFNS[oc, :, half * TPC:(half + 1) * TPC]),
                waits=rel_bfw[half])
            p.op("vector", lambda e, half=half: e.tensor_scalar(
                m2h, wfb[half], thrn_sb[:, 0:1], None, OP.is_lt),
                waits=[twf, tok_thr] + rel_m2 + rel_wtsB)
            rel_m2 = []
            ttn = p.op("vector", lambda e, half=half: e.scalar_tensor_tensor(
                wtsB[:, half * TPC:(half + 1) * TPC], wfb[half], thr_sb[:, 0:1],
                m2h, OP.is_gt, OP.subtract), tok=True)
            rel_bfw[half] = [ttn]
        tdmp = None
        if _DEBUG_OUTPUTS:
            tdmp = p.dma("gpsimd", "sdbg", lambda e, oc=oc: e.dma_start(
                out=WTD_D[oc], in_=wtsB.bitcast(F32)), waits=[ttn])
        for st in range(NST):
            bk = 2 * par + st
            wmm = [ttn] + ([rel_bfbank[bk]] if rel_bfbank[bk] else [])
            if oc == 0 and st == 0:
                wmm += [t for t in p.bank_rel[0:4] if t]
            tmf = None
            for f in range(NF):
                sl = slice(f * TPC + st * STW, f * TPC + (st + 1) * STW)
                p.op("tensor", lambda e, f=f, sl=sl, bk=bk: e.matmul(
                    pb[bk][:, :], wtsB[:, f * 128:(f + 1) * 128], xnh_sb[:, sl],
                    start=(f == 0), stop=False), waits=wmm)
                tmf = p.op("tensor", lambda e, f=f, sl=sl, bk=bk: e.matmul(
                    pb[bk][:, :], wtsB[:, f * 128:(f + 1) * 128], xnl_sb[:, sl],
                    start=False, stop=(f == NF - 1)), tok=(f == NF - 1))
            te1 = p.op("vector", lambda e, par=par, bk=bk: e.tensor_scalar(
                ffb[par], pb[bk][:, :], scale_sb[:, 0:1], bfb[par][:, 0:1],
                OP.mult, OP.add),
                waits=[tmf, tok_scale, tbi] + rel_ffb[par], tok=True)
            rel_bfbank[bk] = te1
            last_ff_store[par] = p.dma("gpsimd", f"sff{par}",
                                       lambda e, oc=oc, st=st, par=par: e.dma_start(
                out=FF[oc * 128:(oc + 1) * 128, st * STW:(st + 1) * STW],
                in_=ffb[par]), waits=[te1])
            rel_ffb[par] = [last_ff_store[par]]
            rel_wtsB = [tmf]
        if tdmp is not None:
            rel_wtsB = rel_wtsB + [tdmp]
        rel_bfb[par] = [rel_bfbank[2 * par + 1]]

    p.barrier()
    stA.close()

    stB = contextlib.ExitStack()
    sbB = lambda name, shape, dt=F32: stB.enter_context(
        nc.sbuf_tensor(name, shape, dt)).ap()
    mask_sb = sbB("mask_sb", [128, 512])
    den16 = sbB("den16", [16, H * TPC // 16])
    den16b = sbB("den16b", [16, H * TPC // 16])

    tok_mask = p.dma("sync", "cst", lambda e: e.dma_start(out=mask_sb, in_=MASKUT))

    sbase_sb = sbB("sbase_sb", [128, H * GW])
    ccseg = [sbB("ccseg0", [128, H * GW])] * 2
    qh_sb = [[sbB(f"qh{q}_{i}", [128, TPC]) for i in range(2)] for q in range(2)]
    kh_sb = [[sbB(f"kh{q}_{i}", [128, TPC]) for i in range(2)] for q in range(2)]
    vtb = [[sbB(f"vtb{q}_{i}", [128, 256]) for i in range(2)] for q in range(2)]
    gsb = [[sbB(f"gsb{q}_{i}", [128, GW]) for i in range(2)] for q in range(2)]
    scur = [[sbB(f"scur{q}_{i}", [128, 132]) for i in range(CPC)] for q in range(2)]
    attsb = [[sbB(f"attsb{q}_{i}", [128, 512]) for i in range(2)] for q in range(2)]
    aoc = [[sbB(f"aoc{q}_{i}", [128, 256]) for i in range(2)] for q in range(2)]
    densb = [[sbB(f"densb{q}_{i}", [1, 256]) for i in range(2)] for q in range(2)]

    tok_sbase = None
    rel_ccseg = [[], []]
    for s in range(4):
        tcs = p.dma("scalar", "lcs0", lambda e, s=s: e.dma_start(
            out=ccseg[0], in_=CCOUT[s * 128:(s + 1) * 128, :]),
            waits=[tok_cc] + rel_ccseg[0] + rel_ccseg[1])
        if s == 0:
            tok_sbase = p.op("vector", lambda e: e.tensor_scalar(
                sbase_sb, ccseg[0], segm_sb[:, 0:1], None, OP.mult),
                waits=[tcs, tok_consts], tok=True)
        else:
            tok_sbase = p.op("vector", lambda e, s=s: e.scalar_tensor_tensor(
                sbase_sb, ccseg[s % 2], segm_sb[:, s:s + 1], sbase_sb,
                OP.mult, OP.add), waits=[tcs], tok=True)
        rel_ccseg[s % 2] = [tok_sbase]

    rel_qh = [[[last_qf_store[0], last_qf_store[1]] for _ in range(2)]
              for _ in range(2)]
    rel_kh = [[[last_kf_store] for _ in range(2)] for _ in range(2)]
    rel_vtb = [[[last_vt_store], [last_vt_store]], [[last_vt_store], [last_vt_store]]]
    rel_gsb = [[[], []], [[], []]]
    rel_attsb = [[[], []], [[], []]]
    rel_scur = [[[], [], [], []], [[], [], [], []]]
    rel_aoc = [[[], []], [[], []]]
    rel_densb = [[[], []], [[], []]]
    last_den_store = {}
    last_ao_store = {}
    for g in range(H // 2):
        scur_toks = [None, None]
        tq = [None, None]
        tk = [None, None]
        gp = g % 2
        for hp in range(2):
            h = 2 * g + hp
            tq[hp] = p.dma("scalar", f"lqh{hp}_{gp}",
                           lambda e, h=h, hp=hp, gp=gp: e.dma_start(
                out=qh_sb[gp][hp], in_=QF[h * 128:(h + 1) * 128, :]),
                waits=rel_qh[gp][hp])
            tk[hp] = p.dma("scalar", f"lkh{hp}_{gp}",
                           lambda e, h=h, hp=hp, gp=gp: e.dma_start(
                out=kh_sb[gp][hp], in_=KF[h * 128:(h + 1) * 128, :]),
                waits=rel_kh[gp][hp])
        for hp in range(2):
            h = 2 * g + hp
            scur_tok = [None] * CPC
            scur_tok[0] = p.op("vector", lambda e, h=h, hp=hp: e.tensor_copy(
                scur[hp][0][:, 0:129], sbase_sb[:, h * GW:h * GW + 129]),
                waits=[tok_sbase] + rel_scur[hp][0], tok=True)
            for c in range(1, CPC):
                tg = p.dma("scalar", f"lgb{hp}_{c % 2}",
                           lambda e, h=h, hp=hp, c=c: e.dma_start(
                    out=gsb[hp][c % 2], in_=GS[h, c - 1, :, :]),
                    waits=rel_gsb[hp][c % 2] + [rel_gst[c - 1]])
                scur_tok[c] = p.op("vector", lambda e, hp=hp, c=c: e.tensor_add(
                    scur[hp][c][:, 0:129], scur[hp][c - 1][:, 0:129],
                    gsb[hp][c % 2][:, 0:129]),
                    waits=[tg, scur_tok[c - 1]] + rel_scur[hp][c], tok=True)
                rel_gsb[hp][c % 2] = [scur_tok[c]]
            scur_toks[hp] = scur_tok
        tden = {}
        att_last = {}
        for c in range(CPC):
            att_toks_hp = {}
            tvtb_hp = {}
            for hp in range(2):
                h = 2 * g + hp
                csl = slice(c * CHK, (c + 1) * CHK)
                ab = 4 * hp
                tvtb_hp[hp] = p.dma("scalar", f"lvb{hp}_{c % 2}",
                             lambda e, h=h, hp=hp, c=c: e.dma_start(
                    out=vtb[hp][c % 2].rearrange("m (t d) -> m t d", t=2),
                    in_=VT[h, c].rearrange("t m d -> m t d")),
                    waits=rel_vtb[hp][c % 2])
                att_toks = []
                for t in range(2):
                    bk = ab + t
                    wmm = [tq[hp], tk[hp]] + ([p.bank_rel[bk]] if p.bank_rel[bk] else [])
                    ta = p.op("tensor", lambda e, hp=hp, c=c, t=t, csl=csl, bk=bk,
                              gp=gp: e.matmul(
                        pb[bk][:, 0:256],
                        kh_sb[gp][hp][:, c * CHK + t * 128:c * CHK + (t + 1) * 128],
                        qh_sb[gp][hp][:, csl], start=True, stop=True), waits=wmm,
                        tok=True)
                    tm = p.op("vector", lambda e, hp=hp, c=c, t=t, bk=bk: e.tensor_mul(
                        attsb[hp][c % 2][:, t * 256:(t + 1) * 256], pb[bk][:, 0:256],
                        mask_sb[:, t * 256:(t + 1) * 256]),
                        waits=[ta, tok_mask] + rel_attsb[hp][c % 2], tok=True)
                    p.bank_rel[bk] = tm
                    att_toks.append(tm)
                att_last[hp] = att_toks[1]
                att_toks_hp[hp] = att_toks
            for hp in range(2):
                h = 2 * g + hp
                csl = slice(c * CHK, (c + 1) * CHK)
                nb = 2 + 4 * hp
                db = 3 + 4 * hp
                att_toks = att_toks_hp[hp]
                tvtb = tvtb_hp[hp]
                wmm = [att_toks[0], att_toks[1], tvtb, scur_toks[hp][c]]
                wmm += [p.bank_rel[nb]] if p.bank_rel[nb] else []
                p.op("tensor", lambda e, hp=hp, c=c, nb=nb: e.matmul(
                    pb[nb][:, 0:256], vtb[hp][c % 2][:, 0:128],
                    attsb[hp][c % 2][:, 0:256], start=True, stop=False), waits=wmm)
                p.op("tensor", lambda e, hp=hp, c=c, nb=nb: e.matmul(
                    pb[nb][:, 0:256], vtb[hp][c % 2][:, 128:256],
                    attsb[hp][c % 2][:, 256:512], start=False, stop=False))
                tnum = p.op("tensor", lambda e, hp=hp, c=c, nb=nb, csl=csl, gp=gp:
                            e.matmul(
                    pb[nb][:, 0:256], scur[hp][c][:, 0:128], qh_sb[gp][hp][:, csl],
                    start=False, stop=True), tok=True)
                wmm = [p.bank_rel[db]] if p.bank_rel[db] else []
                p.op("tensor", lambda e, hp=hp, c=c, db=db: e.matmul(
                    pb[db][0:1, 0:256], ones_sb[:, 0:1],
                    attsb[hp][c % 2][:, 0:256], start=True, stop=False), waits=wmm)
                p.op("tensor", lambda e, hp=hp, c=c, db=db: e.matmul(
                    pb[db][0:1, 0:256], ones_sb[:, 0:1],
                    attsb[hp][c % 2][:, 256:512], start=False, stop=False))
                tden[hp] = p.op("tensor", lambda e, hp=hp, c=c, db=db, csl=csl,
                                gp=gp: e.matmul(
                    pb[db][0:1, 0:256], scur[hp][c][:, 128:129],
                    qh_sb[gp][hp][:, csl], start=False, stop=True), tok=True)
                rel_scur[hp][c] = [tden[hp]]
                rel_attsb[hp][c % 2] = [tden[hp]]
                rel_vtb[hp][c % 2] = [tnum]
                tev = p.op("scalar", lambda e, hp=hp, c=c, nb=nb: e.copy(
                    aoc[hp][c % 2], pb[nb][:, 0:256]),
                    waits=[tnum] + rel_aoc[hp][c % 2], tok=True)
                p.bank_rel[nb] = tev
                last_ao_store[hp, c % 2] = p.dma(
                    "gpsimd", f"sao{hp}_{c % 2}",
                    lambda e, h=h, hp=hp, c=c: e.dma_start(
                        out=AO[h * 128:(h + 1) * 128, c * CHK:(c + 1) * CHK],
                        in_=aoc[hp][c % 2]), waits=[tev])
                rel_aoc[hp][c % 2] = [last_ao_store[hp, c % 2]]
                tevd = p.op("scalar", lambda e, hp=hp, c=c, db=db: e.copy(
                    densb[hp][c % 2], pb[db][0:1, 0:256]),
                    waits=[tden[hp]] + rel_densb[hp][c % 2], tok=True)
                p.bank_rel[db] = tevd
                last_den_store[hp, c % 2] = p.dma(
                    "gpsimd", f"sdn{hp}_{c % 2}",
                    lambda e, h=h, hp=hp, c=c: e.dma_start(
                        out=DEN[h:h + 1, c * CHK:(c + 1) * CHK],
                        in_=densb[hp][c % 2]), waits=[tevd])
                rel_densb[hp][c % 2] = [last_den_store[hp, c % 2]]
        for hp in range(2):
            rel_qh[gp][hp] = [tden[hp]]
            rel_kh[gp][hp] = [att_last[hp]]

    tdr = p.dma("scalar", "ldn", lambda e: e.dma_start(out=den16, in_=DEN),
                waits=[last_den_store[k] for k in last_den_store])
    p.op("vector", lambda e: e.tensor_scalar_max(den16b, den16, 1e-6), waits=[tdr])
    tok_recip = p.op("vector", lambda e: e.reciprocal(den16r, den16b), tok=True)

    p.barrier()
    stB.close()

    stC = contextlib.ExitStack()
    sc = lambda name, shape, dt=F32: stC.enter_context(
        nc.sbuf_tensor(name, shape, dt)).ap()
    aodh = sc("aodh", [128, NF * STW], F32R)
    aodl = sc("aodl", [128, NF * STW], F32R)
    aotmp = [sc(f"aotmp{i}", [128, STW]) for i in range(2)]
    sett = sc("sett", [128, 256])
    wcp = [sc(f"wcp{i}", [128, E]) for i in range(2)]
    woh = [sc(f"woh{i}", [128, E], F32R) for i in range(2)]
    wol = [sc(f"wol{i}", [128, E], F32R) for i in range(2)]
    ffld = [sc(f"ffld{i}", [128, STW]) for i in range(2)]
    selmat_sb = sc("selmat_sb", [16, E])
    e1b = [sc(f"e1b{i}", [128, STW]) for i in range(2)]
    outoc = [sc(f"outoc{i}", [128, STW]) for i in range(2)]
    xrc = [sc(f"xrc{i}", [128, STW]) for i in range(2)]

    tok_sel = p.dma("scalar", "lsel", lambda e: e.dma_start(
        out=selmat_sb, in_=SELMAT))
    rel_pass_ao = [last_ao_store[k] for k in last_ao_store]
    rel_woh = [[], []]
    rel_wcp = [[], []]
    rel_ffld = [[last_ff_store[0], last_ff_store[1]], []]
    rel_aotmp = [[], []]
    rel_e1 = [[], []]
    rel_outoc = [[], []]
    rel_xrc = [[], []]
    last_out = [None, None]
    for st in range(NST):
        div_toks = [None] * NF
        for f in range(NF):
            sl = slice(f * STW, (f + 1) * STW)
            fp = f % 2
            tld = p.dma("scalar", f"lao{fp}", lambda e, f=f, st=st, fp=fp: e.dma_start(
                out=aotmp[fp],
                in_=AO[f * 128:(f + 1) * 128, st * STW:(st + 1) * STW]),
                waits=(rel_pass_ao if f < 2 else []) + rel_aotmp[fp])
            bk = 6 + fp
            wmm = [tok_recip, tok_sel] + ([p.bank_rel[bk]] if p.bank_rel[bk] else [])
            tprr = p.op("tensor", lambda e, f=f, st=st, bk=bk: e.matmul(
                pb[bk][:, :], selmat_sb[:, f * 128:(f + 1) * 128],
                den16r[:, st * STW:(st + 1) * STW], start=True, stop=True),
                waits=wmm, tok=True)
            p.op("vector", lambda e, fp=fp, bk=bk: e.tensor_mul(
                aotmp[fp], aotmp[fp], pb[bk][:, :]), waits=[tprr, tld])
            tdv = p.op("vector", lambda e, fp=fp, sl=sl: e.tensor_copy(
                aodh[:, sl], aotmp[fp]), tok=True)
            p.bank_rel[bk] = tdv
            div_toks[f] = p.op("vector", lambda e, fp=fp, sl=sl: e.tensor_sub(
                aodl[:, sl], aotmp[fp], aodh[:, sl].bitcast(F32)), tok=True)
            rel_aotmp[fp] = [div_toks[f]]
        p.op("vector", lambda e: e.memset(sett, 0.0), waits=[div_toks[NF - 1]])
        div_all = p.op("vector", lambda e: e.memset(sett, 1.0), tok=True)
        rel_pass_ao = []
        if _DEBUG_OUTPUTS and st == 0:
            p.dma("gpsimd", "sdbg", lambda e: e.dma_start(out=ADH, in_=aodh),
                  waits=[div_toks[NF - 1]])
            p.dma("gpsimd", "sdbg", lambda e: e.dma_start(out=ADL, in_=aodl),
                  waits=[div_toks[NF - 1]])
        for oc in range(NF):
            par = oc % 2
            bw, bf_ = 2 * par, 2 * par + 1
            tld = p.dma("sync", f"lwoh{par}", lambda e, oc=oc, par=par: e.dma_start(
                out=wcp[par], in_=WOUTS[oc]), waits=rel_wcp[par])
            p.op("vector", lambda e, par=par: e.tensor_copy(woh[par], wcp[par]),
                 waits=[tld] + rel_woh[par])
            twoh = p.op("vector", lambda e, par=par: e.tensor_sub(
                wol[par], wcp[par], woh[par].bitcast(F32)), tok=True)
            rel_wcp[par] = [twoh]
            if _DEBUG_OUTPUTS and st == 0 and oc == 0:
                p.dma("gpsimd", "sdbg", lambda e: e.dma_start(out=WOHB, in_=woh[0]),
                      waits=[twoh])
            tff = p.dma("scalar", f"lff{par}", lambda e, oc=oc, st=st, par=par:
                        e.dma_start(
                out=ffld[par], in_=FF[oc * 128:(oc + 1) * 128,
                                      st * STW:(st + 1) * STW]),
                waits=rel_ffld[par])
            txr = p.dma("scalar", f"lxr{par}", lambda e, oc=oc, st=st, par=par:
                        e.dma_start(
                out=xrc[par], in_=XT[oc * 128:(oc + 1) * 128,
                                     st * STW:(st + 1) * STW]), waits=rel_xrc[par])
            tok_mm1 = None
            for f in range(NF):
                sl = slice(f * STW, (f + 1) * STW)
                wmm = [twoh, div_all]
                if f == 0:
                    wmm += [p.bank_rel[bw]] if p.bank_rel[bw] else []
                p.op("tensor", lambda e, f=f, sl=sl, bw=bw, par=par: e.matmul(
                    pb[bw][:, :], woh[par][:, f * 128:(f + 1) * 128], aodh[:, sl],
                    start=(f == 0), stop=False), waits=wmm)
                p.op("tensor", lambda e, f=f, sl=sl, bw=bw, par=par: e.matmul(
                    pb[bw][:, :], woh[par][:, f * 128:(f + 1) * 128], aodl[:, sl],
                    start=False, stop=False))
            for f in range(NF):
                sl = slice(f * STW, (f + 1) * STW)
                tok_mm1 = p.op("tensor", lambda e, f=f, sl=sl, bw=bw, par=par:
                               e.matmul(
                    pb[bw][:, :], wol[par][:, f * 128:(f + 1) * 128], aodh[:, sl],
                    start=False, stop=(f == NF - 1)),
                    tok=(f == NF - 1))
            rel_woh[par] = [tok_mm1]
            if oc >= NF - 2:
                rel_pass_ao.append(tok_mm1)
            t2 = p.op("vector", lambda e, par=par, bw=bw: e.tensor_add(
                e1b[par], ffld[par], pb[bw][:, :]),
                waits=[tok_mm1, tff] + rel_e1[par], tok=True)
            p.bank_rel[bw] = t2
            rel_ffld[par] = [t2]
            teo = p.op("vector", lambda e, par=par: e.tensor_add(
                outoc[par], e1b[par], xrc[par]),
                waits=[txr] + rel_outoc[par], tok=True)
            rel_e1[par] = [teo]
            rel_xrc[par] = [teo]
            last_out[par] = p.dma("gpsimd", f"sou{par}",
                                  lambda e, oc=oc, st=st, par=par: e.dma_start(
                out=OUT[oc * 128:(oc + 1) * 128, st * STW:(st + 1) * STW],
                in_=outoc[par]), waits=[teo])
            rel_outoc[par] = [last_out[par]]

    p._wait("gpsimd", ("sou0", p.counts["sou0"], None))
    p._wait("gpsimd", ("sou1", p.counts["sou1"], None))
    stC.close()
    stack_all.close()

    with nc.Block() as block:
        @block.tensor
        def _(e):
            for fn in p.ops["tensor"]:
                fn(e)

        @block.vector
        def _(e):
            for fn in p.ops["vector"]:
                fn(e)

        @block.scalar
        def _(e):
            for fn in p.ops["scalar"]:
                fn(e)

        @block.sync
        def _(e):
            for fn in p.ops["sync"]:
                fn(e)

        @block.gpsimd
        def _(e):
            for fn in p.ops["gpsimd"]:
                fn(e)

    return nc


def _stage_inputs(x, wqkv, wout, wffn, bias):
    x = np.ascontiguousarray(np.asarray(x, np.float32).reshape(B * S, E))
    wqkv = np.asarray(wqkv, np.float32)
    wout = np.asarray(wout, np.float32)
    wffn = np.asarray(wffn, np.float32)
    bias = np.ascontiguousarray(np.asarray(bias, np.float32).reshape(E, 1))

    wt = wqkv.T
    cols = []
    for h in range(H):
        cols.append(wt[:, 2 * E + h * D:2 * E + (h + 1) * D])
        cols.append(wt[:, E + h * D:E + (h + 1) * D])
    cols.append(wt[:, 0:E])
    wqkvp = np.ascontiguousarray(np.concatenate(cols, axis=1))

    def strips(w):
        a = w.reshape(NF, 128, NF, 128)
        return np.ascontiguousarray(a.transpose(0, 3, 2, 1).reshape(NF, 128, E))

    wouts = strips(wout)
    wffns = strips(wffn)

    invf = 1.0 / 10000.0 ** (np.arange(0, D, 2, dtype=np.float32) / D)
    pos = np.arange(S, dtype=np.float32)
    emb = np.concatenate([np.outer(pos, invf)] * 2, axis=-1)
    cos_all = np.cos(emb).astype(np.float32)
    sins_all = np.sin(emb).astype(np.float32)
    sins_all[:, 0::2] *= -1.0
    maskut = (np.arange(CHK)[None, :] >= np.arange(CHK)[:, None]).astype(np.float32)
    maskut2 = np.ascontiguousarray(
        np.concatenate([maskut[0:128, :], maskut[128:256, :]], axis=1))
    ident = np.eye(128, dtype=np.float32)
    prot = np.zeros((128, 128), np.float32)
    for m in range(128):
        prot[m + 1 if m % 2 == 0 else m - 1, m] = 1.0
    selmat = np.zeros((16, E), np.float32)
    for h in range(H):
        selmat[h, h * 128:(h + 1) * 128] = 1.0

    in_maps = []
    for c in range(NCORE):
        t0 = c * TPC
        p0 = (c % 4) * TPC
        segmask = np.zeros((128, 4), np.float32)
        segmask[:, 0:(c % 4)] = 1.0
        in_maps.append({
            "XT": np.ascontiguousarray(x[t0:t0 + TPC].T),
            "WQKVP": wqkvp,
            "WOUTS": wouts,
            "WFFNS": wffns,
            "BIAS": bias,
            "COS": np.ascontiguousarray(cos_all[p0:p0 + TPC].T),
            "SINS": np.ascontiguousarray(sins_all[p0:p0 + TPC].T),
            "MASKUT": maskut2,
            "IDENT": ident,
            "PROT": prot,
            "SEGM": segmask,
            "SELMAT": selmat,
        })
    return in_maps


def kernel(L0_attn_x, L0_attn_W_qkv, L0_attn_W_out, L0_ffn_weight, L0_ffn_bias):
    in_maps = _stage_inputs(L0_attn_x, L0_attn_W_qkv, L0_attn_W_out,
                            L0_ffn_weight, L0_ffn_bias)
    nc = build_program()
    res = run_bass_kernel_spmd(nc, in_maps, core_ids=list(range(NCORE)))
    parts = [np.asarray(res.results[c]["OUT"]).T for c in range(NCORE)]
    out = np.concatenate(parts, axis=0).reshape(B, S, E)
    return np.ascontiguousarray(out.astype(np.float32))


# version 23
# speedup vs baseline: 1.0385x; 1.0100x over previous
import contextlib
import numpy as np
import concourse.bass as bass
import concourse.mybir as mybir
from concourse.bass_utils import run_bass_kernel_spmd

F32 = mybir.dt.float32
F32R = mybir.dt.float32r
AF = mybir.ActivationFunctionType
OP = mybir.AluOpType
AX = mybir.AxisListType

B, S, E, H, D, CHK = 2, 4096, 2048, 16, 128, 256
NCORE = 8
TPC = (B * S) // NCORE
CPC = TPC // CHK
NF = E // 128
NST = 2
STW = TPC // NST
GW = 136

_DEBUG_OUTPUTS = False
_SKIP_CC = False


class Prog:
    ENGINES = ("tensor", "vector", "scalar", "sync", "gpsimd")

    def __init__(self, nc):
        self.nc = nc
        self.ops = {e: [] for e in self.ENGINES}
        self.sems = {}
        self.counts = {}
        self.last_wait = {e: {} for e in self.ENGINES}
        for e in self.ENGINES:
            self._mksem("p_" + e)
        self._mksem("cc")
        self.bank_rel = [None] * 8

    def _mksem(self, name):
        if name not in self.sems:
            self.sems[name] = self.nc.alloc_semaphore(name=name)
            self.counts[name] = 0

    def _wait(self, eng, tok):
        if tok is None:
            return
        sem, val, owner = tok
        if val <= 0 or owner == eng:
            return
        lw = self.last_wait[eng]
        if lw.get(sem, 0) >= val:
            return
        lw[sem] = val
        h = self.sems[sem]
        self.ops[eng].append(lambda e, h=h, v=val: e.wait_ge(h, v))

    def op(self, eng, fn, waits=(), tok=False):
        for t in waits:
            self._wait(eng, t)
        if tok:
            pn = "p_" + eng
            h = self.sems[pn]
            self.counts[pn] += 1
            val = self.counts[pn]
            self.ops[eng].append(lambda e, fn=fn, h=h: fn(e).then_inc(h, 1))
            return (pn, val, eng)
        self.ops[eng].append(fn)
        return None

    def dma(self, eng, stream, fn, waits=()):
        self._mksem(stream)
        for t in waits:
            self._wait(eng, t)
        h = self.sems[stream]
        self.counts[stream] += 16
        val = self.counts[stream]
        self.ops[eng].append(lambda e, fn=fn, h=h: fn(e).then_inc(h, 16))
        return (stream, val, None)

    def cc_op(self, fn, waits=()):
        for t in waits:
            self._wait("gpsimd", t)
        h = self.sems["cc"]
        self.counts["cc"] += 1
        val = self.counts["cc"]
        self.ops["gpsimd"].append(lambda e, fn=fn, h=h: fn(e).then_inc(h, 1))
        return ("cc", val, None)

    def barrier(self):
        toks = [(s, self.counts[s], None) for s in self.sems if self.counts[s] > 0]
        for eng in self.ENGINES:
            own = "p_" + eng
            for s, v, _ in toks:
                if s == own:
                    continue
                self._wait(eng, (s, v, None))


def build_program():
    nc = bass.Bass("TRN2", target_bir_lowering=False,
                   detect_race_conditions=False)
    ikind = "ExternalOutput" if _DEBUG_OUTPUTS else "Internal"

    XT = nc.dram_tensor("XT", [E, TPC], F32, kind="ExternalInput").ap()
    WQKVP = nc.dram_tensor("WQKVP", [E, 3 * E], F32, kind="ExternalInput").ap()
    WOUTS = nc.dram_tensor("WOUTS", [NF, 128, E], F32, kind="ExternalInput").ap()
    WFFNS = nc.dram_tensor("WFFNS", [NF, 128, E], F32, kind="ExternalInput").ap()
    BIAS = nc.dram_tensor("BIAS", [E, 1], F32, kind="ExternalInput").ap()
    COS = nc.dram_tensor("COS", [128, TPC], F32, kind="ExternalInput").ap()
    SINS = nc.dram_tensor("SINS", [128, TPC], F32, kind="ExternalInput").ap()
    MASKUT = nc.dram_tensor("MASKUT", [128, 512], F32, kind="ExternalInput").ap()
    IDENT = nc.dram_tensor("IDENT", [128, 128], F32, kind="ExternalInput").ap()
    PROT = nc.dram_tensor("PROT", [128, 128], F32, kind="ExternalInput").ap()
    SEGM = nc.dram_tensor("SEGM", [128, 4], F32, kind="ExternalInput").ap()
    SELMAT = nc.dram_tensor("SELMAT", [16, E], F32, kind="ExternalInput").ap()
    OUT = nc.dram_tensor("OUT", [E, TPC], F32, kind="ExternalOutput").ap()
    QF = nc.dram_tensor("QF", [E, TPC], F32, kind=ikind).ap()
    KF = nc.dram_tensor("KF", [E, TPC], F32, kind=ikind).ap()
    VT = nc.dram_tensor("VT", [H, CPC, 2, 128, D], F32, kind=ikind).ap()
    GS = nc.dram_tensor("GS", [H, CPC, 128, GW], F32, kind=ikind).ap()
    FF = nc.dram_tensor("FF", [E, TPC], F32, kind=ikind).ap()
    XN = nc.dram_tensor("XN", [E, TPC], F32, kind=ikind).ap()
    WOH = nc.dram_tensor("WOH", [NF, 128, E], F32R, kind=ikind).ap()
    WOL = nc.dram_tensor("WOL", [NF, 128, E], F32R, kind=ikind).ap()
    WT = nc.dram_tensor("WT", [NF, 128, E], F32R, kind=ikind).ap()
    AO = nc.dram_tensor("AO", [E, TPC], F32, kind=ikind).ap()
    DEN = nc.dram_tensor("DEN", [16, H * TPC // 16], F32, kind=ikind).ap()
    QRAW = (nc.dram_tensor("QRAW", [E, TPC], F32, kind="ExternalOutput").ap()
            if _DEBUG_OUTPUTS else None)
    SCL = (nc.dram_tensor("SCL", [128, 2], F32, kind="ExternalOutput").ap()
           if _DEBUG_OUTPUTS else None)
    WTS_D = (nc.dram_tensor("WTS_D", [128, E], F32R, kind="ExternalOutput").ap()
             if _DEBUG_OUTPUTS else None)
    XNH_D = (nc.dram_tensor("XNH_D", [128, NF * TPC], F32, kind="ExternalOutput").ap()
             if _DEBUG_OUTPUTS else None)
    XNL_D = (nc.dram_tensor("XNL_D", [128, NF * TPC], F32, kind="ExternalOutput").ap()
             if _DEBUG_OUTPUTS else None)
    WTD_D = (nc.dram_tensor("WTD_D", [NF, 128, E], F32, kind="ExternalOutput").ap()
             if _DEBUG_OUTPUTS else None)
    WTB_D = (nc.dram_tensor("WTB_D", [128, E], F32R, kind="ExternalOutput").ap()
             if _DEBUG_OUTPUTS else None)
    WTS = (nc.dram_tensor("WTS", [128, E], F32, kind="ExternalOutput").ap()
           if _DEBUG_OUTPUTS else None)
    XNC = (nc.dram_tensor("XNC", [128, NF * STW], F32, kind="ExternalOutput").ap()
           if _DEBUG_OUTPUTS else None)
    AOD = (nc.dram_tensor("AOD", [128, NF * STW], F32, kind="ExternalOutput").ap()
           if _DEBUG_OUTPUTS else None)
    ADH = (nc.dram_tensor("ADH", [128, NF * STW], F32R, kind="ExternalOutput").ap()
           if _DEBUG_OUTPUTS else None)
    ADL = (nc.dram_tensor("ADL", [128, NF * STW], F32R, kind="ExternalOutput").ap()
           if _DEBUG_OUTPUTS else None)
    WOHB = (nc.dram_tensor("WOHB", [128, E], F32R, kind="ExternalOutput").ap()
            if _DEBUG_OUTPUTS else None)
    ACC = (nc.dram_tensor("ACC", [128, 34], F32, kind="ExternalOutput").ap()
           if _DEBUG_OUTPUTS else None)
    CCIN = nc.dram_tensor("CCIN", [128, H * GW], F32).ap()
    CCOUT = nc.dram_tensor("CCOUT", [512, H * GW], F32).ap()

    pb = [nc.alloc_psum_tensor(f"pb{i}", [128, 512], F32).ap() for i in range(8)]
    p = Prog(nc)

    stack_all = contextlib.ExitStack()
    sb = lambda name, shape, dt=F32: stack_all.enter_context(
        nc.sbuf_tensor(name, shape, dt)).ap()

    ones_sb = sb("ones_sb", [128, 128])
    segm_sb = sb("segm_sb", [128, 4])
    scale_sb = sb("scale_sb", [128, 1])
    rscaleb_sb = sb("rscaleb_sb", [128, 1])
    thr_sb = sb("thr_sb", [128, 1])
    thrn_sb = sb("thrn_sb", [128, 1])
    acc32 = sb("acc32", [128, 34])
    den16r = sb("den16r", [16, H * TPC // 16])
    wfb = [sb(f"wfb{i}", [128, TPC]) for i in range(2)]
    wtsB = sb("wtsB", [128, E], F32R)
    bfb = [sb(f"bfb{i}", [128, 1]) for i in range(2)]

    stA = contextlib.ExitStack()
    sa = lambda name, shape, dt=F32: stA.enter_context(
        nc.sbuf_tensor(name, shape, dt)).ap()
    cos_sb = sa("cos_sb", [128, TPC])
    sins_sb = sa("sins_sb", [128, TPC])
    ident_sb = sa("ident_sb", [128, 128])
    prot_sb = sa("prot_sb", [128, 128])
    mu_sb = sa("mu_sb", [128, TPC])
    rs_sb = sa("rs_sb", [128, TPC])
    xnh_sb = sa("xnh_sb", [128, NF * TPC], F32R)
    xnl_sb = sa("xnl_sb", [128, NF * TPC], F32R)
    pool = [sa(f"pool{i}", [128, TPC]) for i in range(4)]
    tp1 = sa("tp1s", [128, STW])
    tp2 = sa("tp2s", [128, STW])
    tp3 = sa("tp3s", [128, STW])
    wqb = [sa(f"wqb{i}", [128, 256]) for i in range(2)]
    wqh = [sa(f"wqh{i}", [128, 256], F32R) for i in range(2)]
    wql = [sa(f"wql{i}", [128, 256], F32R) for i in range(2)]

    p.dma("sync", "cst", lambda e: e.dma_start(out=cos_sb, in_=COS))
    p.dma("sync", "cst", lambda e: e.dma_start(out=sins_sb, in_=SINS))
    p.dma("sync", "cst", lambda e: e.dma_start(out=ident_sb, in_=IDENT))
    p.dma("sync", "cst", lambda e: e.dma_start(out=prot_sb, in_=PROT))
    tok_consts = p.dma("sync", "cst", lambda e: e.dma_start(out=segm_sb, in_=SEGM))
    tok_consts = ("cst", p.counts["cst"], None)
    tok_ones = p.op("vector", lambda e: e.memset(ones_sb, 1.0), tok=True)
    vtile_one = sa("vtileA", [128, 8 * 132])
    vtile_all = [vtile_one, vtile_one]
    vtile = [[vtile_one[:, i * 132:(i + 1) * 132] for i in range(8)]
             for q in range(2)]
    ktile1 = [sa(f"ktile_{i}", [128, 128]) for i in range(8)]
    ktile = [ktile1, ktile1]
    gst_all = sa("gstA", [128, CPC * GW])
    gst = [gst_all[:, c * GW:(c + 1) * GW] for c in range(CPC)]
    gseg_sb = sa("gseg_sb", [128, GW])

    tok_vones = None
    for q in range(2):
        for i in range(8):
            p.op("vector", lambda e, q=q, i=i: e.memset(vtile[q][i][:, 0:128], 0.0))
            tok_vones = p.op("vector", lambda e, q=q, i=i: e.memset(
                vtile[q][i][:, 128:132], 1.0), tok=(q == 1 and i == 7))
    for c in range(4):
        p.op("vector", lambda e, c=c: e.memset(gst[c][:, 129:GW], 0.0))
    tok_gz = p.op("vector", lambda e: e.memset(gseg_sb[:, 129:GW], 0.0), tok=True)

    xb = pool[0:2]
    xsq = pool[2:4]
    rel_xb = [[], []]
    rel_xsq = [[], []]
    tok_stats = None
    for f in range(NF):
        tx = p.dma("sync", f"xb{f % 2}", lambda e, f=f: e.dma_start(
            out=xb[f % 2], in_=XT[f * 128:(f + 1) * 128, :]), waits=rel_xb[f % 2])
        tsq = p.op("scalar", lambda e, f=f: e.activation(
            xsq[f % 2], xb[f % 2], AF.Square), waits=[tx] + rel_xsq[f % 2], tok=True)
        tok_s = tok_q = None
        for st in range(NST):
            sl = slice(st * STW, (st + 1) * STW)
            tok_s = p.op("tensor", lambda e, f=f, st=st, sl=sl: e.matmul(
                pb[st][:, :], ones_sb, xb[f % 2][:, sl],
                start=(f == 0), stop=(f == NF - 1)),
                waits=[tx, tok_ones] + ([p.bank_rel[st]] if f == 0 else []),
                tok=(st == NST - 1))
        for st in range(NST):
            sl = slice(st * STW, (st + 1) * STW)
            tok_q = p.op("tensor", lambda e, f=f, st=st, sl=sl: e.matmul(
                pb[2 + st][:, :], ones_sb, xsq[f % 2][:, sl],
                start=(f == 0), stop=(f == NF - 1)),
                waits=[tsq] + ([p.bank_rel[2 + st]] if f == 0 else []),
                tok=(st == NST - 1))
        rel_xb[f % 2] = [tok_s, tsq]
        rel_xsq[f % 2] = [tok_q]
        tok_stats = tok_q

    tmpA, tmpB = pool[2], pool[3]
    for st in range(NST):
        sl = slice(st * STW, (st + 1) * STW)
        p.op("vector", lambda e, st=st, sl=sl: e.tensor_scalar_mul(
            mu_sb[:, sl], pb[st][:, :], 1.0 / E), waits=[tok_stats])
        p.op("vector", lambda e, st=st, sl=sl: e.tensor_scalar_mul(
            tmpB[:, sl], pb[2 + st][:, :], 1.0 / E))
    p.op("vector", lambda e: e.tensor_mul(tmpA, mu_sb, mu_sb))
    p.op("vector", lambda e: e.tensor_sub(tmpB, tmpB, tmpA))
    tok_var = p.op("vector", lambda e: e.tensor_scalar_add(tmpB, tmpB, 1e-5), tok=True)
    for i in range(4):
        p.bank_rel[i] = tok_var
    tok_sd = p.op("scalar", lambda e: e.activation(tmpA, tmpB, AF.Sqrt),
                  waits=[tok_var], tok=True)
    tok_rs = p.op("vector", lambda e: e.reciprocal(rs_sb, tmpA), waits=[tok_sd],
                  tok=True)

    xn_tok = [None] * NF
    rel2 = [rel_xb[0] + [tok_rs], rel_xb[1] + [tok_rs]]
    tmp1, tmp2 = pool[2], pool[3]
    rel_tmp1 = [tok_var]
    last_xn_store = None
    for f in range(NF):
        sl = slice(f * TPC, (f + 1) * TPC)
        tx = p.dma("sync", f"xb{f % 2}", lambda e, f=f: e.dma_start(
            out=xb[f % 2], in_=XT[f * 128:(f + 1) * 128, :]), waits=rel2[f % 2])
        p.op("vector", lambda e, f=f: e.tensor_sub(tmp1, xb[f % 2], mu_sb),
             waits=[tx, tok_rs] + rel_tmp1)
        rel_tmp1 = []
        p.op("vector", lambda e, f=f: e.tensor_mul(tmp2, tmp1, rs_sb))
        rel2[f % 2] = [p.op("vector", lambda e, f=f, sl=sl: e.tensor_copy(
            xnh_sb[:, sl], tmp2), tok=True)]
        xn_tok[f] = p.op("vector", lambda e, f=f, sl=sl: e.tensor_sub(
            xnl_sb[:, sl], tmp2, xnh_sb[:, sl].bitcast(F32)), tok=True)


    rel_tp = {"tp1": [], "tp2": [], "tp3": []}

    def rope_elu(raw, dst, rel_dst, tok_raw, rhbanks):
        dst_tok = None
        raw_rel = None
        for st in range(NST):
            sl = slice(st * STW, (st + 1) * STW)
            bk = rhbanks[st]
            trh = p.op("tensor", lambda e, sl=sl, bk=bk: e.matmul(
                pb[bk][:, :], prot_sb, raw[:, sl], start=True, stop=True),
                waits=[tok_raw, tok_consts] +
                      ([p.bank_rel[bk]] if p.bank_rel[bk] else []), tok=True)
            p.op("vector", lambda e, sl=sl: e.tensor_mul(
                tp1, raw[:, sl], cos_sb[:, sl]),
                waits=[tok_raw, tok_consts] + rel_tp["tp1"])
            tmul2 = p.op("vector", lambda e, sl=sl, bk=bk: e.tensor_mul(
                tp2, pb[bk][:, :], sins_sb[:, sl]),
                waits=[trh] + rel_tp["tp2"], tok=True)
            p.bank_rel[bk] = tmul2
            p.op("vector", lambda e: e.tensor_add(
                tp3, tp1, tp2), waits=rel_tp["tp3"])
            raw_rel = tmin = p.op("vector", lambda e: e.tensor_scalar_min(
                tp1, tp3, 0.0), tok=True)
            texp = p.op("scalar", lambda e: e.activation(
                tp2, tp1, AF.Exp), waits=[tmin], tok=True)
            dst_tok = p.op("vector", lambda e, sl=sl: e.scalar_tensor_tensor(
                dst[:, sl], tp3, 0.0, tp2, OP.max, OP.add),
                waits=[texp] + ([rel_dst] if rel_dst else []), tok=True)
            rel_tp["tp1"] = [texp]
            rel_tp["tp2"] = [dst_tok]
            rel_tp["tp3"] = [dst_tok]
        return dst_tok, raw_rel

    vraw, kraw = pool[0], pool[1]
    kout = [pool[2], pool[3]]
    rel_wq = [[], []]
    rel_vraw = []
    rel_kraw = []
    rel_kout = [[], []]
    rel_vtile1 = [None] * 8
    rel_vtile = [rel_vtile1, rel_vtile1]
    rel_ktile1 = [None] * 8
    rel_ktile = [rel_ktile1, rel_ktile1]
    rel_gst = [None] * 4
    rel_gseg = None
    last_vt_store = None
    last_kf_store = None
    ccin_last = None
    vt_rel_extra = [None, None]
    rel_wcv = [[], []]
    pend = None
    for h in range(H + 1):
        pq = h % 2
        if h < H:
            tok_acc = None
            for f in range(NF):
                fp = f % 2
                tw = p.dma("sync", f"wq{fp}", lambda e, h=h, f=f, fp=fp: e.dma_start(
                    out=wqb[fp],
                    in_=WQKVP[f * 128:(f + 1) * 128, h * 256:(h + 1) * 256]),
                    waits=rel_wq[fp])
                tcv = p.op("vector", lambda e, fp=fp: e.tensor_copy(
                    wqh[fp], wqb[fp]), waits=[tw] + rel_wcv[fp])
                tcv = p.op("vector", lambda e, fp=fp: e.tensor_sub(
                    wql[fp], wqb[fp], wqh[fp].bitcast(F32)), tok=True)
                rel_wq[fp] = [tcv]
                mm_w = [tcv, xn_tok[f]]
                if f == 0:
                    mm_w += [t for t in p.bank_rel[0:4] if t]
                    mm_w += rel_vraw + rel_kraw
                last = None
                for col in range(2):
                    cs = slice(col * 128, (col + 1) * 128)
                    b0 = 2 * col
                    for st in range(NST):
                        sl = slice(f * TPC + st * STW, f * TPC + (st + 1) * STW)
                        p.op("tensor", lambda e, f=f, fp=fp, cs=cs, st=st, sl=sl,
                             b0=b0:
                             e.matmul(pb[b0 + st][:, :], wqh[fp][:, cs],
                                      xnh_sb[:, sl],
                                      start=(f == 0), stop=False), waits=mm_w)
                        p.op("tensor", lambda e, fp=fp, cs=cs, st=st, sl=sl, b0=b0:
                             e.matmul(pb[b0 + st][:, :], wqh[fp][:, cs],
                                      xnl_sb[:, sl],
                                      start=False, stop=False))
                    for st in range(NST):
                        sl = slice(f * TPC + st * STW, f * TPC + (st + 1) * STW)
                        last = p.op("tensor",
                                    lambda e, f=f, fp=fp, cs=cs, st=st, sl=sl,
                                    b0=b0:
                             e.matmul(pb[b0 + st][:, :], wql[fp][:, cs],
                                      xnh_sb[:, sl],
                                      start=False, stop=(f == NF - 1)),
                             tok=(col == 1 and st == NST - 1))
                rel_wcv[fp] = [last]
                tok_acc = last
            ev_v = []
            ev_k = []
            for st in range(NST):
                sl = slice(st * STW, (st + 1) * STW)
                t = p.op("scalar", lambda e, st=st, sl=sl: e.copy(
                    vraw[:, sl], pb[st][:, :]), waits=[tok_acc] + rel_vraw, tok=True)
                ev_v.append(t)
                p.bank_rel[st] = t
            for st in range(NST):
                sl = slice(st * STW, (st + 1) * STW)
                t = p.op("scalar", lambda e, st=st, sl=sl: e.copy(
                    kraw[:, sl], pb[2 + st][:, :]), waits=rel_kraw, tok=True)
                ev_k.append(t)
                p.bank_rel[2 + st] = t
        if pend is not None:
            p1 = pend["p"]
            hh = pend["h"]
            kt_toks = []
            last_ktr = None
            for i in range(8):
                bk = 4 + (i % 2)
                w = [pend["tok_k"]] + ([p.bank_rel[bk]] if p.bank_rel[bk] else [])
                ttr = p.op("tensor", lambda e, i=i, bk=bk, p1=p1: e.transpose(
                    pb[bk][:, 0:128], kout[p1][:, i * 128:(i + 1) * 128], ident_sb),
                    waits=w, tok=True)
                last_ktr = ttr
                tev = p.op("scalar", lambda e, i=i, bk=bk, p1=p1: e.copy(
                    ktile[p1][i], pb[bk][:, 0:128]),
                    waits=[ttr] + ([rel_ktile[p1][i]] if rel_ktile[p1][i] else []),
                    tok=True)
                p.bank_rel[bk] = tev
                kt_toks.append(tev)
            rel_kout[p1] = [last_ktr, pend["kf_store"]]
            gacc_tok = None
            gs_last = None
            gs_evs = []
            vt_toks_p = pend["vt_toks"]
            for c in range(CPC):
                bk = 6 + (c % 2)
                w = [kt_toks[2 * c], kt_toks[2 * c + 1], vt_toks_p[2 * c],
                     vt_toks_p[2 * c + 1]] + ([p.bank_rel[bk]] if p.bank_rel[bk] else [])
                p.op("tensor", lambda e, c=c, bk=bk, p1=p1: e.matmul(
                    pb[bk][:, 0:129], ktile[p1][2 * c], vtile[p1][2 * c][:, 0:129],
                    start=True, stop=False), waits=w)
                tg = p.op("tensor", lambda e, c=c, bk=bk, p1=p1: e.matmul(
                    pb[bk][:, 0:129], ktile[p1][2 * c + 1],
                    vtile[p1][2 * c + 1][:, 0:129],
                    start=False, stop=True), tok=True)
                rel_ktile[p1][2 * c] = tg
                rel_ktile[p1][2 * c + 1] = tg
                rel_vtile[p1][2 * c] = tg
                rel_vtile[p1][2 * c + 1] = tg
                vt_rel_extra[p1] = pend["vt_store"]
                tev = p.op("scalar", lambda e, c=c, bk=bk: e.copy(
                    gst[c][:, 0:129], pb[bk][:, 0:129]),
                    waits=[tg, tok_gz] + ([rel_gst[c]] if rel_gst[c] else []), tok=True)
                p.bank_rel[bk] = tev
                gs_evs.append(tev)
                if c == 0:
                    gacc_tok = p.op("vector", lambda e: e.tensor_copy(
                        gseg_sb[:, 0:129], gst[0][:, 0:129]),
                        waits=[tev] + ([rel_gseg] if rel_gseg else []), tok=True)
                else:
                    gacc_tok = p.op("vector", lambda e, c=c: e.tensor_add(
                        gseg_sb[:, 0:129], gseg_sb[:, 0:129], gst[c][:, 0:129]),
                        waits=[tev, gacc_tok], tok=True)
            gs_last = p.dma("gpsimd", "sgs", lambda e, hh=hh: e.dma_start(
                out=GS[hh].rearrange("c m w -> m c w"),
                in_=gst_all.rearrange("m (c w) -> m c w", w=GW)), waits=gs_evs)
            for c in range(CPC):
                rel_gst[c] = gs_last
            ccin_last = p.dma("gpsimd", "scc", lambda e, hh=hh: e.dma_start(
                out=CCIN[:, hh * GW:(hh + 1) * GW], in_=gseg_sb), waits=[gacc_tok])
            rel_gseg = ccin_last
            pend = None
        if h < H:
            vt_toks = []
            last_tr = None
            for i in range(8):
                bk = 4 + (i % 2)
                w = [ev_v[i // 4], tok_consts] + \
                    ([p.bank_rel[bk]] if p.bank_rel[bk] else [])
                ttr = p.op("tensor", lambda e, i=i, bk=bk, pq=pq: e.transpose(
                    pb[bk][:, 0:128], vraw[:, i * 128:(i + 1) * 128], ident_sb),
                    waits=w, tok=True)
                last_tr = ttr
                tev = p.op("scalar", lambda e, i=i, bk=bk, pq=pq: e.copy(
                    vtile[pq][i][:, 0:128], pb[bk][:, 0:128]),
                    waits=[ttr, tok_vones, vt_rel_extra[pq]] +
                          ([rel_vtile[pq][i]] if rel_vtile[pq][i] else []),
                    tok=True)
                p.bank_rel[bk] = tev
                vt_toks.append(tev)
            last_vt_store = p.dma("gpsimd", "svt",
                                  lambda e, h=h, pq=pq: e.dma_start(
                out=VT[h].rearrange("c t m d -> m c t d"),
                in_=vtile_all[pq].rearrange("m (i w) -> m i w", w=132)[:, :, 0:128]),
                waits=vt_toks)
            for i in range(8):
                rel_vtile[pq][i] = last_vt_store
            rel_vraw = [last_tr]
            tok_k, kraw_rel = rope_elu(kraw, kout[pq],
                                       (rel_kout[pq][0] if rel_kout[pq] else None),
                                       ev_k[1], (6, 7))
            rel_kraw = [kraw_rel]
            last_kf_store = p.dma("gpsimd", "skf", lambda e, h=h, pq=pq: e.dma_start(
                out=KF[h * 128:(h + 1) * 128, :], in_=kout[pq]), waits=[tok_k])
            pend = {"h": h, "p": pq, "tok_k": tok_k, "vt_toks": vt_toks,
                    "vt_store": last_vt_store, "kf_store": last_kf_store}

    if _SKIP_CC:
        for s in range(4):
            tok_cc = p.dma("gpsimd", "scc", lambda e, s=s: e.dma_start(
                out=CCOUT[s * 128:(s + 1) * 128, :], in_=CCIN),
                waits=[ccin_last])
    else:
        tok_cc = p.cc_op(lambda e: e.collective_compute(
            "AllGather", OP.bypass,
            replica_groups=[[0, 1, 2, 3], [4, 5, 6, 7]],
            ins=[CCIN.opt()], outs=[CCOUT.opt()]), waits=[ccin_last])

    qraw = [pool[0], pool[1]]
    qout = [pool[2], pool[3]]
    rel_qraw = [rel_vraw + rel_kraw, list(rel_kraw)]
    rel_wf = [[], []]
    tred = None
    rel_qout = [list(rel_kout[0]), list(rel_kout[1])]
    last_qf_store = [None, None]
    pend3 = None
    for b in range(H // 2 + 1):
        if pend3 is not None:
            bb = pend3["b"]
            for j in range(2):
                hq = 2 * bb + j
                tok_q, raw_rel = rope_elu(qraw[j], qout[j],
                                          (rel_qout[j][0] if rel_qout[j] else None),
                                          pend3["ev"][j][1],
                                          (4, 5) if j == 0 else (6, 7))
                rel_qraw[j] = [raw_rel] + pend3["extra"][j]
                last_qf_store[j] = p.dma("gpsimd", f"sqf{j}",
                                         lambda e, hq=hq, j=j: e.dma_start(
                    out=QF[hq * 128:(hq + 1) * 128, :], in_=qout[j]), waits=[tok_q])
                rel_qout[j] = [last_qf_store[j]]
            pend3 = None
        if b < H // 2:
            tok_acc = None
            for f in range(NF):
                fp = f % 2
                tw = p.dma("sync", f"wq{fp}", lambda e, b=b, f=f, fp=fp: e.dma_start(
                    out=wqb[fp],
                    in_=WQKVP[f * 128:(f + 1) * 128,
                              2 * E + b * 256:2 * E + (b + 1) * 256]),
                    waits=rel_wq[fp])
                tcv = p.op("vector", lambda e, fp=fp: e.tensor_copy(
                    wqh[fp], wqb[fp]), waits=[tw] + rel_wcv[fp])
                tcv = p.op("vector", lambda e, fp=fp: e.tensor_sub(
                    wql[fp], wqb[fp], wqh[fp].bitcast(F32)), tok=True)
                rel_wq[fp] = [tcv]
                mm_w = [tcv, xn_tok[f]]
                if f == 0:
                    mm_w += [t for t in p.bank_rel[0:4] if t]
                last = None
                for col in range(2):
                    cs = slice(col * 128, (col + 1) * 128)
                    b0 = 2 * col
                    for st in range(NST):
                        sl = slice(f * TPC + st * STW, f * TPC + (st + 1) * STW)
                        p.op("tensor", lambda e, f=f, fp=fp, cs=cs, st=st, sl=sl,
                             b0=b0:
                             e.matmul(pb[b0 + st][:, :], wqh[fp][:, cs],
                                      xnh_sb[:, sl],
                                      start=(f == 0), stop=False), waits=mm_w)
                        p.op("tensor", lambda e, fp=fp, cs=cs, st=st, sl=sl, b0=b0:
                             e.matmul(pb[b0 + st][:, :], wqh[fp][:, cs],
                                      xnl_sb[:, sl],
                                      start=False, stop=False))
                    for st in range(NST):
                        sl = slice(f * TPC + st * STW, f * TPC + (st + 1) * STW)
                        last = p.op("tensor",
                                    lambda e, f=f, fp=fp, cs=cs, st=st, sl=sl,
                                    b0=b0:
                             e.matmul(pb[b0 + st][:, :], wql[fp][:, cs],
                                      xnh_sb[:, sl],
                                      start=False, stop=(f == NF - 1)),
                             tok=(col == 1 and st == NST - 1))
                rel_wcv[fp] = [last]
                tok_acc = last
            for k in range(4):
                i = 4 * b + k
                twfb = p.dma("sync", f"lwf{i % 2}", lambda e, i=i: e.dma_start(
                    out=wfb[i % 2],
                    in_=WFFNS[i // 2, :, (i % 2) * TPC:(i % 2 + 1) * TPC]),
                    waits=rel_wf[i % 2])
                tred = p.op("vector", lambda e, i=i: e.tensor_reduce(
                    acc32[:, i:i + 1], wfb[i % 2], AX.X, OP.add,
                    apply_absolute_value=True), waits=[twfb], tok=True)
                rel_wf[i % 2] = [tred]
        if b < H // 2:
            evs = []
            extra = [[], []]
            for j in range(2):
                hq = 2 * b + j
                ev = []
                for st in range(NST):
                    sl = slice(st * STW, (st + 1) * STW)
                    t = p.op("scalar", lambda e, j=j, st=st, sl=sl: e.copy(
                        qraw[j][:, sl], pb[2 * j + st][:, :]),
                        waits=[tok_acc] + rel_qraw[j], tok=True)
                    ev.append(t)
                    p.bank_rel[2 * j + st] = t
                if _DEBUG_OUTPUTS:
                    tqr = p.dma("gpsimd", "sqr", lambda e, hq=hq, j=j: e.dma_start(
                        out=QRAW[hq * 128:(hq + 1) * 128, :], in_=qraw[j]),
                        waits=[ev[0], ev[1]])
                    extra[j] = [tqr]
                evs.append(ev)
            pend3 = {"b": b, "ev": evs, "extra": extra}

    p.op("vector", lambda e: e.memset(gst_all[:, 0:512], 0.0), waits=[tred])
    p.op("vector", lambda e: e.memset(gst_all[:, 0:512], 1.0))
    p.op("vector", lambda e: e.tensor_reduce(
        acc32[:, 32:33], acc32[:, 0:32], AX.X, OP.add))
    p.op("vector", lambda e: e.memset(gst_all[:, 0:512], 0.0))
    tred = p.op("vector", lambda e: e.memset(gst_all[:, 0:512], 1.0), tok=True)
    if _DEBUG_OUTPUTS:
        p.dma("gpsimd", "sdbg", lambda e: e.dma_start(out=ACC, in_=acc32),
              waits=[tred])
    tsc = p.op("tensor", lambda e: e.matmul(
        pb[6][:, 0:1], ones_sb, acc32[:, 32:33], start=True, stop=True),
        waits=[tred] + ([p.bank_rel[6]] if p.bank_rel[6] else []), tok=True)
    tok_scale = p.op("vector", lambda e: e.tensor_scalar_mul(
        scale_sb, pb[6][:, 0:1], 1.0 / (E * E)), waits=[tsc], tok=True)
    p.bank_rel[6] = tok_scale
    p.op("vector", lambda e: e.memset(gst_all[:, 0:512], 0.0))
    p.op("vector", lambda e: e.memset(gst_all[:, 0:512], 1.0))
    p.op("vector", lambda e: e.tensor_scalar_add(rscaleb_sb, scale_sb, 1e-10))
    p.op("vector", lambda e: e.memset(gst_all[:, 0:512], 0.0))
    p.op("vector", lambda e: e.memset(gst_all[:, 0:512], 1.0))
    p.op("vector", lambda e: e.tensor_scalar_mul(thr_sb, rscaleb_sb, 0.5))
    p.op("vector", lambda e: e.tensor_scalar_mul(thrn_sb, rscaleb_sb, -0.5))
    p.op("vector", lambda e: e.memset(gst_all[:, 0:512], 0.0))
    tok_thr = p.op("vector", lambda e: e.memset(gst_all[:, 0:512], 1.0), tok=True)

    if _DEBUG_OUTPUTS:
        p.dma("gpsimd", "sdbg", lambda e: e.dma_start(
            out=SCL[:, 0:1], in_=scale_sb, allow_slow_non_contiguous=True),
            waits=[tok_thr])
        p.dma("gpsimd", "sdbg", lambda e: e.dma_start(
            out=SCL[:, 1:2], in_=thr_sb, allow_slow_non_contiguous=True),
            waits=[tok_thr])
    if _DEBUG_OUTPUTS:
        p.dma("gpsimd", "sdbg", lambda e: e.dma_start(
            out=WTS_D, in_=xnh_sb[:, 0:2048]), waits=[tok_thr])
        p.dma("gpsimd", "sdbg", lambda e: e.dma_start(
            out=XNH_D, in_=xnh_sb.bitcast(F32)), waits=[tok_thr])
        p.dma("gpsimd", "sdbg", lambda e: e.dma_start(
            out=XNL_D, in_=xnl_sb.bitcast(F32)), waits=[tok_thr])
    m2h = pool[0]
    ffb = [pool[1][:, 0:512], pool[1][:, 512:1024]]
    rel_bfw = [rel_wf[0], rel_wf[1]]
    rel_m2 = list(rel_qraw[0])
    rel_wtsB = []
    rel_ffb = [list(rel_qraw[1]), list(rel_qraw[1])]
    rel_bfb = [[], []]
    last_ff_store = [None, None]
    rel_bfbank = [None, None, None, None]
    for oc in range(NF):
        par = oc % 2
        tbi = p.dma("scalar", f"lbf{par}", lambda e, oc=oc, par=par: e.dma_start(
            out=bfb[par], in_=BIAS[oc * 128:(oc + 1) * 128, :]), waits=rel_bfb[par])
        ttn = None
        for half in range(2):
            twf = p.dma("sync", f"lws{half}", lambda e, oc=oc, half=half:
                        e.dma_start(
                out=wfb[half], in_=WFFNS[oc, :, half * TPC:(half + 1) * TPC]),
                waits=rel_bfw[half])
            p.op("vector", lambda e, half=half: e.tensor_scalar(
                m2h, wfb[half], thrn_sb[:, 0:1], None, OP.is_lt),
                waits=[twf, tok_thr] + rel_m2 + rel_wtsB)
            rel_m2 = []
            ttn = p.op("vector", lambda e, half=half: e.scalar_tensor_tensor(
                wtsB[:, half * TPC:(half + 1) * TPC], wfb[half], thr_sb[:, 0:1],
                m2h, OP.is_gt, OP.subtract), tok=True)
            rel_bfw[half] = [ttn]
        tdmp = None
        if _DEBUG_OUTPUTS:
            tdmp = p.dma("gpsimd", "sdbg", lambda e, oc=oc: e.dma_start(
                out=WTD_D[oc], in_=wtsB.bitcast(F32)), waits=[ttn])
        for st in range(NST):
            bk = 2 * par + st
            wmm = [ttn] + ([rel_bfbank[bk]] if rel_bfbank[bk] else [])
            if oc == 0 and st == 0:
                wmm += [t for t in p.bank_rel[0:4] if t]
            tmf = None
            for f in range(NF):
                sl = slice(f * TPC + st * STW, f * TPC + (st + 1) * STW)
                p.op("tensor", lambda e, f=f, sl=sl, bk=bk: e.matmul(
                    pb[bk][:, :], wtsB[:, f * 128:(f + 1) * 128], xnh_sb[:, sl],
                    start=(f == 0), stop=False), waits=wmm)
                tmf = p.op("tensor", lambda e, f=f, sl=sl, bk=bk: e.matmul(
                    pb[bk][:, :], wtsB[:, f * 128:(f + 1) * 128], xnl_sb[:, sl],
                    start=False, stop=(f == NF - 1)), tok=(f == NF - 1))
            te1 = p.op("vector", lambda e, par=par, bk=bk: e.tensor_scalar(
                ffb[par], pb[bk][:, :], scale_sb[:, 0:1], bfb[par][:, 0:1],
                OP.mult, OP.add),
                waits=[tmf, tok_scale, tbi] + rel_ffb[par], tok=True)
            rel_bfbank[bk] = te1
            last_ff_store[par] = p.dma("gpsimd", f"sff{par}",
                                       lambda e, oc=oc, st=st, par=par: e.dma_start(
                out=FF[oc * 128:(oc + 1) * 128, st * STW:(st + 1) * STW],
                in_=ffb[par]), waits=[te1])
            rel_ffb[par] = [last_ff_store[par]]
            rel_wtsB = [tmf]
        if tdmp is not None:
            rel_wtsB = rel_wtsB + [tdmp]
        rel_bfb[par] = [rel_bfbank[2 * par + 1]]

    p.barrier()
    stA.close()

    stB = contextlib.ExitStack()
    sbB = lambda name, shape, dt=F32: stB.enter_context(
        nc.sbuf_tensor(name, shape, dt)).ap()
    mask_sb = sbB("mask_sb", [128, 512])
    den16 = sbB("den16", [16, H * TPC // 16])
    den16b = sbB("den16b", [16, H * TPC // 16])

    tok_mask = p.dma("sync", "cst", lambda e: e.dma_start(out=mask_sb, in_=MASKUT))

    sbase_sb = sbB("sbase_sb", [128, H * GW])
    ccseg = [sbB("ccseg0", [128, H * GW])] * 2
    qh_sb = [[sbB(f"qh{q}_{i}", [128, TPC]) for i in range(2)] for q in range(2)]
    kh_sb = [[sbB(f"kh{q}_{i}", [128, TPC]) for i in range(2)] for q in range(2)]
    vtb = [[sbB(f"vtb{q}_{i}", [128, 256]) for i in range(2)] for q in range(2)]
    gsb = [[sbB(f"gsb{q}_{i}", [128, GW]) for i in range(2)] for q in range(2)]
    scur = [[sbB(f"scur{q}_{i}", [128, 132]) for i in range(CPC)] for q in range(2)]
    attsb = [[sbB(f"attsb{q}_{i}", [128, 512]) for i in range(2)] for q in range(2)]
    aoc = [[sbB(f"aoc{q}_{i}", [128, 256]) for i in range(2)] for q in range(2)]
    densb = [[sbB(f"densb{q}_{i}", [1, 256]) for i in range(2)] for q in range(2)]

    tok_sbase = None
    rel_ccseg = [[], []]
    for s in range(4):
        tcs = p.dma("scalar", "lcs0", lambda e, s=s: e.dma_start(
            out=ccseg[0], in_=CCOUT[s * 128:(s + 1) * 128, :]),
            waits=[tok_cc] + rel_ccseg[0] + rel_ccseg[1])
        if s == 0:
            tok_sbase = p.op("vector", lambda e: e.tensor_scalar(
                sbase_sb, ccseg[0], segm_sb[:, 0:1], None, OP.mult),
                waits=[tcs, tok_consts], tok=True)
        else:
            tok_sbase = p.op("vector", lambda e, s=s: e.scalar_tensor_tensor(
                sbase_sb, ccseg[s % 2], segm_sb[:, s:s + 1], sbase_sb,
                OP.mult, OP.add), waits=[tcs], tok=True)
        rel_ccseg[s % 2] = [tok_sbase]

    rel_qh = [[[last_qf_store[0], last_qf_store[1]] for _ in range(2)]
              for _ in range(2)]
    rel_kh = [[[last_kf_store] for _ in range(2)] for _ in range(2)]
    rel_vtb = [[[last_vt_store], [last_vt_store]], [[last_vt_store], [last_vt_store]]]
    rel_gsb = [[[], []], [[], []]]
    rel_attsb = [[[], []], [[], []]]
    rel_scur = [[[], [], [], []], [[], [], [], []]]
    rel_aoc = [[[], []], [[], []]]
    rel_densb = [[[], []], [[], []]]
    last_den_store = {}
    last_ao_store = {}
    tq2 = [[None, None], [None, None]]
    tk2 = [[None, None], [None, None]]
    for hp in range(2):
        tq2[0][hp] = p.dma("sync", f"lqh{hp}_0", lambda e, hp=hp: e.dma_start(
            out=qh_sb[0][hp], in_=QF[hp * 128:(hp + 1) * 128, :]),
            waits=rel_qh[0][hp])
        tk2[0][hp] = p.dma("sync", f"lkh{hp}_0", lambda e, hp=hp: e.dma_start(
            out=kh_sb[0][hp], in_=KF[hp * 128:(hp + 1) * 128, :]),
            waits=rel_kh[0][hp])
    for g in range(H // 2):
        scur_toks = [None, None]
        gp = g % 2
        if g + 1 < H // 2:
            gn = (g + 1) % 2
            for hp in range(2):
                h = 2 * (g + 1) + hp
                tq2[gn][hp] = p.dma("sync", f"lqh{hp}_{gn}",
                                    lambda e, h=h, hp=hp, gn=gn: e.dma_start(
                    out=qh_sb[gn][hp], in_=QF[h * 128:(h + 1) * 128, :]),
                    waits=rel_qh[gn][hp])
                tk2[gn][hp] = p.dma("sync", f"lkh{hp}_{gn}",
                                    lambda e, h=h, hp=hp, gn=gn: e.dma_start(
                    out=kh_sb[gn][hp], in_=KF[h * 128:(h + 1) * 128, :]),
                    waits=rel_kh[gn][hp])
        tq = tq2[gp]
        tk = tk2[gp]
        for hp in range(2):
            h = 2 * g + hp
            scur_tok = [None] * CPC
            scur_tok[0] = p.op("vector", lambda e, h=h, hp=hp: e.tensor_copy(
                scur[hp][0][:, 0:129], sbase_sb[:, h * GW:h * GW + 129]),
                waits=[tok_sbase] + rel_scur[hp][0], tok=True)
            for c in range(1, CPC):
                tg = p.dma("scalar", f"lgb{hp}_{c % 2}",
                           lambda e, h=h, hp=hp, c=c: e.dma_start(
                    out=gsb[hp][c % 2], in_=GS[h, c - 1, :, :]),
                    waits=rel_gsb[hp][c % 2] + [rel_gst[c - 1]])
                scur_tok[c] = p.op("vector", lambda e, hp=hp, c=c: e.tensor_add(
                    scur[hp][c][:, 0:129], scur[hp][c - 1][:, 0:129],
                    gsb[hp][c % 2][:, 0:129]),
                    waits=[tg, scur_tok[c - 1]] + rel_scur[hp][c], tok=True)
                rel_gsb[hp][c % 2] = [scur_tok[c]]
            scur_toks[hp] = scur_tok
        tden = {}
        att_last = {}
        for c in range(CPC):
            att_toks_hp = {}
            tvtb_hp = {}
            for hp in range(2):
                h = 2 * g + hp
                csl = slice(c * CHK, (c + 1) * CHK)
                ab = 4 * hp
                tvtb_hp[hp] = p.dma("scalar", f"lvb{hp}_{c % 2}",
                             lambda e, h=h, hp=hp, c=c: e.dma_start(
                    out=vtb[hp][c % 2].rearrange("m (t d) -> m t d", t=2),
                    in_=VT[h, c].rearrange("t m d -> m t d")),
                    waits=rel_vtb[hp][c % 2])
                att_toks = []
                for t in range(2):
                    bk = ab + t
                    wmm = [tq[hp], tk[hp]] + ([p.bank_rel[bk]] if p.bank_rel[bk] else [])
                    ta = p.op("tensor", lambda e, hp=hp, c=c, t=t, csl=csl, bk=bk,
                              gp=gp: e.matmul(
                        pb[bk][:, 0:256],
                        kh_sb[gp][hp][:, c * CHK + t * 128:c * CHK + (t + 1) * 128],
                        qh_sb[gp][hp][:, csl], start=True, stop=True), waits=wmm,
                        tok=True)
                    tm = p.op("vector", lambda e, hp=hp, c=c, t=t, bk=bk: e.tensor_mul(
                        attsb[hp][c % 2][:, t * 256:(t + 1) * 256], pb[bk][:, 0:256],
                        mask_sb[:, t * 256:(t + 1) * 256]),
                        waits=[ta, tok_mask] + rel_attsb[hp][c % 2], tok=True)
                    p.bank_rel[bk] = tm
                    att_toks.append(tm)
                att_last[hp] = att_toks[1]
                att_toks_hp[hp] = att_toks
            for hp in range(2):
                h = 2 * g + hp
                csl = slice(c * CHK, (c + 1) * CHK)
                nb = 2 + 4 * hp
                db = 3 + 4 * hp
                att_toks = att_toks_hp[hp]
                tvtb = tvtb_hp[hp]
                wmm = [att_toks[0], att_toks[1], tvtb, scur_toks[hp][c]]
                wmm += [p.bank_rel[nb]] if p.bank_rel[nb] else []
                p.op("tensor", lambda e, hp=hp, c=c, nb=nb: e.matmul(
                    pb[nb][:, 0:256], vtb[hp][c % 2][:, 0:128],
                    attsb[hp][c % 2][:, 0:256], start=True, stop=False), waits=wmm)
                p.op("tensor", lambda e, hp=hp, c=c, nb=nb: e.matmul(
                    pb[nb][:, 0:256], vtb[hp][c % 2][:, 128:256],
                    attsb[hp][c % 2][:, 256:512], start=False, stop=False))
                tnum = p.op("tensor", lambda e, hp=hp, c=c, nb=nb, csl=csl, gp=gp:
                            e.matmul(
                    pb[nb][:, 0:256], scur[hp][c][:, 0:128], qh_sb[gp][hp][:, csl],
                    start=False, stop=True), tok=True)
                wmm = [p.bank_rel[db]] if p.bank_rel[db] else []
                p.op("tensor", lambda e, hp=hp, c=c, db=db: e.matmul(
                    pb[db][0:1, 0:256], ones_sb[:, 0:1],
                    attsb[hp][c % 2][:, 0:256], start=True, stop=False), waits=wmm)
                p.op("tensor", lambda e, hp=hp, c=c, db=db: e.matmul(
                    pb[db][0:1, 0:256], ones_sb[:, 0:1],
                    attsb[hp][c % 2][:, 256:512], start=False, stop=False))
                tden[hp] = p.op("tensor", lambda e, hp=hp, c=c, db=db, csl=csl,
                                gp=gp: e.matmul(
                    pb[db][0:1, 0:256], scur[hp][c][:, 128:129],
                    qh_sb[gp][hp][:, csl], start=False, stop=True), tok=True)
                rel_scur[hp][c] = [tden[hp]]
                rel_attsb[hp][c % 2] = [tden[hp]]
                rel_vtb[hp][c % 2] = [tnum]
                tev = p.op("scalar", lambda e, hp=hp, c=c, nb=nb: e.copy(
                    aoc[hp][c % 2], pb[nb][:, 0:256]),
                    waits=[tnum] + rel_aoc[hp][c % 2], tok=True)
                p.bank_rel[nb] = tev
                last_ao_store[hp, c % 2] = p.dma(
                    "gpsimd", f"sao{hp}_{c % 2}",
                    lambda e, h=h, hp=hp, c=c: e.dma_start(
                        out=AO[h * 128:(h + 1) * 128, c * CHK:(c + 1) * CHK],
                        in_=aoc[hp][c % 2]), waits=[tev])
                rel_aoc[hp][c % 2] = [last_ao_store[hp, c % 2]]
                tevd = p.op("scalar", lambda e, hp=hp, c=c, db=db: e.copy(
                    densb[hp][c % 2], pb[db][0:1, 0:256]),
                    waits=[tden[hp]] + rel_densb[hp][c % 2], tok=True)
                p.bank_rel[db] = tevd
                last_den_store[hp, c % 2] = p.dma(
                    "gpsimd", f"sdn{hp}_{c % 2}",
                    lambda e, h=h, hp=hp, c=c: e.dma_start(
                        out=DEN[h:h + 1, c * CHK:(c + 1) * CHK],
                        in_=densb[hp][c % 2]), waits=[tevd])
                rel_densb[hp][c % 2] = [last_den_store[hp, c % 2]]
        for hp in range(2):
            rel_qh[gp][hp] = [tden[hp]]
            rel_kh[gp][hp] = [att_last[hp]]

    tdr = p.dma("scalar", "ldn", lambda e: e.dma_start(out=den16, in_=DEN),
                waits=[last_den_store[k] for k in last_den_store])
    p.op("vector", lambda e: e.tensor_scalar_max(den16b, den16, 1e-6), waits=[tdr])
    tok_recip = p.op("vector", lambda e: e.reciprocal(den16r, den16b), tok=True)

    p.barrier()
    stB.close()

    stC = contextlib.ExitStack()
    sc = lambda name, shape, dt=F32: stC.enter_context(
        nc.sbuf_tensor(name, shape, dt)).ap()
    aodh = sc("aodh", [128, NF * STW], F32R)
    aodl = sc("aodl", [128, NF * STW], F32R)
    aotmp = [sc(f"aotmp{i}", [128, STW]) for i in range(2)]
    sett = sc("sett", [128, 256])
    wcp = [sc(f"wcp{i}", [128, E]) for i in range(2)]
    woh = [sc(f"woh{i}", [128, E], F32R) for i in range(2)]
    wol = [sc(f"wol{i}", [128, E], F32R) for i in range(2)]
    ffld = [sc(f"ffld{i}", [128, STW]) for i in range(2)]
    selmat_sb = sc("selmat_sb", [16, E])
    e1b = [sc(f"e1b{i}", [128, STW]) for i in range(2)]
    outoc = [sc(f"outoc{i}", [128, STW]) for i in range(2)]
    xrc = [sc(f"xrc{i}", [128, STW]) for i in range(2)]

    tok_sel = p.dma("scalar", "lsel", lambda e: e.dma_start(
        out=selmat_sb, in_=SELMAT))
    rel_pass_ao = [last_ao_store[k] for k in last_ao_store]
    rel_woh = [[], []]
    rel_wcp = [[], []]
    rel_ffld = [[last_ff_store[0], last_ff_store[1]], []]
    rel_aotmp = [[], []]
    rel_e1 = [[], []]
    rel_outoc = [[], []]
    rel_xrc = [[], []]
    last_out = [None, None]
    for st in range(NST):
        div_toks = [None] * NF
        for f in range(NF):
            sl = slice(f * STW, (f + 1) * STW)
            fp = f % 2
            tld = p.dma("scalar", f"lao{fp}", lambda e, f=f, st=st, fp=fp: e.dma_start(
                out=aotmp[fp],
                in_=AO[f * 128:(f + 1) * 128, st * STW:(st + 1) * STW]),
                waits=(rel_pass_ao if f < 2 else []) + rel_aotmp[fp])
            bk = 6 + fp
            wmm = [tok_recip, tok_sel] + ([p.bank_rel[bk]] if p.bank_rel[bk] else [])
            tprr = p.op("tensor", lambda e, f=f, st=st, bk=bk: e.matmul(
                pb[bk][:, :], selmat_sb[:, f * 128:(f + 1) * 128],
                den16r[:, st * STW:(st + 1) * STW], start=True, stop=True),
                waits=wmm, tok=True)
            p.op("vector", lambda e, fp=fp, bk=bk: e.tensor_mul(
                aotmp[fp], aotmp[fp], pb[bk][:, :]), waits=[tprr, tld])
            tdv = p.op("vector", lambda e, fp=fp, sl=sl: e.tensor_copy(
                aodh[:, sl], aotmp[fp]), tok=True)
            p.bank_rel[bk] = tdv
            div_toks[f] = p.op("vector", lambda e, fp=fp, sl=sl: e.tensor_sub(
                aodl[:, sl], aotmp[fp], aodh[:, sl].bitcast(F32)), tok=True)
            rel_aotmp[fp] = [div_toks[f]]
        p.op("vector", lambda e: e.memset(sett, 0.0), waits=[div_toks[NF - 1]])
        div_all = p.op("vector", lambda e: e.memset(sett, 1.0), tok=True)
        rel_pass_ao = []
        if _DEBUG_OUTPUTS and st == 0:
            p.dma("gpsimd", "sdbg", lambda e: e.dma_start(out=ADH, in_=aodh),
                  waits=[div_toks[NF - 1]])
            p.dma("gpsimd", "sdbg", lambda e: e.dma_start(out=ADL, in_=aodl),
                  waits=[div_toks[NF - 1]])
        for oc in range(NF):
            par = oc % 2
            bw, bf_ = 2 * par, 2 * par + 1
            tld = p.dma("sync", f"lwoh{par}", lambda e, oc=oc, par=par: e.dma_start(
                out=wcp[par], in_=WOUTS[oc]), waits=rel_wcp[par])
            p.op("vector", lambda e, par=par: e.tensor_copy(woh[par], wcp[par]),
                 waits=[tld] + rel_woh[par])
            twoh = p.op("vector", lambda e, par=par: e.tensor_sub(
                wol[par], wcp[par], woh[par].bitcast(F32)), tok=True)
            rel_wcp[par] = [twoh]
            if _DEBUG_OUTPUTS and st == 0 and oc == 0:
                p.dma("gpsimd", "sdbg", lambda e: e.dma_start(out=WOHB, in_=woh[0]),
                      waits=[twoh])
            tff = p.dma("scalar", f"lff{par}", lambda e, oc=oc, st=st, par=par:
                        e.dma_start(
                out=ffld[par], in_=FF[oc * 128:(oc + 1) * 128,
                                      st * STW:(st + 1) * STW]),
                waits=rel_ffld[par])
            txr = p.dma("scalar", f"lxr{par}", lambda e, oc=oc, st=st, par=par:
                        e.dma_start(
                out=xrc[par], in_=XT[oc * 128:(oc + 1) * 128,
                                     st * STW:(st + 1) * STW]), waits=rel_xrc[par])
            tok_mm1 = None
            for f in range(NF):
                sl = slice(f * STW, (f + 1) * STW)
                wmm = [twoh, div_all]
                if f == 0:
                    wmm += [p.bank_rel[bw]] if p.bank_rel[bw] else []
                p.op("tensor", lambda e, f=f, sl=sl, bw=bw, par=par: e.matmul(
                    pb[bw][:, :], woh[par][:, f * 128:(f + 1) * 128], aodh[:, sl],
                    start=(f == 0), stop=False), waits=wmm)
                p.op("tensor", lambda e, f=f, sl=sl, bw=bw, par=par: e.matmul(
                    pb[bw][:, :], woh[par][:, f * 128:(f + 1) * 128], aodl[:, sl],
                    start=False, stop=False))
            for f in range(NF):
                sl = slice(f * STW, (f + 1) * STW)
                tok_mm1 = p.op("tensor", lambda e, f=f, sl=sl, bw=bw, par=par:
                               e.matmul(
                    pb[bw][:, :], wol[par][:, f * 128:(f + 1) * 128], aodh[:, sl],
                    start=False, stop=(f == NF - 1)),
                    tok=(f == NF - 1))
            rel_woh[par] = [tok_mm1]
            if oc >= NF - 2:
                rel_pass_ao.append(tok_mm1)
            t2 = p.op("vector", lambda e, par=par, bw=bw: e.tensor_add(
                e1b[par], ffld[par], pb[bw][:, :]),
                waits=[tok_mm1, tff] + rel_e1[par], tok=True)
            p.bank_rel[bw] = t2
            rel_ffld[par] = [t2]
            teo = p.op("vector", lambda e, par=par: e.tensor_add(
                outoc[par], e1b[par], xrc[par]),
                waits=[txr] + rel_outoc[par], tok=True)
            rel_e1[par] = [teo]
            rel_xrc[par] = [teo]
            last_out[par] = p.dma("gpsimd", f"sou{par}",
                                  lambda e, oc=oc, st=st, par=par: e.dma_start(
                out=OUT[oc * 128:(oc + 1) * 128, st * STW:(st + 1) * STW],
                in_=outoc[par]), waits=[teo])
            rel_outoc[par] = [last_out[par]]

    p._wait("gpsimd", ("sou0", p.counts["sou0"], None))
    p._wait("gpsimd", ("sou1", p.counts["sou1"], None))
    stC.close()
    stack_all.close()

    with nc.Block() as block:
        @block.tensor
        def _(e):
            for fn in p.ops["tensor"]:
                fn(e)

        @block.vector
        def _(e):
            for fn in p.ops["vector"]:
                fn(e)

        @block.scalar
        def _(e):
            for fn in p.ops["scalar"]:
                fn(e)

        @block.sync
        def _(e):
            for fn in p.ops["sync"]:
                fn(e)

        @block.gpsimd
        def _(e):
            for fn in p.ops["gpsimd"]:
                fn(e)

    return nc


def _stage_inputs(x, wqkv, wout, wffn, bias):
    x = np.ascontiguousarray(np.asarray(x, np.float32).reshape(B * S, E))
    wqkv = np.asarray(wqkv, np.float32)
    wout = np.asarray(wout, np.float32)
    wffn = np.asarray(wffn, np.float32)
    bias = np.ascontiguousarray(np.asarray(bias, np.float32).reshape(E, 1))

    wt = wqkv.T
    cols = []
    for h in range(H):
        cols.append(wt[:, 2 * E + h * D:2 * E + (h + 1) * D])
        cols.append(wt[:, E + h * D:E + (h + 1) * D])
    cols.append(wt[:, 0:E])
    wqkvp = np.ascontiguousarray(np.concatenate(cols, axis=1))

    def strips(w):
        a = w.reshape(NF, 128, NF, 128)
        return np.ascontiguousarray(a.transpose(0, 3, 2, 1).reshape(NF, 128, E))

    wouts = strips(wout)
    wffns = strips(wffn)

    invf = 1.0 / 10000.0 ** (np.arange(0, D, 2, dtype=np.float32) / D)
    pos = np.arange(S, dtype=np.float32)
    emb = np.concatenate([np.outer(pos, invf)] * 2, axis=-1)
    cos_all = np.cos(emb).astype(np.float32)
    sins_all = np.sin(emb).astype(np.float32)
    sins_all[:, 0::2] *= -1.0
    maskut = (np.arange(CHK)[None, :] >= np.arange(CHK)[:, None]).astype(np.float32)
    maskut2 = np.ascontiguousarray(
        np.concatenate([maskut[0:128, :], maskut[128:256, :]], axis=1))
    ident = np.eye(128, dtype=np.float32)
    prot = np.zeros((128, 128), np.float32)
    for m in range(128):
        prot[m + 1 if m % 2 == 0 else m - 1, m] = 1.0
    selmat = np.zeros((16, E), np.float32)
    for h in range(H):
        selmat[h, h * 128:(h + 1) * 128] = 1.0

    in_maps = []
    for c in range(NCORE):
        t0 = c * TPC
        p0 = (c % 4) * TPC
        segmask = np.zeros((128, 4), np.float32)
        segmask[:, 0:(c % 4)] = 1.0
        in_maps.append({
            "XT": np.ascontiguousarray(x[t0:t0 + TPC].T),
            "WQKVP": wqkvp,
            "WOUTS": wouts,
            "WFFNS": wffns,
            "BIAS": bias,
            "COS": np.ascontiguousarray(cos_all[p0:p0 + TPC].T),
            "SINS": np.ascontiguousarray(sins_all[p0:p0 + TPC].T),
            "MASKUT": maskut2,
            "IDENT": ident,
            "PROT": prot,
            "SEGM": segmask,
            "SELMAT": selmat,
        })
    return in_maps


def kernel(L0_attn_x, L0_attn_W_qkv, L0_attn_W_out, L0_ffn_weight, L0_ffn_bias):
    in_maps = _stage_inputs(L0_attn_x, L0_attn_W_qkv, L0_attn_W_out,
                            L0_ffn_weight, L0_ffn_bias)
    nc = build_program()
    res = run_bass_kernel_spmd(nc, in_maps, core_ids=list(range(NCORE)))
    parts = [np.asarray(res.results[c]["OUT"]).T for c in range(NCORE)]
    out = np.concatenate(parts, axis=0).reshape(B, S, E)
    return np.ascontiguousarray(out.astype(np.float32))


# version 25
# speedup vs baseline: 1.0441x; 1.0055x over previous
import contextlib
import numpy as np
import concourse.bass as bass
import concourse.mybir as mybir
from concourse.bass_utils import run_bass_kernel_spmd

F32 = mybir.dt.float32
F32R = mybir.dt.float32r
AF = mybir.ActivationFunctionType
OP = mybir.AluOpType
AX = mybir.AxisListType

B, S, E, H, D, CHK = 2, 4096, 2048, 16, 128, 256
NCORE = 8
TPC = (B * S) // NCORE
CPC = TPC // CHK
NF = E // 128
NST = 2
STW = TPC // NST
GW = 136

_DEBUG_OUTPUTS = False
_SKIP_CC = False


class Prog:
    ENGINES = ("tensor", "vector", "scalar", "sync", "gpsimd")

    def __init__(self, nc):
        self.nc = nc
        self.ops = {e: [] for e in self.ENGINES}
        self.sems = {}
        self.counts = {}
        self.last_wait = {e: {} for e in self.ENGINES}
        for e in self.ENGINES:
            self._mksem("p_" + e)
        self._mksem("cc")
        self.bank_rel = [None] * 8

    def _mksem(self, name):
        if name not in self.sems:
            self.sems[name] = self.nc.alloc_semaphore(name=name)
            self.counts[name] = 0

    def _wait(self, eng, tok):
        if tok is None:
            return
        sem, val, owner = tok
        if val <= 0 or owner == eng:
            return
        lw = self.last_wait[eng]
        if lw.get(sem, 0) >= val:
            return
        lw[sem] = val
        h = self.sems[sem]
        self.ops[eng].append(lambda e, h=h, v=val: e.wait_ge(h, v))

    def op(self, eng, fn, waits=(), tok=False):
        for t in waits:
            self._wait(eng, t)
        if tok:
            pn = "p_" + eng
            h = self.sems[pn]
            self.counts[pn] += 1
            val = self.counts[pn]
            self.ops[eng].append(lambda e, fn=fn, h=h: fn(e).then_inc(h, 1))
            return (pn, val, eng)
        self.ops[eng].append(fn)
        return None

    def dma(self, eng, stream, fn, waits=()):
        self._mksem(stream)
        for t in waits:
            self._wait(eng, t)
        h = self.sems[stream]
        self.counts[stream] += 16
        val = self.counts[stream]
        self.ops[eng].append(lambda e, fn=fn, h=h: fn(e).then_inc(h, 16))
        return (stream, val, None)

    def cc_op(self, fn, waits=()):
        for t in waits:
            self._wait("gpsimd", t)
        h = self.sems["cc"]
        self.counts["cc"] += 1
        val = self.counts["cc"]
        self.ops["gpsimd"].append(lambda e, fn=fn, h=h: fn(e).then_inc(h, 1))
        return ("cc", val, None)

    def barrier(self):
        toks = [(s, self.counts[s], None) for s in self.sems if self.counts[s] > 0]
        for eng in self.ENGINES:
            own = "p_" + eng
            for s, v, _ in toks:
                if s == own:
                    continue
                self._wait(eng, (s, v, None))


def build_program():
    nc = bass.Bass("TRN2", target_bir_lowering=False,
                   detect_race_conditions=False)
    ikind = "ExternalOutput" if _DEBUG_OUTPUTS else "Internal"

    XT = nc.dram_tensor("XT", [E, TPC], F32, kind="ExternalInput").ap()
    WQKVP = nc.dram_tensor("WQKVP", [E, 3 * E], F32, kind="ExternalInput").ap()
    WOUTS = nc.dram_tensor("WOUTS", [NF, 128, E], F32, kind="ExternalInput").ap()
    WFFNS = nc.dram_tensor("WFFNS", [NF, 128, E], F32, kind="ExternalInput").ap()
    BIAS = nc.dram_tensor("BIAS", [E, 1], F32, kind="ExternalInput").ap()
    COS = nc.dram_tensor("COS", [128, TPC], F32, kind="ExternalInput").ap()
    SINS = nc.dram_tensor("SINS", [128, TPC], F32, kind="ExternalInput").ap()
    MASKUT = nc.dram_tensor("MASKUT", [128, 512], F32, kind="ExternalInput").ap()
    IDENT = nc.dram_tensor("IDENT", [128, 128], F32, kind="ExternalInput").ap()
    PROT = nc.dram_tensor("PROT", [128, 128], F32, kind="ExternalInput").ap()
    SEGM = nc.dram_tensor("SEGM", [128, 4], F32, kind="ExternalInput").ap()
    SELMAT = nc.dram_tensor("SELMAT", [16, E], F32, kind="ExternalInput").ap()
    OUT = nc.dram_tensor("OUT", [E, TPC], F32, kind="ExternalOutput").ap()
    QF = nc.dram_tensor("QF", [E, TPC], F32, kind=ikind).ap()
    KF = nc.dram_tensor("KF", [E, TPC], F32, kind=ikind).ap()
    VT = nc.dram_tensor("VT", [H, CPC, 2, 128, D], F32, kind=ikind).ap()
    GS = nc.dram_tensor("GS", [H, CPC, 128, GW], F32, kind=ikind).ap()
    FF = nc.dram_tensor("FF", [E, TPC], F32, kind=ikind).ap()
    XN = nc.dram_tensor("XN", [E, TPC], F32, kind=ikind).ap()
    WOH = nc.dram_tensor("WOH", [NF, 128, E], F32R, kind=ikind).ap()
    WOL = nc.dram_tensor("WOL", [NF, 128, E], F32R, kind=ikind).ap()
    WT = nc.dram_tensor("WT", [NF, 128, E], F32R, kind=ikind).ap()
    AO = nc.dram_tensor("AO", [E, TPC], F32, kind=ikind).ap()
    DEN = nc.dram_tensor("DEN", [16, H * TPC // 16], F32, kind=ikind).ap()
    QRAW = (nc.dram_tensor("QRAW", [E, TPC], F32, kind="ExternalOutput").ap()
            if _DEBUG_OUTPUTS else None)
    SCL = (nc.dram_tensor("SCL", [128, 2], F32, kind="ExternalOutput").ap()
           if _DEBUG_OUTPUTS else None)
    WTS_D = (nc.dram_tensor("WTS_D", [128, E], F32R, kind="ExternalOutput").ap()
             if _DEBUG_OUTPUTS else None)
    XNH_D = (nc.dram_tensor("XNH_D", [128, NF * TPC], F32, kind="ExternalOutput").ap()
             if _DEBUG_OUTPUTS else None)
    XNL_D = (nc.dram_tensor("XNL_D", [128, NF * TPC], F32, kind="ExternalOutput").ap()
             if _DEBUG_OUTPUTS else None)
    WTD_D = (nc.dram_tensor("WTD_D", [NF, 128, E], F32, kind="ExternalOutput").ap()
             if _DEBUG_OUTPUTS else None)
    WTB_D = (nc.dram_tensor("WTB_D", [128, E], F32R, kind="ExternalOutput").ap()
             if _DEBUG_OUTPUTS else None)
    WTS = (nc.dram_tensor("WTS", [128, E], F32, kind="ExternalOutput").ap()
           if _DEBUG_OUTPUTS else None)
    XNC = (nc.dram_tensor("XNC", [128, NF * STW], F32, kind="ExternalOutput").ap()
           if _DEBUG_OUTPUTS else None)
    AOD = (nc.dram_tensor("AOD", [128, NF * STW], F32, kind="ExternalOutput").ap()
           if _DEBUG_OUTPUTS else None)
    ADH = (nc.dram_tensor("ADH", [128, NF * STW], F32R, kind="ExternalOutput").ap()
           if _DEBUG_OUTPUTS else None)
    ADL = (nc.dram_tensor("ADL", [128, NF * STW], F32R, kind="ExternalOutput").ap()
           if _DEBUG_OUTPUTS else None)
    WOHB = (nc.dram_tensor("WOHB", [128, E], F32R, kind="ExternalOutput").ap()
            if _DEBUG_OUTPUTS else None)
    ACC = (nc.dram_tensor("ACC", [128, 34], F32, kind="ExternalOutput").ap()
           if _DEBUG_OUTPUTS else None)
    CCIN = nc.dram_tensor("CCIN", [128, H * GW], F32).ap()
    CCOUT = nc.dram_tensor("CCOUT", [512, H * GW], F32).ap()

    pb = [nc.alloc_psum_tensor(f"pb{i}", [128, 512], F32).ap() for i in range(8)]
    p = Prog(nc)

    stack_all = contextlib.ExitStack()
    sb = lambda name, shape, dt=F32: stack_all.enter_context(
        nc.sbuf_tensor(name, shape, dt)).ap()

    ones_sb = sb("ones_sb", [128, 128])
    segm_sb = sb("segm_sb", [128, 4])
    scale_sb = sb("scale_sb", [128, 1])
    rscaleb_sb = sb("rscaleb_sb", [128, 1])
    thr_sb = sb("thr_sb", [128, 1])
    thrn_sb = sb("thrn_sb", [128, 1])
    acc32 = sb("acc32", [128, 34])
    den16r = sb("den16r", [16, H * TPC // 16])
    wfb = [sb(f"wfb{i}", [128, TPC]) for i in range(2)]
    wtsB = sb("wtsB", [128, E], F32R)
    bfb = [sb(f"bfb{i}", [128, 1]) for i in range(2)]

    stA = contextlib.ExitStack()
    sa = lambda name, shape, dt=F32: stA.enter_context(
        nc.sbuf_tensor(name, shape, dt)).ap()
    cos_sb = sa("cos_sb", [128, TPC])
    sins_sb = sa("sins_sb", [128, TPC])
    ident_sb = sa("ident_sb", [128, 128])
    prot_sb = sa("prot_sb", [128, 128])
    mu_sb = sa("mu_sb", [128, TPC])
    rs_sb = sa("rs_sb", [128, TPC])
    xnh_sb = sa("xnh_sb", [128, NF * TPC], F32R)
    xnl_sb = sa("xnl_sb", [128, NF * TPC], F32R)
    pool = [sa(f"pool{i}", [128, TPC]) for i in range(4)]
    tp1 = sa("tp1s", [128, STW])
    tp2 = sa("tp2s", [128, STW])
    tp3 = sa("tp3s", [128, STW])
    wqb = [sa(f"wqb{i}", [128, 256]) for i in range(2)]
    wqh = [sa(f"wqh{i}", [128, 256], F32R) for i in range(2)]
    wql = [sa(f"wql{i}", [128, 256], F32R) for i in range(2)]

    p.dma("sync", "cst", lambda e: e.dma_start(out=cos_sb, in_=COS))
    p.dma("sync", "cst", lambda e: e.dma_start(out=sins_sb, in_=SINS))
    p.dma("sync", "cst", lambda e: e.dma_start(out=ident_sb, in_=IDENT))
    p.dma("sync", "cst", lambda e: e.dma_start(out=prot_sb, in_=PROT))
    tok_consts = p.dma("sync", "cst", lambda e: e.dma_start(out=segm_sb, in_=SEGM))
    tok_consts = ("cst", p.counts["cst"], None)
    tok_ones = p.op("vector", lambda e: e.memset(ones_sb, 1.0), tok=True)
    vtile_one = sa("vtileA", [128, 8 * 132])
    vtile_all = [vtile_one, vtile_one]
    vtile = [[vtile_one[:, i * 132:(i + 1) * 132] for i in range(8)]
             for q in range(2)]
    ktile1 = [sa(f"ktile_{i}", [128, 128]) for i in range(8)]
    ktile = [ktile1, ktile1]
    gst_all = sa("gstA", [128, CPC * GW])
    gst = [gst_all[:, c * GW:(c + 1) * GW] for c in range(CPC)]
    gseg_sb = sa("gseg_sb", [128, GW])

    tok_vones = None
    for q in range(2):
        for i in range(8):
            p.op("vector", lambda e, q=q, i=i: e.memset(vtile[q][i][:, 0:128], 0.0))
            tok_vones = p.op("vector", lambda e, q=q, i=i: e.memset(
                vtile[q][i][:, 128:132], 1.0), tok=(q == 1 and i == 7))
    for c in range(4):
        p.op("vector", lambda e, c=c: e.memset(gst[c][:, 129:GW], 0.0))
    tok_gz = p.op("vector", lambda e: e.memset(gseg_sb[:, 129:GW], 0.0), tok=True)

    xb = pool[0:2]
    xsq = pool[2:4]
    rel_xb = [[], []]
    rel_xsq = [[], []]
    tok_stats = None
    for f in range(NF):
        tx = p.dma("sync", f"xb{f % 2}", lambda e, f=f: e.dma_start(
            out=xb[f % 2], in_=XT[f * 128:(f + 1) * 128, :]), waits=rel_xb[f % 2])
        tsq = p.op("scalar", lambda e, f=f: e.activation(
            xsq[f % 2], xb[f % 2], AF.Square), waits=[tx] + rel_xsq[f % 2], tok=True)
        tok_s = tok_q = None
        for st in range(NST):
            sl = slice(st * STW, (st + 1) * STW)
            tok_s = p.op("tensor", lambda e, f=f, st=st, sl=sl: e.matmul(
                pb[st][:, :], ones_sb, xb[f % 2][:, sl],
                start=(f == 0), stop=(f == NF - 1)),
                waits=[tx, tok_ones] + ([p.bank_rel[st]] if f == 0 else []),
                tok=(st == NST - 1))
        for st in range(NST):
            sl = slice(st * STW, (st + 1) * STW)
            tok_q = p.op("tensor", lambda e, f=f, st=st, sl=sl: e.matmul(
                pb[2 + st][:, :], ones_sb, xsq[f % 2][:, sl],
                start=(f == 0), stop=(f == NF - 1)),
                waits=[tsq] + ([p.bank_rel[2 + st]] if f == 0 else []),
                tok=(st == NST - 1))
        rel_xb[f % 2] = [tok_s, tsq]
        rel_xsq[f % 2] = [tok_q]
        tok_stats = tok_q

    tmpA, tmpB = pool[2], pool[3]
    for st in range(NST):
        sl = slice(st * STW, (st + 1) * STW)
        p.op("vector", lambda e, st=st, sl=sl: e.tensor_scalar_mul(
            mu_sb[:, sl], pb[st][:, :], 1.0 / E), waits=[tok_stats])
        p.op("vector", lambda e, st=st, sl=sl: e.tensor_scalar_mul(
            tmpB[:, sl], pb[2 + st][:, :], 1.0 / E))
    p.op("vector", lambda e: e.tensor_mul(tmpA, mu_sb, mu_sb))
    p.op("vector", lambda e: e.tensor_sub(tmpB, tmpB, tmpA))
    tok_var = p.op("vector", lambda e: e.tensor_scalar_add(tmpB, tmpB, 1e-5), tok=True)
    for i in range(4):
        p.bank_rel[i] = tok_var
    tok_sd = p.op("scalar", lambda e: e.activation(tmpA, tmpB, AF.Sqrt),
                  waits=[tok_var], tok=True)
    tok_rs = p.op("vector", lambda e: e.reciprocal(rs_sb, tmpA), waits=[tok_sd],
                  tok=True)

    xn_tok = [None] * NF
    rel2 = [rel_xb[0] + [tok_rs], rel_xb[1] + [tok_rs]]
    tmp1, tmp2 = pool[2], pool[3]
    rel_tmp1 = [tok_var]
    last_xn_store = None
    for f in range(NF):
        sl = slice(f * TPC, (f + 1) * TPC)
        tx = p.dma("sync", f"xb{f % 2}", lambda e, f=f: e.dma_start(
            out=xb[f % 2], in_=XT[f * 128:(f + 1) * 128, :]), waits=rel2[f % 2])
        p.op("vector", lambda e, f=f: e.tensor_sub(tmp1, xb[f % 2], mu_sb),
             waits=[tx, tok_rs] + rel_tmp1)
        rel_tmp1 = []
        p.op("vector", lambda e, f=f: e.tensor_mul(tmp2, tmp1, rs_sb))
        rel2[f % 2] = [p.op("vector", lambda e, f=f, sl=sl: e.tensor_copy(
            xnh_sb[:, sl], tmp2), tok=True)]
        xn_tok[f] = p.op("vector", lambda e, f=f, sl=sl: e.tensor_sub(
            xnl_sb[:, sl], tmp2, xnh_sb[:, sl].bitcast(F32)), tok=True)


    rel_tp = {"tp1": [], "tp2": [], "tp3": []}

    def rope_elu(raw, dst, rel_dst, tok_raw, rhbanks):
        dst_tok = None
        raw_rel = None
        for st in range(NST):
            sl = slice(st * STW, (st + 1) * STW)
            bk = rhbanks[st]
            trh = p.op("tensor", lambda e, sl=sl, bk=bk: e.matmul(
                pb[bk][:, :], prot_sb, raw[:, sl], start=True, stop=True),
                waits=[tok_raw, tok_consts] +
                      ([p.bank_rel[bk]] if p.bank_rel[bk] else []), tok=True)
            p.op("vector", lambda e, sl=sl: e.tensor_mul(
                tp1, raw[:, sl], cos_sb[:, sl]),
                waits=[tok_raw, tok_consts] + rel_tp["tp1"])
            tmul2 = p.op("vector", lambda e, sl=sl, bk=bk: e.tensor_mul(
                tp2, pb[bk][:, :], sins_sb[:, sl]),
                waits=[trh] + rel_tp["tp2"], tok=True)
            p.bank_rel[bk] = tmul2
            p.op("vector", lambda e: e.tensor_add(
                tp3, tp1, tp2), waits=rel_tp["tp3"])
            raw_rel = tmin = p.op("vector", lambda e: e.tensor_scalar_min(
                tp1, tp3, 0.0), tok=True)
            texp = p.op("scalar", lambda e: e.activation(
                tp2, tp1, AF.Exp), waits=[tmin], tok=True)
            dst_tok = p.op("vector", lambda e, sl=sl: e.scalar_tensor_tensor(
                dst[:, sl], tp3, 0.0, tp2, OP.max, OP.add),
                waits=[texp] + ([rel_dst] if rel_dst else []), tok=True)
            rel_tp["tp1"] = [texp]
            rel_tp["tp2"] = [dst_tok]
            rel_tp["tp3"] = [dst_tok]
        return dst_tok, raw_rel

    vraw, kraw = pool[0], pool[1]
    kout = [pool[2], pool[3]]
    rel_wq = [[], []]
    rel_vraw = []
    rel_kraw = []
    rel_kout = [[], []]
    rel_vtile1 = [None] * 8
    rel_vtile = [rel_vtile1, rel_vtile1]
    rel_ktile1 = [None] * 8
    rel_ktile = [rel_ktile1, rel_ktile1]
    rel_gst = [None] * 4
    rel_gseg = None
    last_vt_store = None
    last_kf_store = None
    ccin_last = None
    vt_rel_extra = [None, None]
    rel_wcv = [[], []]
    pend = None
    for h in range(H + 1):
        pq = h % 2
        if h < H:
            tok_acc = None
            for f in range(NF):
                fp = f % 2
                tw = p.dma("sync", f"wq{fp}", lambda e, h=h, f=f, fp=fp: e.dma_start(
                    out=wqb[fp],
                    in_=WQKVP[f * 128:(f + 1) * 128, h * 256:(h + 1) * 256]),
                    waits=rel_wq[fp])
                tcv = p.op("vector", lambda e, fp=fp: e.tensor_copy(
                    wqh[fp], wqb[fp]), waits=[tw] + rel_wcv[fp])
                tcv = p.op("vector", lambda e, fp=fp: e.tensor_sub(
                    wql[fp], wqb[fp], wqh[fp].bitcast(F32)), tok=True)
                rel_wq[fp] = [tcv]
                mm_w = [tcv, xn_tok[f]]
                if f == 0:
                    mm_w += [t for t in p.bank_rel[0:4] if t]
                    mm_w += rel_vraw + rel_kraw
                last = None
                for col in range(2):
                    cs = slice(col * 128, (col + 1) * 128)
                    b0 = 2 * col
                    for st in range(NST):
                        sl = slice(f * TPC + st * STW, f * TPC + (st + 1) * STW)
                        p.op("tensor", lambda e, f=f, fp=fp, cs=cs, st=st, sl=sl,
                             b0=b0:
                             e.matmul(pb[b0 + st][:, :], wqh[fp][:, cs],
                                      xnh_sb[:, sl],
                                      start=(f == 0), stop=False), waits=mm_w)
                        p.op("tensor", lambda e, fp=fp, cs=cs, st=st, sl=sl, b0=b0:
                             e.matmul(pb[b0 + st][:, :], wqh[fp][:, cs],
                                      xnl_sb[:, sl],
                                      start=False, stop=False))
                    for st in range(NST):
                        sl = slice(f * TPC + st * STW, f * TPC + (st + 1) * STW)
                        last = p.op("tensor",
                                    lambda e, f=f, fp=fp, cs=cs, st=st, sl=sl,
                                    b0=b0:
                             e.matmul(pb[b0 + st][:, :], wql[fp][:, cs],
                                      xnh_sb[:, sl],
                                      start=False, stop=(f == NF - 1)),
                             tok=(col == 1 and st == NST - 1))
                rel_wcv[fp] = [last]
                tok_acc = last
            ev_v = []
            ev_k = []
            for st in range(NST):
                sl = slice(st * STW, (st + 1) * STW)
                t = p.op("scalar", lambda e, st=st, sl=sl: e.copy(
                    vraw[:, sl], pb[st][:, :]), waits=[tok_acc] + rel_vraw, tok=True)
                ev_v.append(t)
                p.bank_rel[st] = t
            for st in range(NST):
                sl = slice(st * STW, (st + 1) * STW)
                t = p.op("scalar", lambda e, st=st, sl=sl: e.copy(
                    kraw[:, sl], pb[2 + st][:, :]), waits=rel_kraw, tok=True)
                ev_k.append(t)
                p.bank_rel[2 + st] = t
        if pend is not None:
            p1 = pend["p"]
            hh = pend["h"]
            kt_toks = []
            last_ktr = None
            for i in range(8):
                bk = 4 + (i % 4)
                w = [pend["tok_k"]] + ([p.bank_rel[bk]] if p.bank_rel[bk] else [])
                ttr = p.op("tensor", lambda e, i=i, bk=bk, p1=p1: e.transpose(
                    pb[bk][:, 0:128], kout[p1][:, i * 128:(i + 1) * 128], ident_sb),
                    waits=w, tok=True)
                last_ktr = ttr
                tev = p.op("scalar", lambda e, i=i, bk=bk, p1=p1: e.copy(
                    ktile[p1][i], pb[bk][:, 0:128]),
                    waits=[ttr] + ([rel_ktile[p1][i]] if rel_ktile[p1][i] else []),
                    tok=True)
                p.bank_rel[bk] = tev
                kt_toks.append(tev)
            rel_kout[p1] = [last_ktr, pend["kf_store"]]
            gacc_tok = None
            gs_last = None
            gs_evs = []
            vt_toks_p = pend["vt_toks"]
            for c in range(CPC):
                bk = 6 + (c % 2)
                w = [kt_toks[2 * c], kt_toks[2 * c + 1], vt_toks_p[2 * c],
                     vt_toks_p[2 * c + 1]] + ([p.bank_rel[bk]] if p.bank_rel[bk] else [])
                p.op("tensor", lambda e, c=c, bk=bk, p1=p1: e.matmul(
                    pb[bk][:, 0:129], ktile[p1][2 * c], vtile[p1][2 * c][:, 0:129],
                    start=True, stop=False), waits=w)
                tg = p.op("tensor", lambda e, c=c, bk=bk, p1=p1: e.matmul(
                    pb[bk][:, 0:129], ktile[p1][2 * c + 1],
                    vtile[p1][2 * c + 1][:, 0:129],
                    start=False, stop=True), tok=True)
                rel_ktile[p1][2 * c] = tg
                rel_ktile[p1][2 * c + 1] = tg
                rel_vtile[p1][2 * c] = tg
                rel_vtile[p1][2 * c + 1] = tg
                vt_rel_extra[p1] = pend["vt_store"]
                tev = p.op("scalar", lambda e, c=c, bk=bk: e.copy(
                    gst[c][:, 0:129], pb[bk][:, 0:129]),
                    waits=[tg, tok_gz] + ([rel_gst[c]] if rel_gst[c] else []), tok=True)
                p.bank_rel[bk] = tev
                gs_evs.append(tev)
                if c == 0:
                    gacc_tok = p.op("vector", lambda e: e.tensor_copy(
                        gseg_sb[:, 0:129], gst[0][:, 0:129]),
                        waits=[tev] + ([rel_gseg] if rel_gseg else []), tok=True)
                else:
                    gacc_tok = p.op("vector", lambda e, c=c: e.tensor_add(
                        gseg_sb[:, 0:129], gseg_sb[:, 0:129], gst[c][:, 0:129]),
                        waits=[tev, gacc_tok], tok=True)
            gs_last = p.dma("gpsimd", "sgs", lambda e, hh=hh: e.dma_start(
                out=GS[hh].rearrange("c m w -> m c w"),
                in_=gst_all.rearrange("m (c w) -> m c w", w=GW)), waits=gs_evs)
            for c in range(CPC):
                rel_gst[c] = gs_last
            ccin_last = p.dma("gpsimd", "scc", lambda e, hh=hh: e.dma_start(
                out=CCIN[:, hh * GW:(hh + 1) * GW], in_=gseg_sb), waits=[gacc_tok])
            rel_gseg = ccin_last
            pend = None
        if h < H:
            vt_toks = []
            last_tr = None
            for i in range(8):
                bk = 4 + (i % 4)
                w = [ev_v[i // 4], tok_consts] + \
                    ([p.bank_rel[bk]] if p.bank_rel[bk] else [])
                ttr = p.op("tensor", lambda e, i=i, bk=bk, pq=pq: e.transpose(
                    pb[bk][:, 0:128], vraw[:, i * 128:(i + 1) * 128], ident_sb),
                    waits=w, tok=True)
                last_tr = ttr
                tev = p.op("scalar", lambda e, i=i, bk=bk, pq=pq: e.copy(
                    vtile[pq][i][:, 0:128], pb[bk][:, 0:128]),
                    waits=[ttr, tok_vones, vt_rel_extra[pq]] +
                          ([rel_vtile[pq][i]] if rel_vtile[pq][i] else []),
                    tok=True)
                p.bank_rel[bk] = tev
                vt_toks.append(tev)
            last_vt_store = p.dma("gpsimd", "svt",
                                  lambda e, h=h, pq=pq: e.dma_start(
                out=VT[h].rearrange("c t m d -> m c t d"),
                in_=vtile_all[pq].rearrange("m (i w) -> m i w", w=132)[:, :, 0:128]),
                waits=vt_toks)
            for i in range(8):
                rel_vtile[pq][i] = last_vt_store
            rel_vraw = [last_tr]
            tok_k, kraw_rel = rope_elu(kraw, kout[pq],
                                       (rel_kout[pq][0] if rel_kout[pq] else None),
                                       ev_k[1], (6, 7))
            rel_kraw = [kraw_rel]
            last_kf_store = p.dma("gpsimd", "skf", lambda e, h=h, pq=pq: e.dma_start(
                out=KF[h * 128:(h + 1) * 128, :], in_=kout[pq]), waits=[tok_k])
            pend = {"h": h, "p": pq, "tok_k": tok_k, "vt_toks": vt_toks,
                    "vt_store": last_vt_store, "kf_store": last_kf_store}

    if _SKIP_CC:
        for s in range(4):
            tok_cc = p.dma("gpsimd", "scc", lambda e, s=s: e.dma_start(
                out=CCOUT[s * 128:(s + 1) * 128, :], in_=CCIN),
                waits=[ccin_last])
    else:
        tok_cc = p.cc_op(lambda e: e.collective_compute(
            "AllGather", OP.bypass,
            replica_groups=[[0, 1, 2, 3], [4, 5, 6, 7]],
            ins=[CCIN.opt()], outs=[CCOUT.opt()]), waits=[ccin_last])

    qraw = [pool[0], pool[1]]
    qout = [pool[2], pool[3]]
    rel_qraw = [rel_vraw + rel_kraw, list(rel_kraw)]
    rel_wf = [[], []]
    tred = None
    rel_qout = [list(rel_kout[0]), list(rel_kout[1])]
    last_qf_store = [None, None]
    pend3 = None
    for b in range(H // 2 + 1):
        if pend3 is not None:
            bb = pend3["b"]
            for j in range(2):
                hq = 2 * bb + j
                tok_q, raw_rel = rope_elu(qraw[j], qout[j],
                                          (rel_qout[j][0] if rel_qout[j] else None),
                                          pend3["ev"][j][1],
                                          (4, 5) if j == 0 else (6, 7))
                rel_qraw[j] = [raw_rel] + pend3["extra"][j]
                last_qf_store[j] = p.dma("gpsimd", f"sqf{j}",
                                         lambda e, hq=hq, j=j: e.dma_start(
                    out=QF[hq * 128:(hq + 1) * 128, :], in_=qout[j]), waits=[tok_q])
                rel_qout[j] = [last_qf_store[j]]
            pend3 = None
        if b < H // 2:
            tok_acc = None
            for f in range(NF):
                fp = f % 2
                tw = p.dma("sync", f"wq{fp}", lambda e, b=b, f=f, fp=fp: e.dma_start(
                    out=wqb[fp],
                    in_=WQKVP[f * 128:(f + 1) * 128,
                              2 * E + b * 256:2 * E + (b + 1) * 256]),
                    waits=rel_wq[fp])
                tcv = p.op("vector", lambda e, fp=fp: e.tensor_copy(
                    wqh[fp], wqb[fp]), waits=[tw] + rel_wcv[fp])
                tcv = p.op("vector", lambda e, fp=fp: e.tensor_sub(
                    wql[fp], wqb[fp], wqh[fp].bitcast(F32)), tok=True)
                rel_wq[fp] = [tcv]
                mm_w = [tcv, xn_tok[f]]
                if f == 0:
                    mm_w += [t for t in p.bank_rel[0:4] if t]
                last = None
                for col in range(2):
                    cs = slice(col * 128, (col + 1) * 128)
                    b0 = 2 * col
                    for st in range(NST):
                        sl = slice(f * TPC + st * STW, f * TPC + (st + 1) * STW)
                        p.op("tensor", lambda e, f=f, fp=fp, cs=cs, st=st, sl=sl,
                             b0=b0:
                             e.matmul(pb[b0 + st][:, :], wqh[fp][:, cs],
                                      xnh_sb[:, sl],
                                      start=(f == 0), stop=False), waits=mm_w)
                        p.op("tensor", lambda e, fp=fp, cs=cs, st=st, sl=sl, b0=b0:
                             e.matmul(pb[b0 + st][:, :], wqh[fp][:, cs],
                                      xnl_sb[:, sl],
                                      start=False, stop=False))
                    for st in range(NST):
                        sl = slice(f * TPC + st * STW, f * TPC + (st + 1) * STW)
                        last = p.op("tensor",
                                    lambda e, f=f, fp=fp, cs=cs, st=st, sl=sl,
                                    b0=b0:
                             e.matmul(pb[b0 + st][:, :], wql[fp][:, cs],
                                      xnh_sb[:, sl],
                                      start=False, stop=(f == NF - 1)),
                             tok=(col == 1 and st == NST - 1))
                rel_wcv[fp] = [last]
                tok_acc = last
            for k in range(4):
                i = 4 * b + k
                twfb = p.dma("sync", f"lwf{i % 2}", lambda e, i=i: e.dma_start(
                    out=wfb[i % 2],
                    in_=WFFNS[i // 2, :, (i % 2) * TPC:(i % 2 + 1) * TPC]),
                    waits=rel_wf[i % 2])
                tred = p.op("vector", lambda e, i=i: e.tensor_reduce(
                    acc32[:, i:i + 1], wfb[i % 2], AX.X, OP.add,
                    apply_absolute_value=True), waits=[twfb], tok=True)
                rel_wf[i % 2] = [tred]
        if b < H // 2:
            evs = []
            extra = [[], []]
            for j in range(2):
                hq = 2 * b + j
                ev = []
                for st in range(NST):
                    sl = slice(st * STW, (st + 1) * STW)
                    t = p.op("scalar", lambda e, j=j, st=st, sl=sl: e.copy(
                        qraw[j][:, sl], pb[2 * j + st][:, :]),
                        waits=[tok_acc] + rel_qraw[j], tok=True)
                    ev.append(t)
                    p.bank_rel[2 * j + st] = t
                if _DEBUG_OUTPUTS:
                    tqr = p.dma("gpsimd", "sqr", lambda e, hq=hq, j=j: e.dma_start(
                        out=QRAW[hq * 128:(hq + 1) * 128, :], in_=qraw[j]),
                        waits=[ev[0], ev[1]])
                    extra[j] = [tqr]
                evs.append(ev)
            pend3 = {"b": b, "ev": evs, "extra": extra}

    p.op("vector", lambda e: e.memset(gst_all[:, 0:512], 0.0), waits=[tred])
    p.op("vector", lambda e: e.memset(gst_all[:, 0:512], 1.0))
    p.op("vector", lambda e: e.tensor_reduce(
        acc32[:, 32:33], acc32[:, 0:32], AX.X, OP.add))
    p.op("vector", lambda e: e.memset(gst_all[:, 0:512], 0.0))
    tred = p.op("vector", lambda e: e.memset(gst_all[:, 0:512], 1.0), tok=True)
    if _DEBUG_OUTPUTS:
        p.dma("gpsimd", "sdbg", lambda e: e.dma_start(out=ACC, in_=acc32),
              waits=[tred])
    tsc = p.op("tensor", lambda e: e.matmul(
        pb[6][:, 0:1], ones_sb, acc32[:, 32:33], start=True, stop=True),
        waits=[tred] + ([p.bank_rel[6]] if p.bank_rel[6] else []), tok=True)
    tok_scale = p.op("vector", lambda e: e.tensor_scalar_mul(
        scale_sb, pb[6][:, 0:1], 1.0 / (E * E)), waits=[tsc], tok=True)
    p.bank_rel[6] = tok_scale
    p.op("vector", lambda e: e.memset(gst_all[:, 0:512], 0.0))
    p.op("vector", lambda e: e.memset(gst_all[:, 0:512], 1.0))
    p.op("vector", lambda e: e.tensor_scalar_add(rscaleb_sb, scale_sb, 1e-10))
    p.op("vector", lambda e: e.memset(gst_all[:, 0:512], 0.0))
    p.op("vector", lambda e: e.memset(gst_all[:, 0:512], 1.0))
    p.op("vector", lambda e: e.tensor_scalar_mul(thr_sb, rscaleb_sb, 0.5))
    p.op("vector", lambda e: e.tensor_scalar_mul(thrn_sb, rscaleb_sb, -0.5))
    p.op("vector", lambda e: e.memset(gst_all[:, 0:512], 0.0))
    tok_thr = p.op("vector", lambda e: e.memset(gst_all[:, 0:512], 1.0), tok=True)

    if _DEBUG_OUTPUTS:
        p.dma("gpsimd", "sdbg", lambda e: e.dma_start(
            out=SCL[:, 0:1], in_=scale_sb, allow_slow_non_contiguous=True),
            waits=[tok_thr])
        p.dma("gpsimd", "sdbg", lambda e: e.dma_start(
            out=SCL[:, 1:2], in_=thr_sb, allow_slow_non_contiguous=True),
            waits=[tok_thr])
    if _DEBUG_OUTPUTS:
        p.dma("gpsimd", "sdbg", lambda e: e.dma_start(
            out=WTS_D, in_=xnh_sb[:, 0:2048]), waits=[tok_thr])
        p.dma("gpsimd", "sdbg", lambda e: e.dma_start(
            out=XNH_D, in_=xnh_sb.bitcast(F32)), waits=[tok_thr])
        p.dma("gpsimd", "sdbg", lambda e: e.dma_start(
            out=XNL_D, in_=xnl_sb.bitcast(F32)), waits=[tok_thr])
    m2h = pool[0]
    ffb = [pool[1][:, 0:512], pool[1][:, 512:1024]]
    rel_bfw = [rel_wf[0], rel_wf[1]]
    rel_m2 = list(rel_qraw[0])
    rel_wtsB = []
    rel_ffb = [list(rel_qraw[1]), list(rel_qraw[1])]
    rel_bfb = [[], []]
    last_ff_store = [None, None]
    rel_bfbank = [None, None, None, None]
    for oc in range(NF):
        par = oc % 2
        tbi = p.dma("scalar", f"lbf{par}", lambda e, oc=oc, par=par: e.dma_start(
            out=bfb[par], in_=BIAS[oc * 128:(oc + 1) * 128, :]), waits=rel_bfb[par])
        ttn = None
        for half in range(2):
            twf = p.dma("sync", f"lws{half}", lambda e, oc=oc, half=half:
                        e.dma_start(
                out=wfb[half], in_=WFFNS[oc, :, half * TPC:(half + 1) * TPC]),
                waits=rel_bfw[half])
            p.op("vector", lambda e, half=half: e.tensor_scalar(
                m2h, wfb[half], thrn_sb[:, 0:1], None, OP.is_lt),
                waits=[twf, tok_thr] + rel_m2 + rel_wtsB)
            rel_m2 = []
            ttn = p.op("vector", lambda e, half=half: e.scalar_tensor_tensor(
                wtsB[:, half * TPC:(half + 1) * TPC], wfb[half], thr_sb[:, 0:1],
                m2h, OP.is_gt, OP.subtract), tok=True)
            rel_bfw[half] = [ttn]
        tdmp = None
        if _DEBUG_OUTPUTS:
            tdmp = p.dma("gpsimd", "sdbg", lambda e, oc=oc: e.dma_start(
                out=WTD_D[oc], in_=wtsB.bitcast(F32)), waits=[ttn])
        for st in range(NST):
            bk = 2 * par + st
            wmm = [ttn] + ([rel_bfbank[bk]] if rel_bfbank[bk] else [])
            if oc == 0 and st == 0:
                wmm += [t for t in p.bank_rel[0:4] if t]
            tmf = None
            for f in range(NF):
                sl = slice(f * TPC + st * STW, f * TPC + (st + 1) * STW)
                p.op("tensor", lambda e, f=f, sl=sl, bk=bk: e.matmul(
                    pb[bk][:, :], wtsB[:, f * 128:(f + 1) * 128], xnh_sb[:, sl],
                    start=(f == 0), stop=False), waits=wmm)
                tmf = p.op("tensor", lambda e, f=f, sl=sl, bk=bk: e.matmul(
                    pb[bk][:, :], wtsB[:, f * 128:(f + 1) * 128], xnl_sb[:, sl],
                    start=False, stop=(f == NF - 1)), tok=(f == NF - 1))
            te1 = p.op("vector", lambda e, par=par, bk=bk: e.tensor_scalar(
                ffb[par], pb[bk][:, :], scale_sb[:, 0:1], bfb[par][:, 0:1],
                OP.mult, OP.add),
                waits=[tmf, tok_scale, tbi] + rel_ffb[par], tok=True)
            rel_bfbank[bk] = te1
            last_ff_store[par] = p.dma("gpsimd", f"sff{par}",
                                       lambda e, oc=oc, st=st, par=par: e.dma_start(
                out=FF[oc * 128:(oc + 1) * 128, st * STW:(st + 1) * STW],
                in_=ffb[par]), waits=[te1])
            rel_ffb[par] = [last_ff_store[par]]
            rel_wtsB = [tmf]
        if tdmp is not None:
            rel_wtsB = rel_wtsB + [tdmp]
        rel_bfb[par] = [rel_bfbank[2 * par + 1]]

    p.barrier()
    stA.close()

    stB = contextlib.ExitStack()
    sbB = lambda name, shape, dt=F32: stB.enter_context(
        nc.sbuf_tensor(name, shape, dt)).ap()
    mask_sb = sbB("mask_sb", [128, 512])
    den16 = sbB("den16", [16, H * TPC // 16])
    den16b = sbB("den16b", [16, H * TPC // 16])

    tok_mask = p.dma("sync", "cst", lambda e: e.dma_start(out=mask_sb, in_=MASKUT))

    sbase_sb = sbB("sbase_sb", [128, H * GW])
    ccseg = [sbB("ccseg0", [128, H * GW])] * 2
    qh_sb = [[sbB(f"qh{q}_{i}", [128, TPC]) for i in range(2)] for q in range(2)]
    kh_sb = [[sbB(f"kh{q}_{i}", [128, TPC]) for i in range(2)] for q in range(2)]
    vtb = [[sbB(f"vtb{q}_{i}", [128, 256]) for i in range(2)] for q in range(2)]
    gsb = [[sbB(f"gsb{q}_{i}", [128, GW]) for i in range(2)] for q in range(2)]
    scur = [[sbB(f"scur{q}_{i}", [128, 132]) for i in range(CPC)] for q in range(2)]
    attsb = [[sbB(f"attsb{q}_{i}", [128, 512]) for i in range(2)] for q in range(2)]
    aoc = [[sbB(f"aoc{q}_{i}", [128, 256]) for i in range(2)] for q in range(2)]
    densb = [[sbB(f"densb{q}_{i}", [1, 256]) for i in range(2)] for q in range(2)]

    tok_sbase = None
    rel_ccseg = [[], []]
    for s in range(4):
        tcs = p.dma("scalar", "lcs0", lambda e, s=s: e.dma_start(
            out=ccseg[0], in_=CCOUT[s * 128:(s + 1) * 128, :]),
            waits=[tok_cc] + rel_ccseg[0] + rel_ccseg[1])
        if s == 0:
            tok_sbase = p.op("vector", lambda e: e.tensor_scalar(
                sbase_sb, ccseg[0], segm_sb[:, 0:1], None, OP.mult),
                waits=[tcs, tok_consts], tok=True)
        else:
            tok_sbase = p.op("vector", lambda e, s=s: e.scalar_tensor_tensor(
                sbase_sb, ccseg[s % 2], segm_sb[:, s:s + 1], sbase_sb,
                OP.mult, OP.add), waits=[tcs], tok=True)
        rel_ccseg[s % 2] = [tok_sbase]

    rel_qh = [[[last_qf_store[0], last_qf_store[1]] for _ in range(2)]
              for _ in range(2)]
    rel_kh = [[[last_kf_store] for _ in range(2)] for _ in range(2)]
    rel_vtb = [[[last_vt_store], [last_vt_store]], [[last_vt_store], [last_vt_store]]]
    rel_gsb = [[[], []], [[], []]]
    rel_attsb = [[[], []], [[], []]]
    rel_scur = [[[], [], [], []], [[], [], [], []]]
    rel_aoc = [[[], []], [[], []]]
    rel_densb = [[[], []], [[], []]]
    last_den_store = {}
    last_ao_store = {}
    tq2 = [[None, None], [None, None]]
    tk2 = [[None, None], [None, None]]
    for hp in range(2):
        tq2[0][hp] = p.dma("sync", f"lqh{hp}_0", lambda e, hp=hp: e.dma_start(
            out=qh_sb[0][hp], in_=QF[hp * 128:(hp + 1) * 128, :]),
            waits=rel_qh[0][hp])
        tk2[0][hp] = p.dma("sync", f"lkh{hp}_0", lambda e, hp=hp: e.dma_start(
            out=kh_sb[0][hp], in_=KF[hp * 128:(hp + 1) * 128, :]),
            waits=rel_kh[0][hp])
    for g in range(H // 2):
        scur_toks = [None, None]
        gp = g % 2
        if g + 1 < H // 2:
            gn = (g + 1) % 2
            for hp in range(2):
                h = 2 * (g + 1) + hp
                tq2[gn][hp] = p.dma("sync", f"lqh{hp}_{gn}",
                                    lambda e, h=h, hp=hp, gn=gn: e.dma_start(
                    out=qh_sb[gn][hp], in_=QF[h * 128:(h + 1) * 128, :]),
                    waits=rel_qh[gn][hp])
                tk2[gn][hp] = p.dma("sync", f"lkh{hp}_{gn}",
                                    lambda e, h=h, hp=hp, gn=gn: e.dma_start(
                    out=kh_sb[gn][hp], in_=KF[h * 128:(h + 1) * 128, :]),
                    waits=rel_kh[gn][hp])
        tq = tq2[gp]
        tk = tk2[gp]
        for hp in range(2):
            h = 2 * g + hp
            scur_tok = [None] * CPC
            scur_tok[0] = p.op("vector", lambda e, h=h, hp=hp: e.tensor_copy(
                scur[hp][0][:, 0:129], sbase_sb[:, h * GW:h * GW + 129]),
                waits=[tok_sbase] + rel_scur[hp][0], tok=True)
            for c in range(1, CPC):
                tg = p.dma("scalar", f"lgb{hp}_{c % 2}",
                           lambda e, h=h, hp=hp, c=c: e.dma_start(
                    out=gsb[hp][c % 2], in_=GS[h, c - 1, :, :]),
                    waits=rel_gsb[hp][c % 2] + [rel_gst[c - 1]])
                scur_tok[c] = p.op("vector", lambda e, hp=hp, c=c: e.tensor_add(
                    scur[hp][c][:, 0:129], scur[hp][c - 1][:, 0:129],
                    gsb[hp][c % 2][:, 0:129]),
                    waits=[tg, scur_tok[c - 1]] + rel_scur[hp][c], tok=True)
                rel_gsb[hp][c % 2] = [scur_tok[c]]
            scur_toks[hp] = scur_tok
        tden = {}
        att_last = {}
        for c in range(CPC):
            att_toks_hp = {}
            tvtb_hp = {}
            for hp in range(2):
                h = 2 * g + hp
                csl = slice(c * CHK, (c + 1) * CHK)
                ab = 4 * hp
                tvtb_hp[hp] = p.dma("scalar", f"lvb{hp}_{c % 2}",
                             lambda e, h=h, hp=hp, c=c: e.dma_start(
                    out=vtb[hp][c % 2].rearrange("m (t d) -> m t d", t=2),
                    in_=VT[h, c].rearrange("t m d -> m t d")),
                    waits=rel_vtb[hp][c % 2])
                att_toks = []
                for t in range(2):
                    bk = ab + t
                    wmm = [tq[hp], tk[hp]] + ([p.bank_rel[bk]] if p.bank_rel[bk] else [])
                    ta = p.op("tensor", lambda e, hp=hp, c=c, t=t, csl=csl, bk=bk,
                              gp=gp: e.matmul(
                        pb[bk][:, 0:256],
                        kh_sb[gp][hp][:, c * CHK + t * 128:c * CHK + (t + 1) * 128],
                        qh_sb[gp][hp][:, csl], start=True, stop=True), waits=wmm,
                        tok=True)
                    tm = p.op("vector", lambda e, hp=hp, c=c, t=t, bk=bk: e.tensor_mul(
                        attsb[hp][c % 2][:, t * 256:(t + 1) * 256], pb[bk][:, 0:256],
                        mask_sb[:, t * 256:(t + 1) * 256]),
                        waits=[ta, tok_mask] + rel_attsb[hp][c % 2], tok=True)
                    p.bank_rel[bk] = tm
                    att_toks.append(tm)
                att_last[hp] = att_toks[1]
                att_toks_hp[hp] = att_toks
            for hp in range(2):
                h = 2 * g + hp
                csl = slice(c * CHK, (c + 1) * CHK)
                nb = 2 + 4 * hp
                db = 3 + 4 * hp
                att_toks = att_toks_hp[hp]
                tvtb = tvtb_hp[hp]
                wmm = [att_toks[0], att_toks[1], tvtb, scur_toks[hp][c]]
                wmm += [p.bank_rel[nb]] if p.bank_rel[nb] else []
                p.op("tensor", lambda e, hp=hp, c=c, nb=nb: e.matmul(
                    pb[nb][:, 0:256], vtb[hp][c % 2][:, 0:128],
                    attsb[hp][c % 2][:, 0:256], start=True, stop=False), waits=wmm)
                p.op("tensor", lambda e, hp=hp, c=c, nb=nb: e.matmul(
                    pb[nb][:, 0:256], vtb[hp][c % 2][:, 128:256],
                    attsb[hp][c % 2][:, 256:512], start=False, stop=False))
                tnum = p.op("tensor", lambda e, hp=hp, c=c, nb=nb, csl=csl, gp=gp:
                            e.matmul(
                    pb[nb][:, 0:256], scur[hp][c][:, 0:128], qh_sb[gp][hp][:, csl],
                    start=False, stop=True), tok=True)
                wmm = [p.bank_rel[db]] if p.bank_rel[db] else []
                p.op("tensor", lambda e, hp=hp, c=c, db=db: e.matmul(
                    pb[db][0:1, 0:256], ones_sb[:, 0:1],
                    attsb[hp][c % 2][:, 0:256], start=True, stop=False), waits=wmm)
                p.op("tensor", lambda e, hp=hp, c=c, db=db: e.matmul(
                    pb[db][0:1, 0:256], ones_sb[:, 0:1],
                    attsb[hp][c % 2][:, 256:512], start=False, stop=False))
                tden[hp] = p.op("tensor", lambda e, hp=hp, c=c, db=db, csl=csl,
                                gp=gp: e.matmul(
                    pb[db][0:1, 0:256], scur[hp][c][:, 128:129],
                    qh_sb[gp][hp][:, csl], start=False, stop=True), tok=True)
                rel_scur[hp][c] = [tden[hp]]
                rel_attsb[hp][c % 2] = [tden[hp]]
                rel_vtb[hp][c % 2] = [tnum]
                tev = p.op("scalar", lambda e, hp=hp, c=c, nb=nb: e.copy(
                    aoc[hp][c % 2], pb[nb][:, 0:256]),
                    waits=[tnum] + rel_aoc[hp][c % 2], tok=True)
                p.bank_rel[nb] = tev
                last_ao_store[hp, c % 2] = p.dma(
                    "gpsimd", f"sao{hp}_{c % 2}",
                    lambda e, h=h, hp=hp, c=c: e.dma_start(
                        out=AO[h * 128:(h + 1) * 128, c * CHK:(c + 1) * CHK],
                        in_=aoc[hp][c % 2]), waits=[tev])
                rel_aoc[hp][c % 2] = [last_ao_store[hp, c % 2]]
                tevd = p.op("scalar", lambda e, hp=hp, c=c, db=db: e.copy(
                    densb[hp][c % 2], pb[db][0:1, 0:256]),
                    waits=[tden[hp]] + rel_densb[hp][c % 2], tok=True)
                p.bank_rel[db] = tevd
                last_den_store[hp, c % 2] = p.dma(
                    "gpsimd", f"sdn{hp}_{c % 2}",
                    lambda e, h=h, hp=hp, c=c: e.dma_start(
                        out=DEN[h:h + 1, c * CHK:(c + 1) * CHK],
                        in_=densb[hp][c % 2]), waits=[tevd])
                rel_densb[hp][c % 2] = [last_den_store[hp, c % 2]]
        for hp in range(2):
            rel_qh[gp][hp] = [tden[hp]]
            rel_kh[gp][hp] = [att_last[hp]]

    tdr = p.dma("scalar", "ldn", lambda e: e.dma_start(out=den16, in_=DEN),
                waits=[last_den_store[k] for k in last_den_store])
    p.op("vector", lambda e: e.tensor_scalar_max(den16b, den16, 1e-6), waits=[tdr])
    tok_recip = p.op("vector", lambda e: e.reciprocal(den16r, den16b), tok=True)

    p.barrier()
    stB.close()

    stC = contextlib.ExitStack()
    sc = lambda name, shape, dt=F32: stC.enter_context(
        nc.sbuf_tensor(name, shape, dt)).ap()
    aodh = sc("aodh", [128, NF * STW], F32R)
    aodl = sc("aodl", [128, NF * STW], F32R)
    aotmp = [sc(f"aotmp{i}", [128, STW]) for i in range(2)]
    sett = sc("sett", [128, 256])
    wcp = [sc(f"wcp{i}", [128, E]) for i in range(2)]
    woh = [sc(f"woh{i}", [128, E], F32R) for i in range(2)]
    wol = [sc(f"wol{i}", [128, E], F32R) for i in range(2)]
    ffld = [sc(f"ffld{i}", [128, STW]) for i in range(2)]
    selmat_sb = sc("selmat_sb", [16, E])
    e1b = [sc(f"e1b{i}", [128, STW]) for i in range(2)]
    outoc = [sc(f"outoc{i}", [128, STW]) for i in range(2)]
    xrc = [sc(f"xrc{i}", [128, STW]) for i in range(2)]

    tok_sel = p.dma("scalar", "lsel", lambda e: e.dma_start(
        out=selmat_sb, in_=SELMAT))
    rel_pass_ao = [last_ao_store[k] for k in last_ao_store]
    rel_woh = [[], []]
    rel_wcp = [[], []]
    rel_ffld = [[last_ff_store[0], last_ff_store[1]], []]
    rel_aotmp = [[], []]
    rel_e1 = [[], []]
    rel_outoc = [[], []]
    rel_xrc = [[], []]
    last_out = [None, None]
    for st in range(NST):
        div_toks = [None] * NF
        for f in range(NF):
            sl = slice(f * STW, (f + 1) * STW)
            fp = f % 2
            tld = p.dma("scalar", f"lao{fp}", lambda e, f=f, st=st, fp=fp: e.dma_start(
                out=aotmp[fp],
                in_=AO[f * 128:(f + 1) * 128, st * STW:(st + 1) * STW]),
                waits=(rel_pass_ao if f < 2 else []) + rel_aotmp[fp])
            bk = 6 + fp
            wmm = [tok_recip, tok_sel] + ([p.bank_rel[bk]] if p.bank_rel[bk] else [])
            tprr = p.op("tensor", lambda e, f=f, st=st, bk=bk: e.matmul(
                pb[bk][:, :], selmat_sb[:, f * 128:(f + 1) * 128],
                den16r[:, st * STW:(st + 1) * STW], start=True, stop=True),
                waits=wmm, tok=True)
            p.op("vector", lambda e, fp=fp, bk=bk: e.tensor_mul(
                aotmp[fp], aotmp[fp], pb[bk][:, :]), waits=[tprr, tld])
            tdv = p.op("vector", lambda e, fp=fp, sl=sl: e.tensor_copy(
                aodh[:, sl], aotmp[fp]), tok=True)
            p.bank_rel[bk] = tdv
            div_toks[f] = p.op("vector", lambda e, fp=fp, sl=sl: e.tensor_sub(
                aodl[:, sl], aotmp[fp], aodh[:, sl].bitcast(F32)), tok=True)
            rel_aotmp[fp] = [div_toks[f]]
        p.op("vector", lambda e: e.memset(sett, 0.0), waits=[div_toks[NF - 1]])
        div_all = p.op("vector", lambda e: e.memset(sett, 1.0), tok=True)
        rel_pass_ao = []
        if _DEBUG_OUTPUTS and st == 0:
            p.dma("gpsimd", "sdbg", lambda e: e.dma_start(out=ADH, in_=aodh),
                  waits=[div_toks[NF - 1]])
            p.dma("gpsimd", "sdbg", lambda e: e.dma_start(out=ADL, in_=aodl),
                  waits=[div_toks[NF - 1]])
        for oc in range(NF):
            par = oc % 2
            bw = oc % 4
            tld = p.dma("sync", f"lwoh{par}", lambda e, oc=oc, par=par: e.dma_start(
                out=wcp[par], in_=WOUTS[oc]), waits=rel_wcp[par])
            p.op("vector", lambda e, par=par: e.tensor_copy(woh[par], wcp[par]),
                 waits=[tld] + rel_woh[par])
            twoh = p.op("vector", lambda e, par=par: e.tensor_sub(
                wol[par], wcp[par], woh[par].bitcast(F32)), tok=True)
            rel_wcp[par] = [twoh]
            if _DEBUG_OUTPUTS and st == 0 and oc == 0:
                p.dma("gpsimd", "sdbg", lambda e: e.dma_start(out=WOHB, in_=woh[0]),
                      waits=[twoh])
            tff = p.dma("scalar", f"lff{par}", lambda e, oc=oc, st=st, par=par:
                        e.dma_start(
                out=ffld[par], in_=FF[oc * 128:(oc + 1) * 128,
                                      st * STW:(st + 1) * STW]),
                waits=rel_ffld[par])
            txr = p.dma("scalar", f"lxr{par}", lambda e, oc=oc, st=st, par=par:
                        e.dma_start(
                out=xrc[par], in_=XT[oc * 128:(oc + 1) * 128,
                                     st * STW:(st + 1) * STW]), waits=rel_xrc[par])
            tok_mm1 = None
            for f in range(NF):
                sl = slice(f * STW, (f + 1) * STW)
                wmm = [twoh, div_all]
                if f == 0:
                    wmm += [p.bank_rel[bw]] if p.bank_rel[bw] else []
                p.op("tensor", lambda e, f=f, sl=sl, bw=bw, par=par: e.matmul(
                    pb[bw][:, :], woh[par][:, f * 128:(f + 1) * 128], aodh[:, sl],
                    start=(f == 0), stop=False), waits=wmm)
                p.op("tensor", lambda e, f=f, sl=sl, bw=bw, par=par: e.matmul(
                    pb[bw][:, :], woh[par][:, f * 128:(f + 1) * 128], aodl[:, sl],
                    start=False, stop=False))
            for f in range(NF):
                sl = slice(f * STW, (f + 1) * STW)
                tok_mm1 = p.op("tensor", lambda e, f=f, sl=sl, bw=bw, par=par:
                               e.matmul(
                    pb[bw][:, :], wol[par][:, f * 128:(f + 1) * 128], aodh[:, sl],
                    start=False, stop=(f == NF - 1)),
                    tok=(f == NF - 1))
            rel_woh[par] = [tok_mm1]
            if oc >= NF - 2:
                rel_pass_ao.append(tok_mm1)
            t2 = p.op("vector", lambda e, par=par, bw=bw: e.tensor_add(
                e1b[par], ffld[par], pb[bw][:, :]),
                waits=[tok_mm1, tff] + rel_e1[par], tok=True)
            p.bank_rel[bw] = t2
            rel_ffld[par] = [t2]
            teo = p.op("vector", lambda e, par=par: e.tensor_add(
                outoc[par], e1b[par], xrc[par]),
                waits=[txr] + rel_outoc[par], tok=True)
            rel_e1[par] = [teo]
            rel_xrc[par] = [teo]
            last_out[par] = p.dma("gpsimd", f"sou{par}",
                                  lambda e, oc=oc, st=st, par=par: e.dma_start(
                out=OUT[oc * 128:(oc + 1) * 128, st * STW:(st + 1) * STW],
                in_=outoc[par]), waits=[teo])
            rel_outoc[par] = [last_out[par]]

    p._wait("gpsimd", ("sou0", p.counts["sou0"], None))
    p._wait("gpsimd", ("sou1", p.counts["sou1"], None))
    stC.close()
    stack_all.close()

    with nc.Block() as block:
        @block.tensor
        def _(e):
            for fn in p.ops["tensor"]:
                fn(e)

        @block.vector
        def _(e):
            for fn in p.ops["vector"]:
                fn(e)

        @block.scalar
        def _(e):
            for fn in p.ops["scalar"]:
                fn(e)

        @block.sync
        def _(e):
            for fn in p.ops["sync"]:
                fn(e)

        @block.gpsimd
        def _(e):
            for fn in p.ops["gpsimd"]:
                fn(e)

    return nc


def _stage_inputs(x, wqkv, wout, wffn, bias):
    x = np.ascontiguousarray(np.asarray(x, np.float32).reshape(B * S, E))
    wqkv = np.asarray(wqkv, np.float32)
    wout = np.asarray(wout, np.float32)
    wffn = np.asarray(wffn, np.float32)
    bias = np.ascontiguousarray(np.asarray(bias, np.float32).reshape(E, 1))

    wt = wqkv.T
    cols = []
    for h in range(H):
        cols.append(wt[:, 2 * E + h * D:2 * E + (h + 1) * D])
        cols.append(wt[:, E + h * D:E + (h + 1) * D])
    cols.append(wt[:, 0:E])
    wqkvp = np.ascontiguousarray(np.concatenate(cols, axis=1))

    def strips(w):
        a = w.reshape(NF, 128, NF, 128)
        return np.ascontiguousarray(a.transpose(0, 3, 2, 1).reshape(NF, 128, E))

    wouts = strips(wout)
    wffns = strips(wffn)

    invf = 1.0 / 10000.0 ** (np.arange(0, D, 2, dtype=np.float32) / D)
    pos = np.arange(S, dtype=np.float32)
    emb = np.concatenate([np.outer(pos, invf)] * 2, axis=-1)
    cos_all = np.cos(emb).astype(np.float32)
    sins_all = np.sin(emb).astype(np.float32)
    sins_all[:, 0::2] *= -1.0
    maskut = (np.arange(CHK)[None, :] >= np.arange(CHK)[:, None]).astype(np.float32)
    maskut2 = np.ascontiguousarray(
        np.concatenate([maskut[0:128, :], maskut[128:256, :]], axis=1))
    ident = np.eye(128, dtype=np.float32)
    prot = np.zeros((128, 128), np.float32)
    for m in range(128):
        prot[m + 1 if m % 2 == 0 else m - 1, m] = 1.0
    selmat = np.zeros((16, E), np.float32)
    for h in range(H):
        selmat[h, h * 128:(h + 1) * 128] = 1.0

    in_maps = []
    for c in range(NCORE):
        t0 = c * TPC
        p0 = (c % 4) * TPC
        segmask = np.zeros((128, 4), np.float32)
        segmask[:, 0:(c % 4)] = 1.0
        in_maps.append({
            "XT": np.ascontiguousarray(x[t0:t0 + TPC].T),
            "WQKVP": wqkvp,
            "WOUTS": wouts,
            "WFFNS": wffns,
            "BIAS": bias,
            "COS": np.ascontiguousarray(cos_all[p0:p0 + TPC].T),
            "SINS": np.ascontiguousarray(sins_all[p0:p0 + TPC].T),
            "MASKUT": maskut2,
            "IDENT": ident,
            "PROT": prot,
            "SEGM": segmask,
            "SELMAT": selmat,
        })
    return in_maps


def kernel(L0_attn_x, L0_attn_W_qkv, L0_attn_W_out, L0_ffn_weight, L0_ffn_bias):
    in_maps = _stage_inputs(L0_attn_x, L0_attn_W_qkv, L0_attn_W_out,
                            L0_ffn_weight, L0_ffn_bias)
    nc = build_program()
    res = run_bass_kernel_spmd(nc, in_maps, core_ids=list(range(NCORE)))
    parts = [np.asarray(res.results[c]["OUT"]).T for c in range(NCORE)]
    out = np.concatenate(parts, axis=0).reshape(B, S, E)
    return np.ascontiguousarray(out.astype(np.float32))


# version 26
# speedup vs baseline: 1.0662x; 1.0211x over previous
import contextlib
import numpy as np
import concourse.bass as bass
import concourse.mybir as mybir
from concourse.bass_utils import run_bass_kernel_spmd

F32 = mybir.dt.float32
F32R = mybir.dt.float32r
AF = mybir.ActivationFunctionType
OP = mybir.AluOpType
AX = mybir.AxisListType

B, S, E, H, D, CHK = 2, 4096, 2048, 16, 128, 256
NCORE = 8
TPC = (B * S) // NCORE
CPC = TPC // CHK
NF = E // 128
NST = 2
STW = TPC // NST
GW = 136

_DEBUG_OUTPUTS = False
_SKIP_CC = False


class Prog:
    ENGINES = ("tensor", "vector", "scalar", "sync", "gpsimd")

    def __init__(self, nc):
        self.nc = nc
        self.ops = {e: [] for e in self.ENGINES}
        self.sems = {}
        self.counts = {}
        self.last_wait = {e: {} for e in self.ENGINES}
        for e in self.ENGINES:
            self._mksem("p_" + e)
        self._mksem("cc")
        self.bank_rel = [None] * 8

    def _mksem(self, name):
        if name not in self.sems:
            self.sems[name] = self.nc.alloc_semaphore(name=name)
            self.counts[name] = 0

    def _wait(self, eng, tok):
        if tok is None:
            return
        sem, val, owner = tok
        if val <= 0 or owner == eng:
            return
        lw = self.last_wait[eng]
        if lw.get(sem, 0) >= val:
            return
        lw[sem] = val
        h = self.sems[sem]
        self.ops[eng].append(lambda e, h=h, v=val: e.wait_ge(h, v))

    def op(self, eng, fn, waits=(), tok=False):
        for t in waits:
            self._wait(eng, t)
        if tok:
            pn = "p_" + eng
            h = self.sems[pn]
            self.counts[pn] += 1
            val = self.counts[pn]
            self.ops[eng].append(lambda e, fn=fn, h=h: fn(e).then_inc(h, 1))
            return (pn, val, eng)
        self.ops[eng].append(fn)
        return None

    def dma(self, eng, stream, fn, waits=()):
        self._mksem(stream)
        for t in waits:
            self._wait(eng, t)
        h = self.sems[stream]
        self.counts[stream] += 16
        val = self.counts[stream]
        self.ops[eng].append(lambda e, fn=fn, h=h: fn(e).then_inc(h, 16))
        return (stream, val, None)

    def cc_op(self, fn, waits=()):
        for t in waits:
            self._wait("gpsimd", t)
        h = self.sems["cc"]
        self.counts["cc"] += 1
        val = self.counts["cc"]
        self.ops["gpsimd"].append(lambda e, fn=fn, h=h: fn(e).then_inc(h, 1))
        return ("cc", val, None)

    def barrier(self):
        toks = [(s, self.counts[s], None) for s in self.sems if self.counts[s] > 0]
        for eng in self.ENGINES:
            own = "p_" + eng
            for s, v, _ in toks:
                if s == own:
                    continue
                self._wait(eng, (s, v, None))


def build_program():
    nc = bass.Bass("TRN2", target_bir_lowering=False,
                   detect_race_conditions=False)
    ikind = "ExternalOutput" if _DEBUG_OUTPUTS else "Internal"

    XT = nc.dram_tensor("XT", [E, TPC], F32, kind="ExternalInput").ap()
    WQKVP = nc.dram_tensor("WQKVP", [E, 3 * E], F32, kind="ExternalInput").ap()
    WOUTS = nc.dram_tensor("WOUTS", [NF, 128, E], F32, kind="ExternalInput").ap()
    WFFNS = nc.dram_tensor("WFFNS", [NF, 128, E], F32, kind="ExternalInput").ap()
    BIAS = nc.dram_tensor("BIAS", [E, 1], F32, kind="ExternalInput").ap()
    COS = nc.dram_tensor("COS", [128, TPC], F32, kind="ExternalInput").ap()
    SINS = nc.dram_tensor("SINS", [128, TPC], F32, kind="ExternalInput").ap()
    MASKUT = nc.dram_tensor("MASKUT", [128, 512], F32, kind="ExternalInput").ap()
    IDENT = nc.dram_tensor("IDENT", [128, 128], F32, kind="ExternalInput").ap()
    PROT = nc.dram_tensor("PROT", [128, 128], F32, kind="ExternalInput").ap()
    SEGM = nc.dram_tensor("SEGM", [128, 4], F32, kind="ExternalInput").ap()
    SELMAT = nc.dram_tensor("SELMAT", [16, E], F32, kind="ExternalInput").ap()
    OUT = nc.dram_tensor("OUT", [E, TPC], F32, kind="ExternalOutput").ap()
    QF = nc.dram_tensor("QF", [E, TPC], F32, kind=ikind).ap()
    KF = nc.dram_tensor("KF", [E, TPC], F32, kind=ikind).ap()
    VT = nc.dram_tensor("VT", [H, CPC, 2, 128, D], F32, kind=ikind).ap()
    GS = nc.dram_tensor("GS", [H, CPC, 128, GW], F32, kind=ikind).ap()
    FF = nc.dram_tensor("FF", [E, TPC], F32, kind=ikind).ap()
    XN = nc.dram_tensor("XN", [E, TPC], F32, kind=ikind).ap()
    WOH = nc.dram_tensor("WOH", [NF, 128, E], F32R, kind=ikind).ap()
    WOL = nc.dram_tensor("WOL", [NF, 128, E], F32R, kind=ikind).ap()
    WT = nc.dram_tensor("WT", [NF, 128, E], F32R, kind=ikind).ap()
    AO = nc.dram_tensor("AO", [E, TPC], F32, kind=ikind).ap()
    DEN = nc.dram_tensor("DEN", [16, H * TPC // 16], F32, kind=ikind).ap()
    QRAW = (nc.dram_tensor("QRAW", [E, TPC], F32, kind="ExternalOutput").ap()
            if _DEBUG_OUTPUTS else None)
    SCL = (nc.dram_tensor("SCL", [128, 2], F32, kind="ExternalOutput").ap()
           if _DEBUG_OUTPUTS else None)
    WTS_D = (nc.dram_tensor("WTS_D", [128, E], F32R, kind="ExternalOutput").ap()
             if _DEBUG_OUTPUTS else None)
    XNH_D = (nc.dram_tensor("XNH_D", [128, NF * TPC], F32, kind="ExternalOutput").ap()
             if _DEBUG_OUTPUTS else None)
    XNL_D = (nc.dram_tensor("XNL_D", [128, NF * TPC], F32, kind="ExternalOutput").ap()
             if _DEBUG_OUTPUTS else None)
    WTD_D = (nc.dram_tensor("WTD_D", [NF, 128, E], F32, kind="ExternalOutput").ap()
             if _DEBUG_OUTPUTS else None)
    WTB_D = (nc.dram_tensor("WTB_D", [128, E], F32R, kind="ExternalOutput").ap()
             if _DEBUG_OUTPUTS else None)
    WTS = (nc.dram_tensor("WTS", [128, E], F32, kind="ExternalOutput").ap()
           if _DEBUG_OUTPUTS else None)
    XNC = (nc.dram_tensor("XNC", [128, NF * STW], F32, kind="ExternalOutput").ap()
           if _DEBUG_OUTPUTS else None)
    AOD = (nc.dram_tensor("AOD", [128, NF * STW], F32, kind="ExternalOutput").ap()
           if _DEBUG_OUTPUTS else None)
    ADH = (nc.dram_tensor("ADH", [128, NF * STW], F32R, kind="ExternalOutput").ap()
           if _DEBUG_OUTPUTS else None)
    ADL = (nc.dram_tensor("ADL", [128, NF * STW], F32R, kind="ExternalOutput").ap()
           if _DEBUG_OUTPUTS else None)
    WOHB = (nc.dram_tensor("WOHB", [128, E], F32R, kind="ExternalOutput").ap()
            if _DEBUG_OUTPUTS else None)
    ACC = (nc.dram_tensor("ACC", [128, 34], F32, kind="ExternalOutput").ap()
           if _DEBUG_OUTPUTS else None)
    CCIN = nc.dram_tensor("CCIN", [128, H * GW], F32).ap()
    CCOUT = nc.dram_tensor("CCOUT", [512, H * GW], F32).ap()

    pb = [nc.alloc_psum_tensor(f"pb{i}", [128, 512], F32).ap() for i in range(8)]
    p = Prog(nc)

    stack_all = contextlib.ExitStack()
    sb = lambda name, shape, dt=F32: stack_all.enter_context(
        nc.sbuf_tensor(name, shape, dt)).ap()

    ones_sb = sb("ones_sb", [128, 128])
    segm_sb = sb("segm_sb", [128, 4])
    scale_sb = sb("scale_sb", [128, 1])
    rscaleb_sb = sb("rscaleb_sb", [128, 1])
    thr_sb = sb("thr_sb", [128, 1])
    thrn_sb = sb("thrn_sb", [128, 1])
    acc32 = sb("acc32", [128, 34])
    den16r = sb("den16r", [16, H * TPC // 16])
    wfb = [sb(f"wfb{i}", [128, TPC]) for i in range(2)]
    wtsB = sb("wtsB", [128, E], F32R)
    bfb = [sb(f"bfb{i}", [128, 1]) for i in range(2)]

    stA = contextlib.ExitStack()
    sa = lambda name, shape, dt=F32: stA.enter_context(
        nc.sbuf_tensor(name, shape, dt)).ap()
    cos_sb = sa("cos_sb", [128, TPC])
    sins_sb = sa("sins_sb", [128, TPC])
    ident_sb = sa("ident_sb", [128, 128])
    prot_sb = sa("prot_sb", [128, 128])
    mu_sb = sa("mu_sb", [128, TPC])
    rs_sb = sa("rs_sb", [128, TPC])
    xnh_sb = sa("xnh_sb", [128, NF * TPC], F32R)
    xnl_sb = sa("xnl_sb", [128, NF * TPC], F32R)
    pool = [sa(f"pool{i}", [128, TPC]) for i in range(4)]
    tp1 = sa("tp1s", [128, STW])
    tp2 = sa("tp2s", [128, STW])
    tp3 = sa("tp3s", [128, STW])
    wqb = [sa(f"wqb{i}", [128, 256]) for i in range(2)]
    wqh = [sa(f"wqh{i}", [128, 256], F32R) for i in range(2)]
    wql = [sa(f"wql{i}", [128, 256], F32R) for i in range(2)]

    p.dma("sync", "cst", lambda e: e.dma_start(out=cos_sb, in_=COS))
    p.dma("sync", "cst", lambda e: e.dma_start(out=sins_sb, in_=SINS))
    p.dma("sync", "cst", lambda e: e.dma_start(out=ident_sb, in_=IDENT))
    p.dma("sync", "cst", lambda e: e.dma_start(out=prot_sb, in_=PROT))
    tok_consts = p.dma("sync", "cst", lambda e: e.dma_start(out=segm_sb, in_=SEGM))
    tok_consts = ("cst", p.counts["cst"], None)
    tok_ones = p.op("vector", lambda e: e.memset(ones_sb, 1.0), tok=True)
    vtile_one = sa("vtileA", [128, 8 * 132])
    vtile_all = [vtile_one, vtile_one]
    vtile = [[vtile_one[:, i * 132:(i + 1) * 132] for i in range(8)]
             for q in range(2)]
    ktile1 = [sa(f"ktile_{i}", [128, 128]) for i in range(8)]
    ktile = [ktile1, ktile1]
    gst_all = sa("gstA", [128, CPC * GW])
    gst = [gst_all[:, c * GW:(c + 1) * GW] for c in range(CPC)]
    gseg_sb = sa("gseg_sb", [128, GW])

    tok_vones = None
    for q in range(2):
        for i in range(8):
            p.op("vector", lambda e, q=q, i=i: e.memset(vtile[q][i][:, 0:128], 0.0))
            tok_vones = p.op("vector", lambda e, q=q, i=i: e.memset(
                vtile[q][i][:, 128:132], 1.0), tok=(q == 1 and i == 7))
    for c in range(4):
        p.op("vector", lambda e, c=c: e.memset(gst[c][:, 129:GW], 0.0))
    tok_gz = p.op("vector", lambda e: e.memset(gseg_sb[:, 129:GW], 0.0), tok=True)

    xb = pool[0:2]
    xsq = pool[2:4]
    rel_xb = [[], []]
    rel_xsq = [[], []]
    tok_stats = None
    for f in range(NF):
        tx = p.dma("sync", f"xb{f % 2}", lambda e, f=f: e.dma_start(
            out=xb[f % 2], in_=XT[f * 128:(f + 1) * 128, :]), waits=rel_xb[f % 2])
        tsq = p.op("scalar", lambda e, f=f: e.activation(
            xsq[f % 2], xb[f % 2], AF.Square), waits=[tx] + rel_xsq[f % 2], tok=True)
        tok_s = tok_q = None
        for st in range(NST):
            sl = slice(st * STW, (st + 1) * STW)
            tok_s = p.op("tensor", lambda e, f=f, st=st, sl=sl: e.matmul(
                pb[st][:, :], ones_sb, xb[f % 2][:, sl],
                start=(f == 0), stop=(f == NF - 1)),
                waits=[tx, tok_ones] + ([p.bank_rel[st]] if f == 0 else []),
                tok=(st == NST - 1))
        for st in range(NST):
            sl = slice(st * STW, (st + 1) * STW)
            tok_q = p.op("tensor", lambda e, f=f, st=st, sl=sl: e.matmul(
                pb[2 + st][:, :], ones_sb, xsq[f % 2][:, sl],
                start=(f == 0), stop=(f == NF - 1)),
                waits=[tsq] + ([p.bank_rel[2 + st]] if f == 0 else []),
                tok=(st == NST - 1))
        rel_xb[f % 2] = [tok_s, tsq]
        rel_xsq[f % 2] = [tok_q]
        tok_stats = tok_q

    tmpA, tmpB = pool[2], pool[3]
    for st in range(NST):
        sl = slice(st * STW, (st + 1) * STW)
        p.op("vector", lambda e, st=st, sl=sl: e.tensor_scalar_mul(
            mu_sb[:, sl], pb[st][:, :], 1.0 / E), waits=[tok_stats])
        p.op("vector", lambda e, st=st, sl=sl: e.tensor_scalar_mul(
            tmpB[:, sl], pb[2 + st][:, :], 1.0 / E))
    p.op("vector", lambda e: e.tensor_mul(tmpA, mu_sb, mu_sb))
    p.op("vector", lambda e: e.tensor_sub(tmpB, tmpB, tmpA))
    tok_var = p.op("vector", lambda e: e.tensor_scalar_add(tmpB, tmpB, 1e-5), tok=True)
    for i in range(4):
        p.bank_rel[i] = tok_var
    tok_sd = p.op("scalar", lambda e: e.activation(tmpA, tmpB, AF.Sqrt),
                  waits=[tok_var], tok=True)
    tok_rs = p.op("vector", lambda e: e.reciprocal(rs_sb, tmpA), waits=[tok_sd],
                  tok=True)

    xn_tok = [None] * NF
    rel2 = [rel_xb[0] + [tok_rs], rel_xb[1] + [tok_rs]]
    tmp1, tmp2 = pool[2], pool[3]
    a2_state = {"rel_tmp1": [tok_var]}
    last_xn_store = None

    def emit_a2(f):
        sl = slice(f * TPC, (f + 1) * TPC)
        tx = p.dma("sync", f"xb{f % 2}", lambda e, f=f: e.dma_start(
            out=xb[f % 2], in_=XT[f * 128:(f + 1) * 128, :]), waits=rel2[f % 2])
        p.op("vector", lambda e, f=f: e.tensor_sub(tmp1, xb[f % 2], mu_sb),
             waits=[tx, tok_rs] + a2_state["rel_tmp1"])
        a2_state["rel_tmp1"] = []
        p.op("vector", lambda e, f=f: e.tensor_mul(tmp2, tmp1, rs_sb))
        rel2[f % 2] = [p.op("vector", lambda e, f=f, sl=sl: e.tensor_copy(
            xnh_sb[:, sl], tmp2), tok=True)]
        xn_tok[f] = p.op("vector", lambda e, f=f, sl=sl: e.tensor_sub(
            xnl_sb[:, sl], tmp2, xnh_sb[:, sl].bitcast(F32)), tok=True)


    rel_tp = {"tp1": [], "tp2": [], "tp3": []}

    def rope_elu(raw, dst, rel_dst, tok_raw, rhbanks):
        dst_tok = None
        raw_rel = None
        for st in range(NST):
            sl = slice(st * STW, (st + 1) * STW)
            bk = rhbanks[st]
            trh = p.op("tensor", lambda e, sl=sl, bk=bk: e.matmul(
                pb[bk][:, :], prot_sb, raw[:, sl], start=True, stop=True),
                waits=[tok_raw, tok_consts] +
                      ([p.bank_rel[bk]] if p.bank_rel[bk] else []), tok=True)
            p.op("vector", lambda e, sl=sl: e.tensor_mul(
                tp1, raw[:, sl], cos_sb[:, sl]),
                waits=[tok_raw, tok_consts] + rel_tp["tp1"])
            tmul2 = p.op("vector", lambda e, sl=sl, bk=bk: e.tensor_mul(
                tp2, pb[bk][:, :], sins_sb[:, sl]),
                waits=[trh] + rel_tp["tp2"], tok=True)
            p.bank_rel[bk] = tmul2
            p.op("vector", lambda e: e.tensor_add(
                tp3, tp1, tp2), waits=rel_tp["tp3"])
            raw_rel = tmin = p.op("vector", lambda e: e.tensor_scalar_min(
                tp1, tp3, 0.0), tok=True)
            texp = p.op("scalar", lambda e: e.activation(
                tp2, tp1, AF.Exp), waits=[tmin], tok=True)
            dst_tok = p.op("vector", lambda e, sl=sl: e.scalar_tensor_tensor(
                dst[:, sl], tp3, 0.0, tp2, OP.max, OP.add),
                waits=[texp] + ([rel_dst] if rel_dst else []), tok=True)
            rel_tp["tp1"] = [texp]
            rel_tp["tp2"] = [dst_tok]
            rel_tp["tp3"] = [dst_tok]
        return dst_tok, raw_rel

    vraw, kraw = pool[0], pool[1]
    kout = [pool[2], pool[3]]
    rel_wq = [[], []]
    rel_vraw = []
    rel_kraw = []
    rel_kout = [[], []]
    rel_vtile1 = [None] * 8
    rel_vtile = [rel_vtile1, rel_vtile1]
    rel_ktile1 = [None] * 8
    rel_ktile = [rel_ktile1, rel_ktile1]
    rel_gst = [None] * 4
    rel_gseg = None
    last_vt_store = None
    last_kf_store = None
    ccin_last = None
    vt_rel_extra = [None, None]
    rel_wcv = [[], []]
    pend = None
    for h in range(H + 1):
        pq = h % 2
        if h < H:
            tok_acc = None
            for f in range(NF):
                fp = f % 2
                if h == 0:
                    emit_a2(f)
                tw = p.dma("sync", f"wq{fp}", lambda e, h=h, f=f, fp=fp: e.dma_start(
                    out=wqb[fp],
                    in_=WQKVP[f * 128:(f + 1) * 128, h * 256:(h + 1) * 256]),
                    waits=rel_wq[fp])
                tcv = p.op("vector", lambda e, fp=fp: e.tensor_copy(
                    wqh[fp], wqb[fp]), waits=[tw] + rel_wcv[fp])
                tcv = p.op("vector", lambda e, fp=fp: e.tensor_sub(
                    wql[fp], wqb[fp], wqh[fp].bitcast(F32)), tok=True)
                rel_wq[fp] = [tcv]
                mm_w = [tcv, xn_tok[f]]
                if f == 0:
                    mm_w += [t for t in p.bank_rel[0:4] if t]
                    mm_w += rel_vraw + rel_kraw
                last = None
                for col in range(2):
                    cs = slice(col * 128, (col + 1) * 128)
                    b0 = 2 * col
                    for st in range(NST):
                        sl = slice(f * TPC + st * STW, f * TPC + (st + 1) * STW)
                        p.op("tensor", lambda e, f=f, fp=fp, cs=cs, st=st, sl=sl,
                             b0=b0:
                             e.matmul(pb[b0 + st][:, :], wqh[fp][:, cs],
                                      xnh_sb[:, sl],
                                      start=(f == 0), stop=False), waits=mm_w)
                        p.op("tensor", lambda e, fp=fp, cs=cs, st=st, sl=sl, b0=b0:
                             e.matmul(pb[b0 + st][:, :], wqh[fp][:, cs],
                                      xnl_sb[:, sl],
                                      start=False, stop=False))
                    for st in range(NST):
                        sl = slice(f * TPC + st * STW, f * TPC + (st + 1) * STW)
                        last = p.op("tensor",
                                    lambda e, f=f, fp=fp, cs=cs, st=st, sl=sl,
                                    b0=b0:
                             e.matmul(pb[b0 + st][:, :], wql[fp][:, cs],
                                      xnh_sb[:, sl],
                                      start=False, stop=(f == NF - 1)),
                             tok=(col == 1 and st == NST - 1))
                rel_wcv[fp] = [last]
                tok_acc = last
            ev_v = []
            ev_k = []
            for st in range(NST):
                sl = slice(st * STW, (st + 1) * STW)
                t = p.op("scalar", lambda e, st=st, sl=sl: e.copy(
                    vraw[:, sl], pb[st][:, :]), waits=[tok_acc] + rel_vraw, tok=True)
                ev_v.append(t)
                p.bank_rel[st] = t
            for st in range(NST):
                sl = slice(st * STW, (st + 1) * STW)
                t = p.op("scalar", lambda e, st=st, sl=sl: e.copy(
                    kraw[:, sl], pb[2 + st][:, :]), waits=rel_kraw, tok=True)
                ev_k.append(t)
                p.bank_rel[2 + st] = t
        if pend is not None:
            p1 = pend["p"]
            hh = pend["h"]
            kt_toks = []
            last_ktr = None
            for i in range(8):
                bk = 4 + (i % 4)
                w = [pend["tok_k"]] + ([p.bank_rel[bk]] if p.bank_rel[bk] else [])
                ttr = p.op("tensor", lambda e, i=i, bk=bk, p1=p1: e.transpose(
                    pb[bk][:, 0:128], kout[p1][:, i * 128:(i + 1) * 128], ident_sb),
                    waits=w, tok=True)
                last_ktr = ttr
                tev = p.op("scalar", lambda e, i=i, bk=bk, p1=p1: e.copy(
                    ktile[p1][i], pb[bk][:, 0:128]),
                    waits=[ttr] + ([rel_ktile[p1][i]] if rel_ktile[p1][i] else []),
                    tok=True)
                p.bank_rel[bk] = tev
                kt_toks.append(tev)
            rel_kout[p1] = [last_ktr, pend["kf_store"]]
            gacc_tok = None
            gs_last = None
            gs_evs = []
            vt_toks_p = pend["vt_toks"]
            for c in range(CPC):
                bk = 6 + (c % 2)
                w = [kt_toks[2 * c], kt_toks[2 * c + 1], vt_toks_p[2 * c],
                     vt_toks_p[2 * c + 1]] + ([p.bank_rel[bk]] if p.bank_rel[bk] else [])
                p.op("tensor", lambda e, c=c, bk=bk, p1=p1: e.matmul(
                    pb[bk][:, 0:129], ktile[p1][2 * c], vtile[p1][2 * c][:, 0:129],
                    start=True, stop=False), waits=w)
                tg = p.op("tensor", lambda e, c=c, bk=bk, p1=p1: e.matmul(
                    pb[bk][:, 0:129], ktile[p1][2 * c + 1],
                    vtile[p1][2 * c + 1][:, 0:129],
                    start=False, stop=True), tok=True)
                rel_ktile[p1][2 * c] = tg
                rel_ktile[p1][2 * c + 1] = tg
                rel_vtile[p1][2 * c] = tg
                rel_vtile[p1][2 * c + 1] = tg
                vt_rel_extra[p1] = pend["vt_store"]
                tev = p.op("scalar", lambda e, c=c, bk=bk: e.copy(
                    gst[c][:, 0:129], pb[bk][:, 0:129]),
                    waits=[tg, tok_gz] + ([rel_gst[c]] if rel_gst[c] else []), tok=True)
                p.bank_rel[bk] = tev
                gs_evs.append(tev)
                if c == 0:
                    gacc_tok = p.op("vector", lambda e: e.tensor_copy(
                        gseg_sb[:, 0:129], gst[0][:, 0:129]),
                        waits=[tev] + ([rel_gseg] if rel_gseg else []), tok=True)
                else:
                    gacc_tok = p.op("vector", lambda e, c=c: e.tensor_add(
                        gseg_sb[:, 0:129], gseg_sb[:, 0:129], gst[c][:, 0:129]),
                        waits=[tev, gacc_tok], tok=True)
            gs_last = p.dma("gpsimd", "sgs", lambda e, hh=hh: e.dma_start(
                out=GS[hh].rearrange("c m w -> m c w"),
                in_=gst_all.rearrange("m (c w) -> m c w", w=GW)), waits=gs_evs)
            for c in range(CPC):
                rel_gst[c] = gs_last
            ccin_last = p.dma("gpsimd", "scc", lambda e, hh=hh: e.dma_start(
                out=CCIN[:, hh * GW:(hh + 1) * GW], in_=gseg_sb), waits=[gacc_tok])
            rel_gseg = ccin_last
            pend = None
        if h < H:
            vt_toks = []
            last_tr = None
            for i in range(8):
                bk = 4 + (i % 4)
                w = [ev_v[i // 4], tok_consts] + \
                    ([p.bank_rel[bk]] if p.bank_rel[bk] else [])
                ttr = p.op("tensor", lambda e, i=i, bk=bk, pq=pq: e.transpose(
                    pb[bk][:, 0:128], vraw[:, i * 128:(i + 1) * 128], ident_sb),
                    waits=w, tok=True)
                last_tr = ttr
                tev = p.op("scalar", lambda e, i=i, bk=bk, pq=pq: e.copy(
                    vtile[pq][i][:, 0:128], pb[bk][:, 0:128]),
                    waits=[ttr, tok_vones, vt_rel_extra[pq]] +
                          ([rel_vtile[pq][i]] if rel_vtile[pq][i] else []),
                    tok=True)
                p.bank_rel[bk] = tev
                vt_toks.append(tev)
            last_vt_store = p.dma("gpsimd", "svt",
                                  lambda e, h=h, pq=pq: e.dma_start(
                out=VT[h].rearrange("c t m d -> m c t d"),
                in_=vtile_all[pq].rearrange("m (i w) -> m i w", w=132)[:, :, 0:128]),
                waits=vt_toks)
            for i in range(8):
                rel_vtile[pq][i] = last_vt_store
            rel_vraw = [last_tr]
            tok_k, kraw_rel = rope_elu(kraw, kout[pq],
                                       (rel_kout[pq][0] if rel_kout[pq] else None),
                                       ev_k[1], (6, 7))
            rel_kraw = [kraw_rel]
            last_kf_store = p.dma("gpsimd", "skf", lambda e, h=h, pq=pq: e.dma_start(
                out=KF[h * 128:(h + 1) * 128, :], in_=kout[pq]), waits=[tok_k])
            pend = {"h": h, "p": pq, "tok_k": tok_k, "vt_toks": vt_toks,
                    "vt_store": last_vt_store, "kf_store": last_kf_store}

    if _SKIP_CC:
        for s in range(4):
            tok_cc = p.dma("gpsimd", "scc", lambda e, s=s: e.dma_start(
                out=CCOUT[s * 128:(s + 1) * 128, :], in_=CCIN),
                waits=[ccin_last])
    else:
        tok_cc = p.cc_op(lambda e: e.collective_compute(
            "AllGather", OP.bypass,
            replica_groups=[[0, 1, 2, 3], [4, 5, 6, 7]],
            ins=[CCIN.opt()], outs=[CCOUT.opt()]), waits=[ccin_last])

    qraw = [pool[0], pool[1]]
    qout = [pool[2], pool[3]]
    rel_qraw = [rel_vraw + rel_kraw, list(rel_kraw)]
    rel_wf = [[], []]
    tred = None
    rel_qout = [list(rel_kout[0]), list(rel_kout[1])]
    last_qf_store = [None, None]
    pend3 = None
    for b in range(H // 2 + 1):
        if pend3 is not None:
            bb = pend3["b"]
            for j in range(2):
                hq = 2 * bb + j
                tok_q, raw_rel = rope_elu(qraw[j], qout[j],
                                          (rel_qout[j][0] if rel_qout[j] else None),
                                          pend3["ev"][j][1],
                                          (4, 5) if j == 0 else (6, 7))
                rel_qraw[j] = [raw_rel] + pend3["extra"][j]
                last_qf_store[j] = p.dma("gpsimd", f"sqf{j}",
                                         lambda e, hq=hq, j=j: e.dma_start(
                    out=QF[hq * 128:(hq + 1) * 128, :], in_=qout[j]), waits=[tok_q])
                rel_qout[j] = [last_qf_store[j]]
            pend3 = None
        if b < H // 2:
            tok_acc = None
            for f in range(NF):
                fp = f % 2
                tw = p.dma("sync", f"wq{fp}", lambda e, b=b, f=f, fp=fp: e.dma_start(
                    out=wqb[fp],
                    in_=WQKVP[f * 128:(f + 1) * 128,
                              2 * E + b * 256:2 * E + (b + 1) * 256]),
                    waits=rel_wq[fp])
                tcv = p.op("vector", lambda e, fp=fp: e.tensor_copy(
                    wqh[fp], wqb[fp]), waits=[tw] + rel_wcv[fp])
                tcv = p.op("vector", lambda e, fp=fp: e.tensor_sub(
                    wql[fp], wqb[fp], wqh[fp].bitcast(F32)), tok=True)
                rel_wq[fp] = [tcv]
                mm_w = [tcv, xn_tok[f]]
                if f == 0:
                    mm_w += [t for t in p.bank_rel[0:4] if t]
                last = None
                for col in range(2):
                    cs = slice(col * 128, (col + 1) * 128)
                    b0 = 2 * col
                    for st in range(NST):
                        sl = slice(f * TPC + st * STW, f * TPC + (st + 1) * STW)
                        p.op("tensor", lambda e, f=f, fp=fp, cs=cs, st=st, sl=sl,
                             b0=b0:
                             e.matmul(pb[b0 + st][:, :], wqh[fp][:, cs],
                                      xnh_sb[:, sl],
                                      start=(f == 0), stop=False), waits=mm_w)
                        p.op("tensor", lambda e, fp=fp, cs=cs, st=st, sl=sl, b0=b0:
                             e.matmul(pb[b0 + st][:, :], wqh[fp][:, cs],
                                      xnl_sb[:, sl],
                                      start=False, stop=False))
                    for st in range(NST):
                        sl = slice(f * TPC + st * STW, f * TPC + (st + 1) * STW)
                        last = p.op("tensor",
                                    lambda e, f=f, fp=fp, cs=cs, st=st, sl=sl,
                                    b0=b0:
                             e.matmul(pb[b0 + st][:, :], wql[fp][:, cs],
                                      xnh_sb[:, sl],
                                      start=False, stop=(f == NF - 1)),
                             tok=(col == 1 and st == NST - 1))
                rel_wcv[fp] = [last]
                tok_acc = last
            for k in range(4):
                i = 4 * b + k
                twfb = p.dma("sync", f"lwf{i % 2}", lambda e, i=i: e.dma_start(
                    out=wfb[i % 2],
                    in_=WFFNS[i // 2, :, (i % 2) * TPC:(i % 2 + 1) * TPC]),
                    waits=rel_wf[i % 2])
                tred = p.op("vector", lambda e, i=i: e.tensor_reduce(
                    acc32[:, i:i + 1], wfb[i % 2], AX.X, OP.add,
                    apply_absolute_value=True), waits=[twfb], tok=True)
                rel_wf[i % 2] = [tred]
        if b < H // 2:
            evs = []
            extra = [[], []]
            for j in range(2):
                hq = 2 * b + j
                ev = []
                for st in range(NST):
                    sl = slice(st * STW, (st + 1) * STW)
                    t = p.op("scalar", lambda e, j=j, st=st, sl=sl: e.copy(
                        qraw[j][:, sl], pb[2 * j + st][:, :]),
                        waits=[tok_acc] + rel_qraw[j], tok=True)
                    ev.append(t)
                    p.bank_rel[2 * j + st] = t
                if _DEBUG_OUTPUTS:
                    tqr = p.dma("gpsimd", "sqr", lambda e, hq=hq, j=j: e.dma_start(
                        out=QRAW[hq * 128:(hq + 1) * 128, :], in_=qraw[j]),
                        waits=[ev[0], ev[1]])
                    extra[j] = [tqr]
                evs.append(ev)
            pend3 = {"b": b, "ev": evs, "extra": extra}

    p.op("vector", lambda e: e.memset(gst_all[:, 0:512], 0.0), waits=[tred])
    p.op("vector", lambda e: e.memset(gst_all[:, 0:512], 1.0))
    p.op("vector", lambda e: e.tensor_reduce(
        acc32[:, 32:33], acc32[:, 0:32], AX.X, OP.add))
    p.op("vector", lambda e: e.memset(gst_all[:, 0:512], 0.0))
    tred = p.op("vector", lambda e: e.memset(gst_all[:, 0:512], 1.0), tok=True)
    if _DEBUG_OUTPUTS:
        p.dma("gpsimd", "sdbg", lambda e: e.dma_start(out=ACC, in_=acc32),
              waits=[tred])
    tsc = p.op("tensor", lambda e: e.matmul(
        pb[6][:, 0:1], ones_sb, acc32[:, 32:33], start=True, stop=True),
        waits=[tred] + ([p.bank_rel[6]] if p.bank_rel[6] else []), tok=True)
    tok_scale = p.op("vector", lambda e: e.tensor_scalar_mul(
        scale_sb, pb[6][:, 0:1], 1.0 / (E * E)), waits=[tsc], tok=True)
    p.bank_rel[6] = tok_scale
    p.op("vector", lambda e: e.memset(gst_all[:, 0:512], 0.0))
    p.op("vector", lambda e: e.memset(gst_all[:, 0:512], 1.0))
    p.op("vector", lambda e: e.tensor_scalar_add(rscaleb_sb, scale_sb, 1e-10))
    p.op("vector", lambda e: e.memset(gst_all[:, 0:512], 0.0))
    p.op("vector", lambda e: e.memset(gst_all[:, 0:512], 1.0))
    p.op("vector", lambda e: e.tensor_scalar_mul(thr_sb, rscaleb_sb, 0.5))
    p.op("vector", lambda e: e.tensor_scalar_mul(thrn_sb, rscaleb_sb, -0.5))
    p.op("vector", lambda e: e.memset(gst_all[:, 0:512], 0.0))
    tok_thr = p.op("vector", lambda e: e.memset(gst_all[:, 0:512], 1.0), tok=True)

    if _DEBUG_OUTPUTS:
        p.dma("gpsimd", "sdbg", lambda e: e.dma_start(
            out=SCL[:, 0:1], in_=scale_sb, allow_slow_non_contiguous=True),
            waits=[tok_thr])
        p.dma("gpsimd", "sdbg", lambda e: e.dma_start(
            out=SCL[:, 1:2], in_=thr_sb, allow_slow_non_contiguous=True),
            waits=[tok_thr])
    if _DEBUG_OUTPUTS:
        p.dma("gpsimd", "sdbg", lambda e: e.dma_start(
            out=WTS_D, in_=xnh_sb[:, 0:2048]), waits=[tok_thr])
        p.dma("gpsimd", "sdbg", lambda e: e.dma_start(
            out=XNH_D, in_=xnh_sb.bitcast(F32)), waits=[tok_thr])
        p.dma("gpsimd", "sdbg", lambda e: e.dma_start(
            out=XNL_D, in_=xnl_sb.bitcast(F32)), waits=[tok_thr])
    m2h = pool[0]
    ffb = [pool[1][:, 0:512], pool[1][:, 512:1024]]
    rel_bfw = [rel_wf[0], rel_wf[1]]
    rel_m2 = list(rel_qraw[0])
    rel_wtsB = []
    rel_ffb = [list(rel_qraw[1]), list(rel_qraw[1])]
    rel_bfb = [[], []]
    last_ff_store = [None, None]
    rel_bfbank = [None, None, None, None]
    for oc in range(NF):
        par = oc % 2
        tbi = p.dma("scalar", f"lbf{par}", lambda e, oc=oc, par=par: e.dma_start(
            out=bfb[par], in_=BIAS[oc * 128:(oc + 1) * 128, :]), waits=rel_bfb[par])
        ttn = None
        for half in range(2):
            twf = p.dma("sync", f"lws{half}", lambda e, oc=oc, half=half:
                        e.dma_start(
                out=wfb[half], in_=WFFNS[oc, :, half * TPC:(half + 1) * TPC]),
                waits=rel_bfw[half])
            p.op("vector", lambda e, half=half: e.tensor_scalar(
                m2h, wfb[half], thrn_sb[:, 0:1], None, OP.is_lt),
                waits=[twf, tok_thr] + rel_m2 + rel_wtsB)
            rel_m2 = []
            ttn = p.op("vector", lambda e, half=half: e.scalar_tensor_tensor(
                wtsB[:, half * TPC:(half + 1) * TPC], wfb[half], thr_sb[:, 0:1],
                m2h, OP.is_gt, OP.subtract), tok=True)
            rel_bfw[half] = [ttn]
        tdmp = None
        if _DEBUG_OUTPUTS:
            tdmp = p.dma("gpsimd", "sdbg", lambda e, oc=oc: e.dma_start(
                out=WTD_D[oc], in_=wtsB.bitcast(F32)), waits=[ttn])
        for st in range(NST):
            bk = 2 * par + st
            wmm = [ttn] + ([rel_bfbank[bk]] if rel_bfbank[bk] else [])
            if oc == 0 and st == 0:
                wmm += [t for t in p.bank_rel[0:4] if t]
            tmf = None
            for f in range(NF):
                sl = slice(f * TPC + st * STW, f * TPC + (st + 1) * STW)
                p.op("tensor", lambda e, f=f, sl=sl, bk=bk: e.matmul(
                    pb[bk][:, :], wtsB[:, f * 128:(f + 1) * 128], xnh_sb[:, sl],
                    start=(f == 0), stop=False), waits=wmm)
                tmf = p.op("tensor", lambda e, f=f, sl=sl, bk=bk: e.matmul(
                    pb[bk][:, :], wtsB[:, f * 128:(f + 1) * 128], xnl_sb[:, sl],
                    start=False, stop=(f == NF - 1)), tok=(f == NF - 1))
            te1 = p.op("vector", lambda e, par=par, bk=bk: e.tensor_scalar(
                ffb[par], pb[bk][:, :], scale_sb[:, 0:1], bfb[par][:, 0:1],
                OP.mult, OP.add),
                waits=[tmf, tok_scale, tbi] + rel_ffb[par], tok=True)
            rel_bfbank[bk] = te1
            last_ff_store[par] = p.dma("gpsimd", f"sff{par}",
                                       lambda e, oc=oc, st=st, par=par: e.dma_start(
                out=FF[oc * 128:(oc + 1) * 128, st * STW:(st + 1) * STW],
                in_=ffb[par]), waits=[te1])
            rel_ffb[par] = [last_ff_store[par]]
            rel_wtsB = [tmf]
        if tdmp is not None:
            rel_wtsB = rel_wtsB + [tdmp]
        rel_bfb[par] = [rel_bfbank[2 * par + 1]]

    p.barrier()
    stA.close()

    stB = contextlib.ExitStack()
    sbB = lambda name, shape, dt=F32: stB.enter_context(
        nc.sbuf_tensor(name, shape, dt)).ap()
    mask_sb = sbB("mask_sb", [128, 512])
    den16 = sbB("den16", [16, H * TPC // 16])
    den16b = sbB("den16b", [16, H * TPC // 16])

    tok_mask = p.dma("sync", "cst", lambda e: e.dma_start(out=mask_sb, in_=MASKUT))

    sbase_sb = sbB("sbase_sb", [128, H * GW])
    ccseg = [sbB("ccseg0", [128, H * GW])] * 2
    qh_sb = [[sbB(f"qh{q}_{i}", [128, TPC]) for i in range(2)] for q in range(2)]
    kh_sb = [[sbB(f"kh{q}_{i}", [128, TPC]) for i in range(2)] for q in range(2)]
    vtb = [[sbB(f"vtb{q}_{i}", [128, 256]) for i in range(2)] for q in range(2)]
    gsb = [[sbB(f"gsb{q}_{i}", [128, GW]) for i in range(2)] for q in range(2)]
    scur = [[sbB(f"scur{q}_{i}", [128, 132]) for i in range(CPC)] for q in range(2)]
    attsb = [[sbB(f"attsb{q}_{i}", [128, 512]) for i in range(2)] for q in range(2)]
    aoc = [[sbB(f"aoc{q}_{i}", [128, 256]) for i in range(2)] for q in range(2)]
    densb = [[sbB(f"densb{q}_{i}", [1, 256]) for i in range(2)] for q in range(2)]

    tok_sbase = None
    rel_ccseg = [[], []]
    for s in range(4):
        tcs = p.dma("scalar", "lcs0", lambda e, s=s: e.dma_start(
            out=ccseg[0], in_=CCOUT[s * 128:(s + 1) * 128, :]),
            waits=[tok_cc] + rel_ccseg[0] + rel_ccseg[1])
        if s == 0:
            tok_sbase = p.op("vector", lambda e: e.tensor_scalar(
                sbase_sb, ccseg[0], segm_sb[:, 0:1], None, OP.mult),
                waits=[tcs, tok_consts], tok=True)
        else:
            tok_sbase = p.op("vector", lambda e, s=s: e.scalar_tensor_tensor(
                sbase_sb, ccseg[s % 2], segm_sb[:, s:s + 1], sbase_sb,
                OP.mult, OP.add), waits=[tcs], tok=True)
        rel_ccseg[s % 2] = [tok_sbase]

    rel_qh = [[[last_qf_store[0], last_qf_store[1]] for _ in range(2)]
              for _ in range(2)]
    rel_kh = [[[last_kf_store] for _ in range(2)] for _ in range(2)]
    rel_vtb = [[[last_vt_store], [last_vt_store]], [[last_vt_store], [last_vt_store]]]
    rel_gsb = [[[], []], [[], []]]
    rel_attsb = [[[], []], [[], []]]
    rel_scur = [[[], [], [], []], [[], [], [], []]]
    rel_aoc = [[[], []], [[], []]]
    rel_densb = [[[], []], [[], []]]
    last_den_store = {}
    last_ao_store = {}
    tq2 = [[None, None], [None, None]]
    tk2 = [[None, None], [None, None]]
    for hp in range(2):
        tq2[0][hp] = p.dma("sync", f"lqh{hp}_0", lambda e, hp=hp: e.dma_start(
            out=qh_sb[0][hp], in_=QF[hp * 128:(hp + 1) * 128, :]),
            waits=rel_qh[0][hp])
        tk2[0][hp] = p.dma("sync", f"lkh{hp}_0", lambda e, hp=hp: e.dma_start(
            out=kh_sb[0][hp], in_=KF[hp * 128:(hp + 1) * 128, :]),
            waits=rel_kh[0][hp])
    for g in range(H // 2):
        scur_toks = [None, None]
        gp = g % 2
        if g + 1 < H // 2:
            gn = (g + 1) % 2
            for hp in range(2):
                h = 2 * (g + 1) + hp
                tq2[gn][hp] = p.dma("sync", f"lqh{hp}_{gn}",
                                    lambda e, h=h, hp=hp, gn=gn: e.dma_start(
                    out=qh_sb[gn][hp], in_=QF[h * 128:(h + 1) * 128, :]),
                    waits=rel_qh[gn][hp])
                tk2[gn][hp] = p.dma("sync", f"lkh{hp}_{gn}",
                                    lambda e, h=h, hp=hp, gn=gn: e.dma_start(
                    out=kh_sb[gn][hp], in_=KF[h * 128:(h + 1) * 128, :]),
                    waits=rel_kh[gn][hp])
        tq = tq2[gp]
        tk = tk2[gp]
        for hp in range(2):
            h = 2 * g + hp
            scur_tok = [None] * CPC
            scur_tok[0] = p.op("vector", lambda e, h=h, hp=hp: e.tensor_copy(
                scur[hp][0][:, 0:129], sbase_sb[:, h * GW:h * GW + 129]),
                waits=[tok_sbase] + rel_scur[hp][0], tok=True)
            for c in range(1, CPC):
                tg = p.dma("scalar", f"lgb{hp}_{c % 2}",
                           lambda e, h=h, hp=hp, c=c: e.dma_start(
                    out=gsb[hp][c % 2], in_=GS[h, c - 1, :, :]),
                    waits=rel_gsb[hp][c % 2] + [rel_gst[c - 1]])
                scur_tok[c] = p.op("vector", lambda e, hp=hp, c=c: e.tensor_add(
                    scur[hp][c][:, 0:129], scur[hp][c - 1][:, 0:129],
                    gsb[hp][c % 2][:, 0:129]),
                    waits=[tg, scur_tok[c - 1]] + rel_scur[hp][c], tok=True)
                rel_gsb[hp][c % 2] = [scur_tok[c]]
            scur_toks[hp] = scur_tok
        tden = {}
        att_last = {}
        for c in range(CPC):
            att_toks_hp = {}
            tvtb_hp = {}
            for hp in range(2):
                h = 2 * g + hp
                csl = slice(c * CHK, (c + 1) * CHK)
                ab = 4 * hp
                tvtb_hp[hp] = p.dma("scalar", f"lvb{hp}_{c % 2}",
                             lambda e, h=h, hp=hp, c=c: e.dma_start(
                    out=vtb[hp][c % 2].rearrange("m (t d) -> m t d", t=2),
                    in_=VT[h, c].rearrange("t m d -> m t d")),
                    waits=rel_vtb[hp][c % 2])
                att_toks = []
                for t in range(2):
                    bk = ab + t
                    wmm = [tq[hp], tk[hp]] + ([p.bank_rel[bk]] if p.bank_rel[bk] else [])
                    ta = p.op("tensor", lambda e, hp=hp, c=c, t=t, csl=csl, bk=bk,
                              gp=gp: e.matmul(
                        pb[bk][:, 0:256],
                        kh_sb[gp][hp][:, c * CHK + t * 128:c * CHK + (t + 1) * 128],
                        qh_sb[gp][hp][:, csl], start=True, stop=True), waits=wmm,
                        tok=True)
                    tm = p.op("vector", lambda e, hp=hp, c=c, t=t, bk=bk: e.tensor_mul(
                        attsb[hp][c % 2][:, t * 256:(t + 1) * 256], pb[bk][:, 0:256],
                        mask_sb[:, t * 256:(t + 1) * 256]),
                        waits=[ta, tok_mask] + rel_attsb[hp][c % 2], tok=True)
                    p.bank_rel[bk] = tm
                    att_toks.append(tm)
                att_last[hp] = att_toks[1]
                att_toks_hp[hp] = att_toks
            for hp in range(2):
                h = 2 * g + hp
                csl = slice(c * CHK, (c + 1) * CHK)
                nb = 2 + 4 * hp
                db = 3 + 4 * hp
                att_toks = att_toks_hp[hp]
                tvtb = tvtb_hp[hp]
                wmm = [att_toks[0], att_toks[1], tvtb, scur_toks[hp][c]]
                wmm += [p.bank_rel[nb]] if p.bank_rel[nb] else []
                p.op("tensor", lambda e, hp=hp, c=c, nb=nb: e.matmul(
                    pb[nb][:, 0:256], vtb[hp][c % 2][:, 0:128],
                    attsb[hp][c % 2][:, 0:256], start=True, stop=False), waits=wmm)
                p.op("tensor", lambda e, hp=hp, c=c, nb=nb: e.matmul(
                    pb[nb][:, 0:256], vtb[hp][c % 2][:, 128:256],
                    attsb[hp][c % 2][:, 256:512], start=False, stop=False))
                tnum = p.op("tensor", lambda e, hp=hp, c=c, nb=nb, csl=csl, gp=gp:
                            e.matmul(
                    pb[nb][:, 0:256], scur[hp][c][:, 0:128], qh_sb[gp][hp][:, csl],
                    start=False, stop=True), tok=True)
                wmm = [p.bank_rel[db]] if p.bank_rel[db] else []
                p.op("tensor", lambda e, hp=hp, c=c, db=db: e.matmul(
                    pb[db][0:1, 0:256], ones_sb[:, 0:1],
                    attsb[hp][c % 2][:, 0:256], start=True, stop=False), waits=wmm)
                p.op("tensor", lambda e, hp=hp, c=c, db=db: e.matmul(
                    pb[db][0:1, 0:256], ones_sb[:, 0:1],
                    attsb[hp][c % 2][:, 256:512], start=False, stop=False))
                tden[hp] = p.op("tensor", lambda e, hp=hp, c=c, db=db, csl=csl,
                                gp=gp: e.matmul(
                    pb[db][0:1, 0:256], scur[hp][c][:, 128:129],
                    qh_sb[gp][hp][:, csl], start=False, stop=True), tok=True)
                rel_scur[hp][c] = [tden[hp]]
                rel_attsb[hp][c % 2] = [tden[hp]]
                rel_vtb[hp][c % 2] = [tnum]
                tev = p.op("scalar", lambda e, hp=hp, c=c, nb=nb: e.copy(
                    aoc[hp][c % 2], pb[nb][:, 0:256]),
                    waits=[tnum] + rel_aoc[hp][c % 2], tok=True)
                p.bank_rel[nb] = tev
                last_ao_store[hp, c % 2] = p.dma(
                    "gpsimd", f"sao{hp}_{c % 2}",
                    lambda e, h=h, hp=hp, c=c: e.dma_start(
                        out=AO[h * 128:(h + 1) * 128, c * CHK:(c + 1) * CHK],
                        in_=aoc[hp][c % 2]), waits=[tev])
                rel_aoc[hp][c % 2] = [last_ao_store[hp, c % 2]]
                tevd = p.op("scalar", lambda e, hp=hp, c=c, db=db: e.copy(
                    densb[hp][c % 2], pb[db][0:1, 0:256]),
                    waits=[tden[hp]] + rel_densb[hp][c % 2], tok=True)
                p.bank_rel[db] = tevd
                last_den_store[hp, c % 2] = p.dma(
                    "gpsimd", f"sdn{hp}_{c % 2}",
                    lambda e, h=h, hp=hp, c=c: e.dma_start(
                        out=DEN[h:h + 1, c * CHK:(c + 1) * CHK],
                        in_=densb[hp][c % 2]), waits=[tevd])
                rel_densb[hp][c % 2] = [last_den_store[hp, c % 2]]
        for hp in range(2):
            rel_qh[gp][hp] = [tden[hp]]
            rel_kh[gp][hp] = [att_last[hp]]

    tdr = p.dma("scalar", "ldn", lambda e: e.dma_start(out=den16, in_=DEN),
                waits=[last_den_store[k] for k in last_den_store])
    p.op("vector", lambda e: e.tensor_scalar_max(den16b, den16, 1e-6), waits=[tdr])
    tok_recip = p.op("vector", lambda e: e.reciprocal(den16r, den16b), tok=True)

    p.barrier()
    stB.close()

    stC = contextlib.ExitStack()
    sc = lambda name, shape, dt=F32: stC.enter_context(
        nc.sbuf_tensor(name, shape, dt)).ap()
    aodh = sc("aodh", [128, NF * STW], F32R)
    aodl = sc("aodl", [128, NF * STW], F32R)
    aotmp = [sc(f"aotmp{i}", [128, STW]) for i in range(2)]
    sett = sc("sett", [128, 256])
    wcp = [sc(f"wcp{i}", [128, E]) for i in range(2)]
    woh = [sc(f"woh{i}", [128, E], F32R) for i in range(2)]
    wol = [sc(f"wol{i}", [128, E], F32R) for i in range(2)]
    ffld = [sc(f"ffld{i}", [128, STW]) for i in range(2)]
    selmat_sb = sc("selmat_sb", [16, E])
    e1b = [sc(f"e1b{i}", [128, STW]) for i in range(2)]
    outoc = [sc(f"outoc{i}", [128, STW]) for i in range(2)]
    xrc = [sc(f"xrc{i}", [128, STW]) for i in range(2)]

    tok_sel = p.dma("scalar", "lsel", lambda e: e.dma_start(
        out=selmat_sb, in_=SELMAT))
    rel_pass_ao = [last_ao_store[k] for k in last_ao_store]
    rel_woh = [[], []]
    rel_wcp = [[], []]
    rel_ffld = [[last_ff_store[0], last_ff_store[1]], []]
    rel_aotmp = [[], []]
    rel_e1 = [[], []]
    rel_outoc = [[], []]
    rel_xrc = [[], []]
    last_out = [None, None]
    for st in range(NST):
        div_toks = [None] * NF
        for f in range(NF):
            sl = slice(f * STW, (f + 1) * STW)
            fp = f % 2
            tld = p.dma("scalar", f"lao{fp}", lambda e, f=f, st=st, fp=fp: e.dma_start(
                out=aotmp[fp],
                in_=AO[f * 128:(f + 1) * 128, st * STW:(st + 1) * STW]),
                waits=(rel_pass_ao if f < 2 else []) + rel_aotmp[fp])
            bk = 6 + fp
            wmm = [tok_recip, tok_sel] + ([p.bank_rel[bk]] if p.bank_rel[bk] else [])
            tprr = p.op("tensor", lambda e, f=f, st=st, bk=bk: e.matmul(
                pb[bk][:, :], selmat_sb[:, f * 128:(f + 1) * 128],
                den16r[:, st * STW:(st + 1) * STW], start=True, stop=True),
                waits=wmm, tok=True)
            p.op("vector", lambda e, fp=fp, bk=bk: e.tensor_mul(
                aotmp[fp], aotmp[fp], pb[bk][:, :]), waits=[tprr, tld])
            tdv = p.op("vector", lambda e, fp=fp, sl=sl: e.tensor_copy(
                aodh[:, sl], aotmp[fp]), tok=True)
            p.bank_rel[bk] = tdv
            div_toks[f] = p.op("vector", lambda e, fp=fp, sl=sl: e.tensor_sub(
                aodl[:, sl], aotmp[fp], aodh[:, sl].bitcast(F32)), tok=True)
            rel_aotmp[fp] = [div_toks[f]]
        p.op("vector", lambda e: e.memset(sett, 0.0), waits=[div_toks[NF - 1]])
        div_all = p.op("vector", lambda e: e.memset(sett, 1.0), tok=True)
        rel_pass_ao = []
        if _DEBUG_OUTPUTS and st == 0:
            p.dma("gpsimd", "sdbg", lambda e: e.dma_start(out=ADH, in_=aodh),
                  waits=[div_toks[NF - 1]])
            p.dma("gpsimd", "sdbg", lambda e: e.dma_start(out=ADL, in_=aodl),
                  waits=[div_toks[NF - 1]])
        for oc in range(NF):
            par = oc % 2
            bw = oc % 4
            tld = p.dma("sync", f"lwoh{par}", lambda e, oc=oc, par=par: e.dma_start(
                out=wcp[par], in_=WOUTS[oc]), waits=rel_wcp[par])
            p.op("vector", lambda e, par=par: e.tensor_copy(woh[par], wcp[par]),
                 waits=[tld] + rel_woh[par])
            twoh = p.op("vector", lambda e, par=par: e.tensor_sub(
                wol[par], wcp[par], woh[par].bitcast(F32)), tok=True)
            rel_wcp[par] = [twoh]
            if _DEBUG_OUTPUTS and st == 0 and oc == 0:
                p.dma("gpsimd", "sdbg", lambda e: e.dma_start(out=WOHB, in_=woh[0]),
                      waits=[twoh])
            tff = p.dma("scalar", f"lff{par}", lambda e, oc=oc, st=st, par=par:
                        e.dma_start(
                out=ffld[par], in_=FF[oc * 128:(oc + 1) * 128,
                                      st * STW:(st + 1) * STW]),
                waits=rel_ffld[par])
            txr = p.dma("scalar", f"lxr{par}", lambda e, oc=oc, st=st, par=par:
                        e.dma_start(
                out=xrc[par], in_=XT[oc * 128:(oc + 1) * 128,
                                     st * STW:(st + 1) * STW]), waits=rel_xrc[par])
            tok_mm1 = None
            for f in range(NF):
                sl = slice(f * STW, (f + 1) * STW)
                wmm = [twoh, div_all]
                if f == 0:
                    wmm += [p.bank_rel[bw]] if p.bank_rel[bw] else []
                p.op("tensor", lambda e, f=f, sl=sl, bw=bw, par=par: e.matmul(
                    pb[bw][:, :], woh[par][:, f * 128:(f + 1) * 128], aodh[:, sl],
                    start=(f == 0), stop=False), waits=wmm)
                p.op("tensor", lambda e, f=f, sl=sl, bw=bw, par=par: e.matmul(
                    pb[bw][:, :], woh[par][:, f * 128:(f + 1) * 128], aodl[:, sl],
                    start=False, stop=False))
            for f in range(NF):
                sl = slice(f * STW, (f + 1) * STW)
                tok_mm1 = p.op("tensor", lambda e, f=f, sl=sl, bw=bw, par=par:
                               e.matmul(
                    pb[bw][:, :], wol[par][:, f * 128:(f + 1) * 128], aodh[:, sl],
                    start=False, stop=(f == NF - 1)),
                    tok=(f == NF - 1))
            rel_woh[par] = [tok_mm1]
            if oc >= NF - 2:
                rel_pass_ao.append(tok_mm1)
            t2 = p.op("vector", lambda e, par=par, bw=bw: e.tensor_add(
                e1b[par], ffld[par], pb[bw][:, :]),
                waits=[tok_mm1, tff] + rel_e1[par], tok=True)
            p.bank_rel[bw] = t2
            rel_ffld[par] = [t2]
            teo = p.op("vector", lambda e, par=par: e.tensor_add(
                outoc[par], e1b[par], xrc[par]),
                waits=[txr] + rel_outoc[par], tok=True)
            rel_e1[par] = [teo]
            rel_xrc[par] = [teo]
            last_out[par] = p.dma("gpsimd", f"sou{par}",
                                  lambda e, oc=oc, st=st, par=par: e.dma_start(
                out=OUT[oc * 128:(oc + 1) * 128, st * STW:(st + 1) * STW],
                in_=outoc[par]), waits=[teo])
            rel_outoc[par] = [last_out[par]]

    p._wait("gpsimd", ("sou0", p.counts["sou0"], None))
    p._wait("gpsimd", ("sou1", p.counts["sou1"], None))
    stC.close()
    stack_all.close()

    with nc.Block() as block:
        @block.tensor
        def _(e):
            for fn in p.ops["tensor"]:
                fn(e)

        @block.vector
        def _(e):
            for fn in p.ops["vector"]:
                fn(e)

        @block.scalar
        def _(e):
            for fn in p.ops["scalar"]:
                fn(e)

        @block.sync
        def _(e):
            for fn in p.ops["sync"]:
                fn(e)

        @block.gpsimd
        def _(e):
            for fn in p.ops["gpsimd"]:
                fn(e)

    return nc


def _stage_inputs(x, wqkv, wout, wffn, bias):
    x = np.ascontiguousarray(np.asarray(x, np.float32).reshape(B * S, E))
    wqkv = np.asarray(wqkv, np.float32)
    wout = np.asarray(wout, np.float32)
    wffn = np.asarray(wffn, np.float32)
    bias = np.ascontiguousarray(np.asarray(bias, np.float32).reshape(E, 1))

    wt = wqkv.T
    cols = []
    for h in range(H):
        cols.append(wt[:, 2 * E + h * D:2 * E + (h + 1) * D])
        cols.append(wt[:, E + h * D:E + (h + 1) * D])
    cols.append(wt[:, 0:E])
    wqkvp = np.ascontiguousarray(np.concatenate(cols, axis=1))

    def strips(w):
        a = w.reshape(NF, 128, NF, 128)
        return np.ascontiguousarray(a.transpose(0, 3, 2, 1).reshape(NF, 128, E))

    wouts = strips(wout)
    wffns = strips(wffn)

    invf = 1.0 / 10000.0 ** (np.arange(0, D, 2, dtype=np.float32) / D)
    pos = np.arange(S, dtype=np.float32)
    emb = np.concatenate([np.outer(pos, invf)] * 2, axis=-1)
    cos_all = np.cos(emb).astype(np.float32)
    sins_all = np.sin(emb).astype(np.float32)
    sins_all[:, 0::2] *= -1.0
    maskut = (np.arange(CHK)[None, :] >= np.arange(CHK)[:, None]).astype(np.float32)
    maskut2 = np.ascontiguousarray(
        np.concatenate([maskut[0:128, :], maskut[128:256, :]], axis=1))
    ident = np.eye(128, dtype=np.float32)
    prot = np.zeros((128, 128), np.float32)
    for m in range(128):
        prot[m + 1 if m % 2 == 0 else m - 1, m] = 1.0
    selmat = np.zeros((16, E), np.float32)
    for h in range(H):
        selmat[h, h * 128:(h + 1) * 128] = 1.0

    in_maps = []
    for c in range(NCORE):
        t0 = c * TPC
        p0 = (c % 4) * TPC
        segmask = np.zeros((128, 4), np.float32)
        segmask[:, 0:(c % 4)] = 1.0
        in_maps.append({
            "XT": np.ascontiguousarray(x[t0:t0 + TPC].T),
            "WQKVP": wqkvp,
            "WOUTS": wouts,
            "WFFNS": wffns,
            "BIAS": bias,
            "COS": np.ascontiguousarray(cos_all[p0:p0 + TPC].T),
            "SINS": np.ascontiguousarray(sins_all[p0:p0 + TPC].T),
            "MASKUT": maskut2,
            "IDENT": ident,
            "PROT": prot,
            "SEGM": segmask,
            "SELMAT": selmat,
        })
    return in_maps


def kernel(L0_attn_x, L0_attn_W_qkv, L0_attn_W_out, L0_ffn_weight, L0_ffn_bias):
    in_maps = _stage_inputs(L0_attn_x, L0_attn_W_qkv, L0_attn_W_out,
                            L0_ffn_weight, L0_ffn_bias)
    nc = build_program()
    res = run_bass_kernel_spmd(nc, in_maps, core_ids=list(range(NCORE)))
    parts = [np.asarray(res.results[c]["OUT"]).T for c in range(NCORE)]
    out = np.concatenate(parts, axis=0).reshape(B, S, E)
    return np.ascontiguousarray(out.astype(np.float32))
